# version 11
# baseline (speedup 1.0000x reference)
import numpy as np
import concourse.bass as bass
import concourse.mybir as mybir
from concourse.bass_utils import run_bass_kernel_spmd

F32 = mybir.dt.float32
BF16 = mybir.dt.bfloat16
AF = mybir.ActivationFunctionType
OP = mybir.AluOpType
AX = mybir.AxisListType

S = 4096
DM = 2048
NH = 16
HD = 128
BLK = 64
NBLK = S // BLK
IN = 8192
NCI = DM // 128
NCO_I = IN // 128
TA = 1024
TF = 512
NCA = S // TA
NCF = S // TF
NQT = S // 128
NQT_B = NQT

SM_SCALE = float(HD) ** -0.5
S1 = 1.0 / (np.sqrt(2048.0) * 64.0)
S2 = 1.0 / (np.sqrt(8192.0) * 64.0)
EPS = 1e-5
NEG = -1.0e9
BIG = 1.0e30

ENGS = ("sync", "gpsimd", "vector", "scalar", "tensor")


OWN_SEM = {"vector": "vv", "scalar": "sc"}


class Prog:
    def __init__(self):
        self.ops = {e: [] for e in ENGS}
        self.cnt = {}

    def op(self, eng, fname, *a, wait=(), inc=(), small=False, **k):
        own = OWN_SEM.get(eng)
        ws = list(wait)
        if own is not None and fname not in ("nop", "engine_nop", "drain", "dma_start"):
            ws.append((own, self.cnt.get(own, 0)))
            if not any(s == own for s, _ in inc):
                inc = tuple(inc) + ((own, 1),)
        incs = []
        for sem, d in inc:
            self.cnt[sem] = self.cnt.get(sem, 0) + d
            incs.append((sem, d))
        ws = [(s, v) for (s, v) in ws if v > 0]
        if small:
            self.ops[eng].append((fname, a, k, ws, []))
            self.ops[eng].append(("drain", (), {}, [], incs))
        else:
            self.ops[eng].append((fname, a, k, ws, incs))
        return (eng, len(self.ops[eng]) - 1)

    def add_wait(self, handle, sem, val):
        eng, i = handle
        fname, a, k, ws, incs = self.ops[eng][i]
        if val > 0:
            self.ops[eng][i] = (fname, a, k, ws + [(sem, val)], incs)

    POOL = {"sync": ["wS%d" % i for i in range(16)],
            "gpsimd": ["wG%d" % i for i in range(8)],
            "scalar": ["wA%d" % i for i in range(8)]}

    def dma(self, eng, out, in_, sem=None, wait=()):
        pool = self.POOL[eng]
        depth = len(pool) // 2
        hist = self.cnt.setdefault("_hist_" + eng, [])
        i = len(hist)
        s = pool[i % len(pool)]
        w = tuple(wait)
        if i >= depth:
            w = w + (hist[i - depth],)
        self.op(eng, "dma_start", wait=w, inc=((s, 16),), out=out, in_=in_)
        tok = (s, self.cnt[s])
        hist.append(tok)
        return tok

    def c(self, sem):
        return self.cnt.get(sem, 0)

    def all_sems(self):
        return (["tm", "vv", "sc"] + self.POOL["sync"] + self.POOL["gpsimd"]
                + self.POOL["scalar"])

    def barrier(self):
        snap = [(s, self.c(s)) for s in self.all_sems()]
        for e in ENGS:
            nopname = {"sync": "nop", "gpsimd": "engine_nop", "vector": "engine_nop",
                       "scalar": "nop", "tensor": "nop"}[e]
            self.op(e, nopname, wait=snap)

    def check_waits(self):
        bad = []
        for e in ENGS:
            for i, (fn, a, k, ws, incs) in enumerate(self.ops[e]):
                for s, v in ws:
                    if v > self.cnt.get(s, 0):
                        bad.append((e, i, fn, s, v, self.cnt.get(s, 0)))
        return bad

    def replay(self, engname, engobj, sems):
        for fname, a, k, waits, incs in self.ops[engname]:
            for s, v in waits:
                engobj.wait_ge(sems[s], v)
            instr = getattr(engobj, fname)(*a, **k)
            for s, d in incs:
                instr.then_inc(sems[s], d)


def build_nc(dbg=False, phases="AFBC"):
    nc = bass.Bass("TRN2", target_bir_lowering=False)

    x_d = nc.dram_tensor("x", [S, DM], F32, kind="ExternalInput").ap()
    wq_d = nc.dram_tensor("wq", [NCI, 128, DM], F32, kind="ExternalInput").ap()
    wk_d = nc.dram_tensor("wk", [NCI, 128, DM], F32, kind="ExternalInput").ap()
    wv_d = nc.dram_tensor("wv", [NCI, 128, DM], F32, kind="ExternalInput").ap()
    wo_d = nc.dram_tensor("wo", [NCI, 128, DM], F32, kind="ExternalInput").ap()
    wup_d = nc.dram_tensor("wup", [NCO_I, 128, DM], F32, kind="ExternalInput").ap()
    wgt_d = nc.dram_tensor("wgt", [NCO_I, 128, DM], F32, kind="ExternalInput").ap()
    wdn_d = nc.dram_tensor("wdn", [NCI, 128, IN], F32, kind="ExternalInput").ap()
    su_ug_d = nc.dram_tensor("su_ug", [128, NCI], F32, kind="ExternalInput").ap()
    su_dp_d = nc.dram_tensor("su_dp", [128, NCO_I], F32, kind="ExternalInput").ap()
    wsu_d = nc.dram_tensor("wsu", [128, NCO_I], F32, kind="ExternalInput").ap()
    wsg_d = nc.dram_tensor("wsg", [128, NCO_I], F32, kind="ExternalInput").ap()
    wdp_d = nc.dram_tensor("wdp", [128, NCI], F32, kind="ExternalInput").ap()
    h128_d = nc.dram_tensor("h128", [128, 128], F32, kind="ExternalInput").ap()
    ident_d = nc.dram_tensor("ident", [128, 128], F32, kind="ExternalInput").ap()
    causal_d = nc.dram_tensor("causal", [128, 128], F32, kind="ExternalInput").ap()
    gmask_d = nc.dram_tensor("gmask", [128, NQT * NBLK], F32, kind="ExternalInput").ap()

    out_d = nc.dram_tensor("out", [S, DM], F32, kind="ExternalOutput").ap()
    skind = "ExternalOutput" if dbg else "Internal"
    qt_d = nc.dram_tensor("qt_s", [DM, S], F32, kind=skind).ap()
    kt_d = nc.dram_tensor("kt_s", [DM, S], F32, kind=skind).ap()
    vt_d = nc.dram_tensor("vt_s", [DM, S], F32, kind=skind).ap()
    ot_d = nc.dram_tensor("ot_s", [DM, S], F32, kind=skind).ap()
    ft_d = nc.dram_tensor("ft_s", [DM, S], F32, kind=skind).ap()
    kg_d = nc.dram_tensor("kg_s", [128, NCI * NBLK], F32, kind=skind).ap()
    wsp = {}
    for nm, nco_, wid_ in (("wq", NCI, DM), ("wk", NCI, DM), ("wv", NCI, DM),
                           ("wo", NCI, DM), ("wup", NCO_I, DM), ("wgt", NCO_I, DM),
                           ("wdn", NCI, IN)):
        wsp[nm] = (nc.dram_tensor(nm + "_hi", [nco_, 128, wid_], BF16, kind="Internal").ap(),
                   nc.dram_tensor(nm + "_lo", [nco_, 128, wid_], BF16, kind="Internal").ap())
    ot_hi_d = nc.dram_tensor("ot_hi", [DM, S], BF16, kind="Internal").ap()
    ot_lo_d = nc.dram_tensor("ot_lo", [DM, S], BF16, kind="Internal").ap()
    qhi_d = nc.dram_tensor("q_hi", [DM, S], BF16, kind="Internal").ap()
    qlo_d = nc.dram_tensor("q_lo", [DM, S], BF16, kind="Internal").ap()
    khi_d = nc.dram_tensor("k_hi", [DM, S], BF16, kind="Internal").ap()
    klo_d = nc.dram_tensor("k_lo", [DM, S], BF16, kind="Internal").ap()
    xn_d = nc.dram_tensor("xn_s", [128, DM], F32, kind=skind).ap()
    st_d = nc.dram_tensor("st_s", [128, 64], F32, kind=skind).ap()
    xnt_d = nc.dram_tensor("xnt_s", [128, TA], F32, kind=skind).ap()

    p = Prog()

    import contextlib
    with contextlib.ExitStack() as stack:
        en = stack.enter_context
        sb_id = en(nc.sbuf_tensor("sb_ident", [128, 128], F32))
        sb_h = en(nc.sbuf_tensor("sb_h128", [128, 128], F32))
        sb_ca = en(nc.sbuf_tensor("sb_causal", [128, 128], F32))
        sb_suug = en(nc.sbuf_tensor("sb_su_ug", [128, NCI], F32))
        sb_sudp = en(nc.sbuf_tensor("sb_su_dp", [128, NCO_I], F32))
        sb_wsu = en(nc.sbuf_tensor("wsu64su", [128, NCO_I], F32))
        sb_wsg = en(nc.sbuf_tensor("wsg64", [128, NCO_I], F32))
        sb_wdp = en(nc.sbuf_tensor("wdp64", [128, NCI], F32))
        sb_kg = en(nc.sbuf_tensor("kgT", [128, NCI * NBLK], F32))
        sb_xng = en(nc.sbuf_tensor("xng", [128, 768], F32))
        sb_st = en(nc.sbuf_tensor("stats", [128, 64], F32))
        A_ACT = en(nc.sbuf_tensor("ACT", [128, 32768], F32))
        A_W = en(nc.sbuf_tensor("WST", [128, 4096], F32))
        A_Z = en(nc.sbuf_tensor("ZB", [128, 8192], F32))
        A_X = en(nc.sbuf_tensor("XB", [128, 4096], F32))
        A_S = en(nc.sbuf_tensor("STG", [128, 1536], F32))
        PB = [en(nc.psum_tensor(f"pb{i}", [128, 512], F32)) for i in range(6)]
        TRASH = en(nc.psum_tensor("trash", [128, 1024], F32))
        GATE_PS = TRASH[:, 0:512]
        sems = {s: en(nc.semaphore(s)) for s in p.all_sems()}
        block = en(nc.Block())
        bankv = {i: ("vv", 0) for i in range(8)}

        def stc(o, c, n=1):
            return sb_st[:, o * 24 + c: o * 24 + c + n]
        C_SUM, C_MEAN, C_VEPS, C_STD, C_RSTD = 0, 1, 2, 3, 4
        C_SQ0 = 8
        def stb(c, n=1, ofs=0):
            return sb_st[:, 48 + c + ofs: 48 + c + ofs + n]
        C_NMAX, C_SSUM, C_RS, C_T8 = 0, 2, 4, 8

        pre_toks = [
            p.dma("sync", sb_id[:], ident_d[:, :]),
            p.dma("sync", sb_h[:], h128_d[:, :]),
            p.dma("sync", sb_ca[:], causal_d[:, :]),
            p.dma("sync", sb_suug[:], su_ug_d[:, :]),
            p.dma("sync", sb_sudp[:], su_dp_d[:, :]),
            p.dma("sync", sb_wsu[:], wsu_d[:, :]),
            p.dma("sync", sb_wsg[:], wsg_d[:, :]),
            p.dma("sync", sb_wdp[:], wdp_d[:, :]),
        ]
        p.op("vector", "memset", ap=sb_st[:], constant=0.0, inc=(("vv", 1),))
        stats_init_vv = p.c("vv")
        p.op("vector", "tensor_scalar", wait=tuple(pre_toks), inc=(("vv", 1),),
             out=sb_wsu[:], in0=sb_wsu[:], scalar1=64.0, scalar2=None, op0=OP.mult)
        p.op("vector", "tensor_tensor", inc=(("vv", 1),),
             out=sb_wsu[:], in0=sb_wsu[:], in1=sb_sudp[:], op=OP.mult)
        p.op("vector", "tensor_scalar", inc=(("vv", 1),),
             out=sb_wsg[:], in0=sb_wsg[:], scalar1=64.0, scalar2=None, op0=OP.mult)
        p.op("vector", "tensor_scalar", inc=(("vv", 1),), small=True,
             out=sb_wdp[:], in0=sb_wdp[:], scalar1=64.0, scalar2=None, op0=OP.mult)

        ln_prev_vv = [stats_init_vv, stats_init_vv]

        def layernorm(xa, wx, par):
            p.op("scalar", "activation",
                 wait=(wx, ("vv", ln_prev_vv[par]),),
                 inc=(("sc", 1),),
                 out=TRASH[:, :], in_=xa[:, 0:1024], func=AF.Square,
                 accum_out=stc(par, C_SQ0))
            p.op("scalar", "activation", inc=(("sc", 1),),
                 out=TRASH[:, :], in_=xa[:, 1024:2048], func=AF.Square,
                 accum_out=stc(par, C_SQ0 + 1))
            p.op("scalar", "activation", inc=(("sc", 1),),
                 out=TRASH[:, :], in_=xa[:, 0:1024], func=AF.Copy,
                 accum_out=stc(par, C_SUM))
            p.op("scalar", "activation", inc=(("sc", 1),), small=True,
                 out=TRASH[:, :], in_=xa[:, 1024:2048], func=AF.Copy,
                 accum_out=stc(par, C_SQ0 + 2))
            sq_done = p.c("sc")
            p.op("vector", "tensor_tensor", wait=(("sc", sq_done),),
                 inc=(("vv", 1),), small=True,
                 out=stc(par, C_SUM), in0=stc(par, C_SUM),
                 in1=stc(par, C_SQ0 + 2), op=OP.add)
            p.op("vector", "tensor_tensor", inc=(("vv", 1),), small=True,
                 out=stc(par, C_SQ0), in0=stc(par, C_SQ0),
                 in1=stc(par, C_SQ0 + 1), op=OP.add)
            p.op("vector", "tensor_scalar", inc=(("vv", 1),), small=True,
                 out=stc(par, C_MEAN), in0=stc(par, C_SUM), scalar1=1.0 / DM,
                 scalar2=None, op0=OP.mult)
            p.op("vector", "tensor_scalar", inc=(("vv", 1),), small=True,
                 out=stc(par, C_SQ0), in0=stc(par, C_SQ0), scalar1=1.0 / DM,
                 scalar2=EPS, op0=OP.mult, op1=OP.add)
            p.op("vector", "tensor_scalar", inc=(("vv", 1),), small=True,
                 out=stc(par, C_VEPS), in0=stc(par, C_MEAN),
                 scalar1=stc(par, C_MEAN), scalar2=-1.0, op0=OP.mult, op1=OP.mult)
            p.op("vector", "tensor_tensor", inc=(("vv", 1),), small=True,
                 out=stc(par, C_VEPS), in0=stc(par, C_SQ0), in1=stc(par, C_VEPS),
                 op=OP.add)
            veps_done = p.c("vv")
            p.op("scalar", "activation", wait=(("vv", veps_done),), inc=(("sc", 1),),
                 small=True, out=stc(par, C_STD), in_=stc(par, C_VEPS), func=AF.Sqrt)
            p.op("vector", "reciprocal", wait=(("sc", p.c("sc")),), inc=(("vv", 1),),
                 small=True, out=stc(par, C_RSTD), in_=stc(par, C_STD))
            p.op("vector", "tensor_scalar", inc=(("vv", 1),),
                 out=xa[:], in0=xa[:], scalar1=stc(par, C_MEAN),
                 scalar2=stc(par, C_RSTD), op0=OP.subtract, op1=OP.mult)
            ln_prev_vv[par] = p.c("vv")
            return p.c("vv")


        def transpose_tiles(srcs, dests, ln_done, banks, sus=None, scale=None,
                            group_dests=None, group_dests_lo=None, extra_wait=(),
                            ready=None):
            n = len(srcs)
            for g in range((n + 3) // 4):
                b = banks[g % len(banks)]
                n_in_g = min(4, n - g * 4)
                for j in range(n_in_g):
                    i = g * 4 + j
                    w = ()
                    if j == 0:
                        r = ready if ready is not None else ("vv", ln_done)
                        w = (r, (bankv[b][0], bankv[b][1])) + tuple(extra_wait)
                    p.op("tensor", "transpose", wait=w,
                         inc=((("tm", 1),) if j == n_in_g - 1 else ()),
                         out=PB[b][:, j * 128:(j + 1) * 128], in_=srcs[i],
                         identity=sb_id[:])
                tmc = p.c("tm")
                if group_dests is not None:
                    p.op("vector", "tensor_copy", wait=(("tm", tmc),),
                         inc=(("vv", 1),),
                         out=group_dests[g], in_=PB[b][:, 0:n_in_g * 128])
                    if group_dests_lo is not None:
                        gsh = group_dests[g].shape
                        p.op("vector", "tensor_tensor", inc=(("vv", 1),),
                             out=group_dests_lo[g],
                             in0=PB[b][:, 0:n_in_g * 128].rearrange(
                                 "p (a b) -> p a b", b=gsh[-1]),
                             in1=group_dests[g], op=OP.subtract)
                else:
                    for j in range(n_in_g):
                        i = g * 4 + j
                        if sus is None and scale is None:
                            p.op("vector", "tensor_copy",
                                 wait=(("tm", tmc),) if j == 0 else (), inc=(("vv", 1),),
                                 out=dests[i], in_=PB[b][:, j * 128:(j + 1) * 128])
                        else:
                            s1 = sus[i] if sus is not None else 1.0
                            p.op("vector", "tensor_scalar",
                                 wait=(("tm", tmc),) if j == 0 else (), inc=(("vv", 1),),
                                 out=dests[i], in0=PB[b][:, j * 128:(j + 1) * 128],
                                 scalar1=s1, scalar2=scale,
                                 op0=OP.mult, op1=(OP.bypass if scale is None else OP.mult))
                bankv[b] = ("vv", p.c("vv"))
            return p.c("vv")

        NSTG = 3
        stg_slots = [A_S[:, i * 512:(i + 1) * 512] for i in range(NSTG)]
        stg_free = [()] * NSTG
        stg_i = [0]

        def stage_out(bank, cols, dram_slice, mm_wait, scale_col=None):
            si = stg_i[0] % NSTG
            stg_i[0] += 1
            slot = stg_slots[si][:, 0:cols]
            if scale_col is None:
                p.op("vector", "tensor_copy",
                     wait=(("tm", mm_wait),) + tuple(stg_free[si]),
                     inc=(("vv", 1),), out=slot, in_=PB[bank][:, 0:cols])
            else:
                p.op("vector", "tensor_scalar",
                     wait=(("tm", mm_wait),) + tuple(stg_free[si]),
                     inc=(("vv", 1),), out=slot, in0=PB[bank][:, 0:cols],
                     scalar1=scale_col, scalar2=None, op0=OP.mult)
            ev = p.c("vv")
            tok = p.dma("gpsimd", dram_slice, slot, wait=(("vv", ev),))
            stg_free[si] = (tok,)
            bankv[bank] = ("vv", ev)
            return ev

        def stage_out_split(bank, hi_slice, lo_slice, mm_wait):
            si = stg_i[0] % NSTG
            stg_i[0] += 1
            s_hi = stg_slots[si][:, 0:256].bitcast(BF16)
            s_lo = stg_slots[si][:, 256:512].bitcast(BF16)
            p.op("vector", "tensor_copy",
                 wait=(("tm", mm_wait),) + tuple(stg_free[si]),
                 inc=(("vv", 1),), out=s_hi, in_=PB[bank][:])
            p.op("vector", "tensor_tensor", inc=(("vv", 1),),
                 out=s_lo, in0=PB[bank][:], in1=s_hi, op=OP.subtract)
            ev = p.c("vv")
            t1 = p.dma("gpsimd", hi_slice, s_hi, wait=(("vv", ev),))
            t2 = p.dma("gpsimd", lo_slice, s_lo, wait=(("vv", ev),))
            stg_free[si] = (t1, t2)
            bankv[bank] = ("vv", ev)
            return ev

        w_slots = [A_W[:, i * 2048:(i + 1) * 2048] for i in range(2)]
        w_hi = [A_W[:, i * 2048: i * 2048 + 1024].bitcast(BF16) for i in range(2)]
        w_lo = [A_W[:, i * 2048 + 1024:(i + 1) * 2048].bitcast(BF16) for i in range(2)]
        w_free = [("tm", 0)] * 2
        w_i = [0]

        def w_load(dram_slice, extra_wait=()):
            si = w_i[0] % 2
            w_i[0] += 1
            tok = p.dma("sync", w_slots[si], dram_slice,
                        wait=((w_free[si][0], w_free[si][1]),) + tuple(extra_wait))
            return si, tok

        def w_load_pair(whi_slice, wlo_slice, extra_wait=()):
            si = w_i[0] % 2
            w_i[0] += 1
            t1 = p.dma("sync", w_hi[si], whi_slice,
                       wait=((w_free[si][0], w_free[si][1]),) + tuple(extra_wait))
            t2 = p.dma("sync", w_lo[si], wlo_slice)
            return si, t1, t2

        def mm3(bank_ap, si, rhs_hi, rhs_lo, nci, first_wait, width=128):
            first = True
            for wsel, xsel in ((w_hi[si], rhs_hi), (w_hi[si], rhs_lo),
                               (w_lo[si], rhs_hi)):
                for ci in range(nci):
                    w = first_wait if first else ()
                    last = (wsel is w_lo[si]) and ci == nci - 1
                    p.op("tensor", "matmul", wait=w,
                         inc=((("tm", 1),) if last else ()),
                         out=bank_ap,
                         lhsT=wsel[:, ci * width:(ci + 1) * width],
                         rhs=xsel(ci),
                         start=first, stop=last)
                    first = False
            return p.c("tm")

        xa_slots = [A_X[:, 0:2048], A_X[:, 2048:4096]]
        xa_free = [("vv", 0), ("vv", 0)]

        land = [A_Z[:, i * 2048:(i + 1) * 2048] for i in range(2)]
        phi = [A_Z[:, 4096 + i * 2048: 4096 + i * 2048 + 1024].bitcast(BF16)
               for i in range(2)]
        plo = [A_Z[:, 4096 + i * 2048 + 1024: 4096 + (i + 1) * 2048].bitcast(BF16)
               for i in range(2)]
        land_free = [("vv", 0), ("vv", 0)]
        pair_free = [[], []]
        ji = [0]
        prol_done = {}

        def split_chunk(fp32_slice, hi_slice, lo_slice):
            sl = ji[0] % 2
            ji[0] += 1
            t_in = p.dma("scalar", land[sl], fp32_slice,
                         wait=((land_free[sl][0], land_free[sl][1]),))
            p.op("vector", "tensor_copy", wait=(t_in,) + tuple(pair_free[sl]),
                 inc=(("vv", 1),), out=phi[sl], in_=land[sl])
            p.op("vector", "tensor_tensor", inc=(("vv", 1),),
                 out=plo[sl], in0=land[sl], in1=phi[sl], op=OP.subtract)
            land_free[sl] = ("vv", p.c("vv"))
            ev = p.c("vv")
            t1 = p.dma("scalar", hi_slice, phi[sl], wait=(("vv", ev),))
            t2 = p.dma("scalar", lo_slice, plo[sl], wait=(("vv", ev),))
            pair_free[sl] = [t1, t2]
            return (t1, t2)

        for nm, dram, nco_, wid_ in (("wq", wq_d, NCI, DM), ("wk", wk_d, NCI, DM),
                                     ("wv", wv_d, NCI, DM), ("wo", wo_d, NCI, DM),
                                     ("wup", wup_d, NCO_I, DM),
                                     ("wgt", wgt_d, NCO_I, DM),
                                     ("wdn", wdn_d, NCI, IN)):
            whi_t, wlo_t = wsp[nm]
            toks = None
            for co in range(nco_):
                for sub in range(wid_ // 2048):
                    sl_ = slice(sub * 2048, (sub + 1) * 2048)
                    toks = split_chunk(dram[co, :, sl_], whi_t[co, :, sl_],
                                       wlo_t[co, :, sl_])
            prol_done[nm] = tuple((s_, p.c(s_)) for s_ in
                                  {toks[0][0], toks[1][0]})
        prol_all = tuple((s_, p.c(s_)) for s_ in p.POOL["scalar"])

        if "A" in phases:
            xnT = A_ACT
            xnT_hi = A_ACT[:, 0:8192].bitcast(BF16)
            xnT_lo = A_ACT[:, 8192:16384].bitcast(BF16)
            xnT_free = ("tm", 0)
            for tc in range(NCA):
                t0 = tc * TA
                for s in range(8):
                    sl = s % 2
                    xa = xa_slots[sl]
                    wx = p.dma("sync", xa, x_d[t0 + s * 128: t0 + s * 128 + 128, :],
                               wait=((xa_free[sl][0], xa_free[sl][1]),))
                    ln_done = layernorm(xa, wx, s % 2)
                    srcs = [xa[:, ec * 128:(ec + 1) * 128] for ec in range(NCI)]
                    ghi = [xnT_hi.rearrange("p (c t) -> p c t", t=TA)
                           [:, g * 4:g * 4 + 4, s * 128:s * 128 + 128]
                           for g in range(4)]
                    glo = [xnT_lo.rearrange("p (c t) -> p c t", t=TA)
                           [:, g * 4:g * 4 + 4, s * 128:s * 128 + 128]
                           for g in range(4)]
                    if dbg and tc == 0 and s == 0:
                        p.dma("gpsimd", xn_d[:, :], xa[:], wait=(("vv", ln_done),))
                        p.dma("gpsimd", st_d[:, :], sb_st[:, 0:64], wait=(("vv", ln_done),))
                        p.barrier()
                    td = transpose_tiles(srcs, None, ln_done, (0, 1),
                                         group_dests=ghi, group_dests_lo=glo,
                                         extra_wait=((xnT_free[0], xnT_free[1]),)
                                         if s == 0 else ())
                    xa_free[sl] = ("vv", td)
                if dbg and tc == 0:
                    p.dma("gpsimd", xnt_d[:, :], xnT[:, 0:TA],
                          wait=(("vv", p.c("vv")),))
                xnT_done = p.c("vv")
                for ci in range(NCI):
                    p.op("vector", "reduce_sum", wait=(("vv", xnT_done),),
                         inc=(("vv", 1),),
                         out=sb_xng[:, ci * 16:(ci + 1) * 16].unsqueeze(2),
                         in_=xnT_hi[:, ci * TA:(ci + 1) * TA].rearrange(
                             "p (b t) -> p b t", t=BLK),
                         axis=AX.X)
                    p.op("vector", "reduce_sum",
                         inc=(("vv", 1),), small=(ci == NCI - 1),
                         out=sb_xng[:, 256 + ci * 16: 256 + ci * 16 + 16].unsqueeze(2),
                         in_=xnT_lo[:, ci * TA:(ci + 1) * TA].rearrange(
                             "p (b t) -> p b t", t=BLK),
                         axis=AX.X)
                p.op("vector", "tensor_tensor", inc=(("vv", 1),),
                     out=sb_xng[:, 0:256], in0=sb_xng[:, 0:256],
                     in1=sb_xng[:, 256:512], op=OP.add)
                p.op("vector", "tensor_scalar", inc=(("vv", 1),),
                     out=sb_xng[:, 0:256], in0=sb_xng[:, 0:256],
                     scalar1=1.0 / BLK, scalar2=None, op0=OP.mult)
                xng_hi = sb_xng[:, 512:640].bitcast(BF16)
                xng_lo = sb_xng[:, 640:768].bitcast(BF16)
                p.op("vector", "tensor_copy", inc=(("vv", 1),),
                     out=xng_hi, in_=sb_xng[:, 0:256])
                p.op("vector", "tensor_tensor", inc=(("vv", 1),), small=True,
                     out=xng_lo, in0=sb_xng[:, 0:256], in1=xng_hi, op=OP.subtract)
                xng_done = p.c("vv")

                for wnm, spill, iskg in (("wq", (qhi_d, qlo_d), False),
                                         ("wk", (khi_d, klo_d), True),
                                         ("wv", vt_d, False)):
                    whi_t, wlo_t = wsp[wnm]
                    for co in range(NCI):
                        si, wd1, wd2 = w_load_pair(whi_t[co, :, :], wlo_t[co, :, :],
                                                   extra_wait=prol_done[wnm]
                                                   if tc == 0 and co == 0 else ())
                        for th in range(2):
                            b = 2 + th
                            fw = (wd1, wd2, ("vv", xnT_done),
                                  (bankv[b][0], bankv[b][1]))
                            mm3(PB[b][:], si,
                                lambda ci, th=th: xnT_hi[:, ci * TA + th * 512:
                                                         ci * TA + th * 512 + 512],
                                lambda ci, th=th: xnT_lo[:, ci * TA + th * 512:
                                                         ci * TA + th * 512 + 512],
                                NCI, fw)
                            if wnm == "wv":
                                stage_out(b, 512,
                                          spill[co * 128:(co + 1) * 128,
                                                t0 + th * 512: t0 + th * 512 + 512],
                                          p.c("tm"))
                            else:
                                stage_out_split(
                                    b,
                                    spill[0][co * 128:(co + 1) * 128,
                                             t0 + th * 512: t0 + th * 512 + 512],
                                    spill[1][co * 128:(co + 1) * 128,
                                             t0 + th * 512: t0 + th * 512 + 512],
                                    p.c("tm"))
                        if iskg:
                            b = 4
                            fw = (("vv", xng_done), (bankv[b][0], bankv[b][1]))
                            mm3(PB[b][:, 0:16], si,
                                lambda ci: xng_hi[:, ci * 16:(ci + 1) * 16],
                                lambda ci: xng_lo[:, ci * 16:(ci + 1) * 16],
                                NCI, fw)
                            p.op("vector", "tensor_copy", wait=(("tm", p.c("tm")),),
                                 inc=(("vv", 1),), small=True,
                                 out=sb_kg[:, co * NBLK + tc * 16:
                                           co * NBLK + tc * 16 + 16],
                                 in_=PB[b][:, 0:16])
                            bankv[b] = ("vv", p.c("vv"))
                        w_free[si] = ("tm", p.c("tm"))
                xnT_free = ("tm", p.c("tm"))
            if dbg:
                p.dma("gpsimd", kg_d[:, :], sb_kg[:], wait=(("vv", p.c("vv")),))
            p.barrier()

        if "F" in phases:
            ACT = A_ACT
            Z1 = A_Z
            actb = A_ACT[:, :].bitcast(BF16)
            z1b = A_Z[:, :].bitcast(BF16)
            z1hi = lambda ci: z1b[:, ci * 1024: ci * 1024 + 512]
            z1lo = lambda ci: z1b[:, ci * 1024 + 512: ci * 1024 + 1024]
            act_free = ("tm", 0)
            z1_free = ("tm", 0)
            z1_prol = prol_all

            def f_front_pieces(tc):
                t0 = tc * TF

                def mk_ln_tr(s):
                    def go():
                        sl = s % 2
                        xa = xa_slots[sl]
                        wx = p.dma("sync", xa,
                                   x_d[t0 + s * 128: t0 + s * 128 + 128, :],
                                   wait=((xa_free[sl][0], xa_free[sl][1]),))
                        ln_done = layernorm(xa, wx, s % 2)
                        srcs = [xa[:, ec * 128:(ec + 1) * 128] for ec in range(NCI)]
                        dsts = [Z1[:, ec * TF + s * 128: ec * TF + s * 128 + 128]
                                for ec in range(NCI)]
                        sus = [sb_suug[:, ec:ec + 1] for ec in range(NCI)]
                        ew = ((z1_free[0], z1_free[1]),)
                        if tc == 0:
                            ew = ew + z1_prol
                        td = transpose_tiles(srcs, dsts, ln_done, (0, 1), sus=sus,
                                             extra_wait=ew if s == 0 else ())
                        xa_free[sl] = ("vv", td)
                    return go

                def h16():
                    hh = 1
                    while hh < NCI:
                        for base in range(0, NCI, 2 * hh):
                            for j in range(hh):
                                a = (base + j) * TF
                                bcol = (base + j + hh) * TF
                                p.op("vector", "tensor_tensor", inc=(("vv", 1),),
                                     out=Z1[:, a:a + TF], in0=Z1[:, a:a + TF],
                                     in1=Z1[:, bcol:bcol + TF], op=OP.add)
                                p.op("vector", "scalar_tensor_tensor",
                                     inc=(("vv", 1),),
                                     out=Z1[:, bcol:bcol + TF],
                                     in0=Z1[:, bcol:bcol + TF],
                                     scalar=-2.0, in1=Z1[:, a:a + TF],
                                     op0=OP.mult, op1=OP.add)
                        hh *= 2

                def h128_1():
                    bf1_done = p.c("vv")
                    for ci in range(NCI):
                        b = 2 + ci % 2
                        p.op("tensor", "matmul",
                             wait=(("vv", bf1_done), (bankv[b][0], bankv[b][1]),),
                             inc=(("tm", 1),),
                             out=PB[b][:], lhsT=sb_h[:],
                             rhs=Z1[:, ci * TF:(ci + 1) * TF],
                             start=True, stop=True)
                        zh = z1b[:, ci * 1024: ci * 1024 + 512]
                        zl = z1b[:, ci * 1024 + 512: ci * 1024 + 1024]
                        p.op("vector", "tensor_scalar", wait=(("tm", p.c("tm")),),
                             inc=(("vv", 1),),
                             out=zh, in0=PB[b][:], scalar1=S1,
                             scalar2=None, op0=OP.mult)
                        p.op("vector", "scalar_tensor_tensor", inc=(("vv", 1),),
                             out=zl, in0=PB[b][:], scalar=S1, in1=zh,
                             op0=OP.mult, op1=OP.subtract)
                        bankv[b] = ("vv", p.c("vv"))
                return [mk_ln_tr(0), mk_ln_tr(1), mk_ln_tr(2), mk_ln_tr(3),
                        h16, h128_1]

            def f_upgate_h64(tc):
                nonlocal act_free

                def bfly(a, bcol):
                    p.op("vector", "tensor_tensor", inc=(("vv", 1),),
                         out=ACT[:, a * TF:a * TF + TF], in0=ACT[:, a * TF:a * TF + TF],
                         in1=ACT[:, bcol * TF:bcol * TF + TF], op=OP.add)
                    p.op("vector", "scalar_tensor_tensor", inc=(("vv", 1),),
                         out=ACT[:, bcol * TF:bcol * TF + TF],
                         in0=ACT[:, bcol * TF:bcol * TF + TF],
                         scalar=-2.0, in1=ACT[:, a * TF:a * TF + TF],
                         op0=OP.mult, op1=OP.add)

                z1_done = p.c("vv")
                for co in range(NCO_I):
                    first_use = tc == 0 and co == 0
                    whi_t, wlo_t = wsp["wup"]
                    siu, wu1, wu2 = w_load_pair(whi_t[co, :, :], wlo_t[co, :, :],
                                                extra_wait=prol_done["wup"]
                                                if first_use else ())
                    whi_t, wlo_t = wsp["wgt"]
                    sig, wg1, wg2 = w_load_pair(whi_t[co, :, :], wlo_t[co, :, :],
                                                extra_wait=prol_done["wgt"]
                                                if first_use else ())
                    fw = (wu1, wu2, ("vv", z1_done), (bankv[4][0], bankv[4][1]))
                    mm_u = mm3(PB[4][:], siu, z1hi, z1lo, NCI, fw)
                    fw = (wg1, wg2, (bankv[5][0], bankv[5][1]))
                    if co == 0:
                        fw = fw + ((act_free[0], act_free[1]),)
                    mm_g = mm3(PB[5][:], sig, z1hi, z1lo, NCI, fw)
                    w_free[siu] = ("tm", mm_u)
                    w_free[sig] = ("tm", mm_g)
                    si = stg_i[0] % NSTG
                    stg_i[0] += 1
                    p.op("vector", "tensor_scalar",
                         wait=(("tm", mm_u),) + tuple(stg_free[si]),
                         inc=(("vv", 1),),
                         out=stg_slots[si], in0=PB[4][:],
                         scalar1=sb_wsu[:, co:co + 1], scalar2=None, op0=OP.mult)
                    bankv[4] = ("vv", p.c("vv"))
                    p.op("scalar", "activation",
                         wait=(("tm", mm_g),) + (((act_free[0], act_free[1]),)
                                                 if co == 0 else ()),
                         inc=(("sc", 1),),
                         out=ACT[:, co * TF:(co + 1) * TF], in_=PB[5][:],
                         func=AF.Silu, scale=sb_wsg[:, co:co + 1])
                    sg_done = p.c("sc")
                    bankv[5] = ("sc", sg_done)
                    p.op("vector", "tensor_tensor", wait=(("sc", sg_done),),
                         inc=(("vv", 1),),
                         out=ACT[:, co * TF:(co + 1) * TF], in0=stg_slots[si],
                         in1=ACT[:, co * TF:(co + 1) * TF], op=OP.mult)
                    stg_free[si] = (("vv", p.c("vv")),)
                    for s in range(6):
                        blk_sz = 2 << s
                        if (co + 1) % blk_sz == 0:
                            base = co + 1 - blk_sz
                            for j in range(1 << s):
                                bfly(base + j, base + j + (1 << s))
                return p.c("vv")

            def f_h128_2(tc, bf_done):
                qvv = []
                for co in range(NCO_I):
                    b = 2 + co % 2
                    p.op("tensor", "matmul",
                         wait=((("vv", bf_done),) if co == 0 else ())
                         + ((bankv[b][0], bankv[b][1]),),
                         inc=(("tm", 1),),
                         out=PB[b][:], lhsT=sb_h[:],
                         rhs=ACT[:, co * TF:(co + 1) * TF],
                         start=True, stop=True)
                    ahi = actb[:, co * 1024: co * 1024 + 512]
                    alo = actb[:, co * 1024 + 512: co * 1024 + 1024]
                    p.op("vector", "tensor_scalar", wait=(("tm", p.c("tm")),),
                         inc=(("vv", 1),),
                         out=ahi, in0=PB[b][:], scalar1=S2, scalar2=None,
                         op0=OP.mult)
                    p.op("vector", "scalar_tensor_tensor", inc=(("vv", 1),),
                         out=alo, in0=PB[b][:], scalar=S2, in1=ahi,
                         op0=OP.mult, op1=OP.subtract)
                    bankv[b] = ("vv", p.c("vv"))
                    if co % 16 == 15:
                        qvv.append(p.c("vv"))
                return qvv

            def f_down(tc, qvv, pieces):
                t0 = tc * TF
                whi_t, wlo_t = wsp["wdn"]
                for oc in range(NCI):
                    b = 4 + oc % 2
                    T0 = p.c("tm")
                    qinfo = []
                    for q in range(4):
                        ew = (("tm", T0 + q - 1),) if q >= 2 else ()
                        if tc == 0 and oc == 0 and q < 2:
                            ew = ew + prol_done["wdn"]
                        sq, w1, w2 = w_load_pair(
                            whi_t[oc, :, q * 2048:(q + 1) * 2048],
                            wlo_t[oc, :, q * 2048:(q + 1) * 2048], extra_wait=ew)
                        qinfo.append((sq, w1, w2))
                    first = True
                    for q in range(4):
                        sq, w1, w2 = qinfo[q]
                        for wsel, xoff in ((w_hi[sq], 0), (w_hi[sq], 512),
                                           (w_lo[sq], 0)):
                            for c16 in range(16):
                                ci = q * 16 + c16
                                w = ()
                                if c16 == 0 and xoff == 0:
                                    w = (w1, w2, ("vv", qvv[q]))
                                    if first:
                                        w = w + ((bankv[b][0], bankv[b][1]),)
                                last = (q == 3 and wsel is w_lo[sq]
                                        and c16 == 15)
                                p.op("tensor", "matmul", wait=w,
                                     inc=((("tm", 1),)
                                          if (wsel is w_lo[sq] and c16 == 15)
                                          else ()),
                                     out=PB[b][:],
                                     lhsT=wsel[:, c16 * 128: c16 * 128 + 128],
                                     rhs=actb[:, ci * 1024 + xoff:
                                              ci * 1024 + xoff + 512],
                                     start=first, stop=last)
                                first = False
                    mm_d = p.c("tm")
                    for sq, _, _ in qinfo:
                        w_free[sq] = ("tm", mm_d)
                    stage_out(b, 512, ft_d[oc * 128:(oc + 1) * 128, t0:t0 + TF],
                              mm_d, scale_col=sb_wdp[:, oc:oc + 1])
                    if pieces and oc % 2 == 1 and oc // 2 < len(pieces):
                        pieces[oc // 2]()

            pieces = f_front_pieces(0)
            for piece in pieces:
                piece()
            for tc in range(NCF):
                bf_done = f_upgate_h64(tc)
                z1_free = ("tm", p.c("tm"))
                qvv = f_h128_2(tc, bf_done)
                nxt = f_front_pieces(tc + 1) if tc + 1 < NCF else []
                f_down(tc, qvv, nxt)
                act_free = ("tm", p.c("tm"))
            p.barrier()

        if "B" in phases:
            KB = [A_ACT[:, 0:4096], A_ACT[:, 4096:8192]]
            VTB = A_ACT[:, 8192:12288]
            VB = A_ACT[:, 16384:20480]
            SCRP = [A_ACT[:, 12288:16384], A_ACT[:, 20480:24576]]
            PT = A_ACT[:, 24576:28672]
            QH = [A_ACT[:, 28672:29696], A_ACT[:, 29696:30720]]
            GM8 = [A_ACT[:, 30720:31232], A_ACT[:, 31232:31744]]
            GT8 = A_ACT[:, 31744:32256]
            BIA8 = A_ACT[:, 32256:32768]
            OS = [A_S[:, 0:128], A_S[:, 128:256]]
            KGH = A_S[:, 256:768].bitcast(BF16)
            KGL = A_S[:, 768:1280].bitcast(BF16)
            T88 = A_S[:, 1280:1344]

            p.op("vector", "tensor_copy", wait=(("vv", p.c("vv")),),
                 inc=(("vv", 1),), out=KGH, in_=sb_kg[:])
            p.op("vector", "tensor_tensor", inc=(("vv", 1),), small=True,
                 out=KGL, in0=sb_kg[:], in1=KGH, op=OP.subtract)
            kg_split = p.c("vv")

            kb_free = [("tm", 0), ("tm", 0)]
            qh_free = [("tm", 0), ("tm", 0)]
            gm_free = [("vv", 0), ("vv", 0)]
            os_free = [(), ()]
            vtb_free = ("tm", 0)
            vb_free = ("tm", 0)
            scr_free = [("tm", 0), ("tm", 0)]
            pt_free = ("tm", 0)
            SBK = (2, 3, 4)
            TBK = (0, 1)
            NG = NQT // 8

            def emit_gate_group(h, g):
                gp = g % 2
                tq1 = p.dma("sync", QH[gp][:, 0:512].bitcast(BF16),
                            qhi_d[h * 128:(h + 1) * 128,
                                  g * 1024:(g + 1) * 1024],
                            wait=((qh_free[gp][0], qh_free[gp][1]),))
                tq2 = p.dma("sync", QH[gp][:, 512:1024].bitcast(BF16),
                            qlo_d[h * 128:(h + 1) * 128,
                                  g * 1024:(g + 1) * 1024])
                tgm = p.dma("sync", GM8[gp],
                            gmask_d[:, (h * 0 + g * 8) * 64:(g * 8 + 8) * 64],
                            wait=((gm_free[gp][0], gm_free[gp][1]),))
                qhi8 = QH[gp][:, 0:512].bitcast(BF16)
                qlo8 = QH[gp][:, 512:1024].bitcast(BF16)
                first = True
                for j in range(8):
                    for lh, rh in ((qhi8, KGH), (qhi8, KGL), (qlo8, KGH)):
                        w = ()
                        if first:
                            w = (tq1, tq2, ("vv", kg_split),
                                 (bankv[6][0], bankv[6][1]))
                            first = False
                        p.op("tensor", "matmul", wait=w,
                             inc=((("tm", 1),) if (j == 7 and lh is qlo8)
                                  else ()),
                             out=GATE_PS[:, j * 64:(j + 1) * 64],
                             lhsT=lh[:, j * 128:(j + 1) * 128],
                             rhs=rh[:, h * 64:(h + 1) * 64],
                             start=(lh is qhi8 and rh is KGH),
                             stop=(lh is qlo8))
                gmm = p.c("tm")
                p.op("vector", "scalar_tensor_tensor",
                     wait=(("tm", gmm), tgm), inc=(("vv", 1),),
                     out=GT8[:, :], in0=GATE_PS[:, 0:512], scalar=1.0,
                     in1=GM8[gp], op0=OP.mult, op1=OP.add)
                bankv[6] = ("vv", p.c("vv"))
                for j in range(8):
                    p.op("vector", "max", inc=(("vv", 1),), small=(j == 7),
                         out=T88[:, j * 8: j * 8 + 8],
                         in_=GT8[:, j * 64:(j + 1) * 64])
                for j in range(8):
                    p.op("vector", "tensor_scalar", inc=(("vv", 1),),
                         small=(j == 7),
                         out=GM8[gp][:, j * 64:(j + 1) * 64],
                         in0=GT8[:, j * 64:(j + 1) * 64],
                         scalar1=T88[:, j * 8 + 7: j * 8 + 8],
                         scalar2=NEG, op0=OP.is_lt, op1=OP.mult)
                return p.c("vv"), (tq1, tq2), gp

            def emit_tail(h, qt, p_ready):
                nonlocal pt_free
                njt = qt + 1
                par = qt % 2
                srcs = [SCRP[par][:, k * 128:(k + 1) * 128] for k in range(njt)]
                gdsts = [PT[:, g2 * 512: g2 * 512 + min(4, njt - g2 * 4) * 128]
                         for g2 in range((njt + 3) // 4)]
                pt_done = transpose_tiles(srcs, None, 0, TBK, group_dests=gdsts,
                                          extra_wait=(("tm", pt_free[1]),),
                                          ready=p_ready)
                for j in range(njt):
                    w = ()
                    if j == 0:
                        w = (("vv", pt_done), ("vv", v_done[0]),
                             (bankv[5][0], bankv[5][1]))
                    p.op("tensor", "matmul", wait=w,
                         inc=((("tm", 1),) if j == njt - 1 else ()),
                         out=PB[5][:, 0:128], lhsT=VB[:, j * 128:(j + 1) * 128],
                         rhs=PT[:, j * 128:(j + 1) * 128],
                         start=(j == 0), stop=(j == njt - 1))
                pv_mm = p.c("tm")
                pt_free = ("tm", pv_mm)
                scr_free[par] = ("tm", pv_mm)
                oslot = qt % 2
                os_hi = OS[oslot][:, 0:64].bitcast(BF16)
                os_lo = OS[oslot][:, 64:128].bitcast(BF16)
                p.op("vector", "tensor_copy",
                     wait=(("tm", pv_mm),) + tuple(os_free[oslot]),
                     inc=(("vv", 1),),
                     out=os_hi, in_=PB[5][:, 0:128])
                p.op("vector", "tensor_tensor", inc=(("vv", 1),),
                     out=os_lo, in0=PB[5][:, 0:128], in1=os_hi, op=OP.subtract)
                bankv[5] = ("vv", p.c("vv"))
                t_hi = p.dma("gpsimd",
                             ot_hi_d[h * 128:(h + 1) * 128,
                                     qt * 128:(qt + 1) * 128],
                             os_hi, wait=(("vv", p.c("vv")),))
                t_lo = p.dma("gpsimd",
                             ot_lo_d[h * 128:(h + 1) * 128,
                                     qt * 128:(qt + 1) * 128],
                             os_lo)
                os_free[oslot] = (t_hi, t_lo)

            v_done = [0]
            for h in range(NH):
                hs = h % 2
                khi = KB[hs][:, 0:2048].bitcast(BF16)
                klo = KB[hs][:, 2048:4096].bitcast(BF16)
                tk1 = p.dma("sync", khi, khi_d[h * 128:(h + 1) * 128, :],
                            wait=((kb_free[hs][0], kb_free[hs][1]),))
                tk2 = p.dma("sync", klo, klo_d[h * 128:(h + 1) * 128, :])
                wv_ld = p.dma("sync", VTB, vt_d[h * 128:(h + 1) * 128, :],
                              wait=((vtb_free[0], vtb_free[1]),))
                srcs = [VTB[:, k * 128:(k + 1) * 128] for k in range(32)]
                gdsts = [VB[:, g2 * 512:(g2 + 1) * 512] for g2 in range(8)]
                v_done[0] = transpose_tiles(srcs, None, 0, TBK, group_dests=gdsts,
                                            extra_wait=(wv_ld,
                                                        (vb_free[0], vb_free[1]),))
                vtb_free = ("tm", p.c("tm"))
                bias_done, _, _ = emit_gate_group(h, 0)
                bias_vv = {0: bias_done}
                prev1 = None
                prev2 = None
                for qt in range(NQT):
                    g = qt // 8
                    par = qt % 2
                    kvlen = (qt + 1) * 128
                    nkc = (kvlen + 511) // 512
                    gp = g % 2
                    qhi8 = QH[gp][:, 0:512].bitcast(BF16)
                    qlo8 = QH[gp][:, 512:1024].bitcast(BF16)
                    jq = qt % 8
                    if prev2 is not None:
                        emit_tail(h, prev2[0], prev2[1])
                        prev2 = None
                    if jq == 4 and g + 1 < NG:
                        bias_vv[g + 1], _, _ = emit_gate_group(h, g + 1)
                    qk_mms = []
                    qk_handles = []
                    for kc in range(nkc):
                        kcols = min(512, kvlen - kc * 512)
                        b = SBK[kc % 3]
                        first = True
                        for lh, rh in ((qhi8, khi), (qhi8, klo), (qlo8, khi)):
                            w = ()
                            if first:
                                w = (tk1, tk2, (bankv[b][0], bankv[b][1]))
                            hnd = p.op("tensor", "matmul", wait=w,
                                 inc=((("tm", 1),) if lh is qlo8 else ()),
                                 out=PB[b][:, 0:kcols],
                                 lhsT=lh[:, jq * 128:(jq + 1) * 128],
                                 rhs=rh[:, kc * 512: kc * 512 + kcols],
                                 start=first, stop=(lh is qlo8))
                            if first:
                                qk_handles.append(hnd)
                                first = False
                        qk_mms.append((b, p.c("tm")))
                    if jq == 7:
                        qh_free[gp] = ("tm", p.c("tm"))
                    for kc in range(nkc):
                        kcols = min(512, kvlen - kc * 512)
                        b, s_mm = qk_mms[kc]
                        nb = kcols // 64
                        p.op("vector", "scalar_tensor_tensor",
                             wait=(("tm", s_mm), ("vv", bias_vv[g]),)
                             + ((("tm", scr_free[par][1]),) if kc == 0 else ()),
                             inc=(("vv", 1),),
                             out=SCRP[par][:, kc * 512: kc * 512 + kcols].rearrange(
                                 "p (a b) -> p a b", b=64),
                             in0=PB[b][:, 0:kcols].rearrange("p (a b) -> p a b", b=64),
                             scalar=SM_SCALE,
                             in1=GM8[g % 2][:, jq * 64 + kc * 8:
                                            jq * 64 + kc * 8 + nb]
                             .unsqueeze(2).broadcast_to([128, nb, 64]),
                             op0=OP.mult, op1=OP.add)
                        bankv[b] = ("vv", p.c("vv"))
                        if kc + 3 < nkc:
                            p.add_wait(qk_handles[kc + 3], "vv", p.c("vv"))
                    if jq == 7:
                        gm_free[g % 2] = ("vv", p.c("vv"))
                    p.op("vector", "tensor_tensor", inc=(("vv", 1),),
                         out=SCRP[par][:, kvlen - 128: kvlen],
                         in0=SCRP[par][:, kvlen - 128: kvlen], in1=sb_ca[:],
                         op=OP.add)
                    p.op("vector", "reduce_max", inc=(("vv", 1),), small=True,
                         out=stb(C_NMAX, ofs=par), in_=SCRP[par][:, 0:kvlen],
                         axis=AX.X, negate=True)
                    sm_stats = p.c("vv")
                    p.op("scalar", "activation", wait=(("vv", sm_stats),),
                         inc=(("sc", 1),), small=True,
                         out=SCRP[par][:, 0:kvlen], in_=SCRP[par][:, 0:kvlen],
                         func=AF.Exp, bias=stb(C_NMAX, ofs=par), scale=1.0,
                         accum_out=stb(C_SSUM, ofs=par))
                    exp_done = p.c("sc")
                    if prev1 is not None:
                        q1, e1 = prev1
                        par1 = q1 % 2
                        p.op("vector", "reciprocal", wait=(e1,),
                             inc=(("vv", 1),), small=True,
                             out=stb(C_RS, ofs=par1), in_=stb(C_SSUM, ofs=par1))
                        p.op("scalar", "activation", wait=(("vv", p.c("vv")),),
                             inc=(("sc", 1),),
                             out=SCRP[par1][:, 0:(q1 + 1) * 128],
                             in_=SCRP[par1][:, 0:(q1 + 1) * 128],
                             func=AF.Copy, scale=stb(C_RS, ofs=par1))
                        prev2 = (q1, ("sc", p.c("sc")))
                    prev1 = (qt, ("sc", exp_done))
                q1, e1 = prev1
                par1 = q1 % 2
                p.op("vector", "reciprocal", wait=(e1,), inc=(("vv", 1),),
                     small=True, out=stb(C_RS, ofs=par1), in_=stb(C_SSUM, ofs=par1))
                p.op("scalar", "activation", wait=(("vv", p.c("vv")),),
                     inc=(("sc", 1),),
                     out=SCRP[par1][:, 0:(q1 + 1) * 128],
                     in_=SCRP[par1][:, 0:(q1 + 1) * 128],
                     func=AF.Copy, scale=stb(C_RS, ofs=par1))
                emit_tail(h, prev2[0], prev2[1])
                emit_tail(h, q1, ("sc", p.c("sc")))
                kb_free[hs] = ("tm", p.c("tm"))
                vb_free = ("tm", p.c("tm"))
            p.barrier()

        if "C" in phases:
            OTB_hi = A_ACT[:, 0:8192].bitcast(BF16)
            OTB_lo = A_ACT[:, 8192:16384].bitcast(BF16)
            FTB = A_ACT[:, 16384:32768]
            OUTB = A_Z
            outb_frees = ()
            for tc in range(NCA):
                t0 = tc * TA
                otfb_free = ("tm", p.c("tm"))
                otft_toks = []
                for ci in range(NCI):
                    otft_toks.append(p.dma("sync",
                          OTB_hi[:, ci * TA:(ci + 1) * TA],
                          ot_hi_d[ci * 128:(ci + 1) * 128, t0:t0 + TA],
                          wait=(otfb_free,)))
                    otft_toks.append(p.dma("sync",
                          OTB_lo[:, ci * TA:(ci + 1) * TA],
                          ot_lo_d[ci * 128:(ci + 1) * 128, t0:t0 + TA],
                          wait=(otfb_free,)))
                    otft_toks.append(p.dma("sync", FTB[:, ci * TA:(ci + 1) * TA],
                          ft_d[ci * 128:(ci + 1) * 128, t0:t0 + TA],
                          wait=(otfb_free,)))
                seen = {}
                for s_, v_ in otft_toks:
                    seen[s_] = max(v_, seen.get(s_, 0))
                otft_ld = tuple(seen.items())
                for th in range(2):
                    out_toks = []
                    for co in range(NCI):
                        whi_t, wlo_t = wsp["wo"]
                        si, wo1, wo2 = w_load_pair(whi_t[co, :, :], wlo_t[co, :, :],
                                                   extra_wait=prol_done["wo"]
                                                   if tc == 0 and th == 0 and co == 0
                                                   else ())
                        b = 2 + co % 2
                        fw = (wo1, wo2) + otft_ld + ((bankv[b][0], bankv[b][1]),)
                        mm_o = mm3(PB[b][:], si,
                                   lambda ci, th=th: OTB_hi[:, ci * TA + th * 512:
                                                            ci * TA + th * 512 + 512],
                                   lambda ci, th=th: OTB_lo[:, ci * TA + th * 512:
                                                            ci * TA + th * 512 + 512],
                                   NCI, fw)
                        w_free[si] = ("tm", mm_o)
                        si2 = stg_i[0] % NSTG
                        stg_i[0] += 1
                        p.op("vector", "tensor_tensor",
                             wait=(("tm", mm_o),) + tuple(stg_free[si2]),
                             inc=(("vv", 1),),
                             out=stg_slots[si2], in0=PB[b][:],
                             in1=FTB[:, co * TA + th * 512: co * TA + th * 512 + 512],
                             op=OP.add)
                        st_done = p.c("vv")
                        bankv[b] = ("vv", st_done)
                        tb = 4 + co % 2
                        for j in range(4):
                            w = ()
                            if j == 0:
                                w = (("vv", st_done), (bankv[tb][0], bankv[tb][1]))
                                if co == 0:
                                    w = w + outb_frees
                            p.op("tensor", "transpose", wait=w,
                                 inc=((("tm", 1),) if j == 3 else ()),
                                 out=PB[tb][:, j * 128:(j + 1) * 128],
                                 in_=stg_slots[si2][:, j * 128:(j + 1) * 128],
                                 identity=sb_id[:])
                        tr_done = p.c("tm")
                        stg_free[si2] = (("tm", tr_done),)
                        p.op("vector", "tensor_copy",
                             wait=(("tm", tr_done),), inc=(("vv", 1),),
                             out=bass.AP(A_Z, co * 128,
                                         [[8192, 128], [2048, 4], [1, 128]]),
                             in_=PB[tb][:, 0:512])
                        bankv[tb] = ("vv", p.c("vv"))
                    outb_ready = p.c("vv")
                    for s4 in range(4):
                        s = th * 4 + s4
                        sl = s % 2
                        xa = xa_slots[sl]
                        wx = p.dma("sync", xa, x_d[t0 + s * 128: t0 + s * 128 + 128, :],
                              wait=((xa_free[sl][0], xa_free[sl][1]),))
                        p.op("vector", "tensor_tensor",
                             wait=(wx, ("vv", outb_ready),), inc=(("vv", 1),),
                             out=OUTB[:, s4 * 2048:(s4 + 1) * 2048],
                             in0=OUTB[:, s4 * 2048:(s4 + 1) * 2048], in1=xa[:],
                             op=OP.add)
                        xa_free[sl] = ("vv", p.c("vv"))
                        out_toks.append(
                            p.dma("gpsimd", out_d[t0 + s * 128: t0 + s * 128 + 128, :],
                                  OUTB[:, s4 * 2048:(s4 + 1) * 2048],
                                  wait=(("vv", p.c("vv")),)))
                    seen = {}
                    for s_, v_ in out_toks:
                        seen[s_] = max(v_, seen.get(s_, 0))
                    outb_frees = tuple(seen.items())

        final = tuple((s, p.c(s)) for s in p.POOL["gpsimd"] + p.POOL["sync"])
        p.op("gpsimd", "engine_nop", wait=final)
        p.op("sync", "nop", wait=final)

        @block.sync
        def _(sync):
            p.replay("sync", sync, sems)

        @block.gpsimd
        def _(gpsimd):
            p.replay("gpsimd", gpsimd, sems)

        @block.vector
        def _(vector):
            p.replay("vector", vector, sems)

        @block.scalar
        def _(scalar):
            p.replay("scalar", scalar, sems)

        @block.tensor
        def _(tensor):
            p.replay("tensor", tensor, sems)

    return nc


def _host_inputs(xb, wq, wk, wv, wo, su_ug, su_dp, wscale_ug, wscale_dp,
                 wup, wgt, wdn):
    def blk_in_out(w):
        i_, o_ = w.shape
        nci, nco = i_ // 128, o_ // 128
        return np.ascontiguousarray(
            w.reshape(nci, 128, nco, 128).transpose(2, 1, 0, 3).reshape(nco, 128, i_))

    def blk_out_in(w):
        return blk_in_out(np.ascontiguousarray(w.T))

    q_global = np.arange(S).reshape(NQT, 128)
    qb = q_global // BLK
    n_idx = np.arange(NBLK)
    gm = np.zeros((NQT, 128, NBLK), np.float32)
    gm[qb[:, :, None] == n_idx[None, None, :]] = BIG
    gm[qb[:, :, None] < n_idx[None, None, :]] = -BIG
    gmask = np.ascontiguousarray(gm.transpose(1, 0, 2).reshape(128, NQT * NBLK))

    i = np.arange(128)
    causal = np.where(i[:, None] >= i[None, :], 0.0, NEG).astype(np.float32)

    hm = np.ones((1, 1), np.float32)
    while hm.shape[0] < 128:
        hm = np.block([[hm, hm], [hm, -hm]])

    def fm(v):
        return np.ascontiguousarray(np.asarray(v, np.float32).reshape(-1, 128).T)

    return {
        "x": np.ascontiguousarray(xb),
        "wq": blk_in_out(wq), "wk": blk_in_out(wk), "wv": blk_in_out(wv),
        "wo": blk_in_out(wo),
        "wup": blk_out_in(wup), "wgt": blk_out_in(wgt), "wdn": blk_out_in(wdn),
        "su_ug": fm(su_ug), "su_dp": fm(su_dp),
        "wsu": fm(wscale_ug[:IN]), "wsg": fm(wscale_ug[IN:]),
        "wdp": fm(wscale_dp),
        "h128": np.ascontiguousarray(hm.astype(np.float32)),
        "ident": np.eye(128, dtype=np.float32),
        "causal": causal,
        "gmask": gmask,
    }


def kernel(L0_attn_hidden_states, L0_attn_Wq, L0_attn_Wk, L0_attn_Wv, L0_attn_Wo,
           L0_ffn_SU_ug, L0_ffn_SU_dp, L0_ffn_Wscale_ug, L0_ffn_Wscale_dp,
           L0_ffn_W_up, L0_ffn_W_gate, L0_ffn_W_down):
    x = np.asarray(L0_attn_hidden_states, np.float32)
    args = [np.asarray(a, np.float32) for a in
            (L0_attn_Wq, L0_attn_Wk, L0_attn_Wv, L0_attn_Wo, L0_ffn_SU_ug,
             L0_ffn_SU_dp, L0_ffn_Wscale_ug, L0_ffn_Wscale_dp, L0_ffn_W_up,
             L0_ffn_W_gate, L0_ffn_W_down)]
    nc = build_nc(dbg=False)
    in_maps = [_host_inputs(x[b], *args) for b in range(8)]
    res = run_bass_kernel_spmd(nc, in_maps, core_ids=list(range(8)))
    return np.stack([res.results[b]["out"] for b in range(8)], axis=0)


# version 12
# speedup vs baseline: 1.0010x; 1.0010x over previous
import numpy as np
import concourse.bass as bass
import concourse.mybir as mybir
from concourse.bass_utils import run_bass_kernel_spmd

F32 = mybir.dt.float32
BF16 = mybir.dt.bfloat16
AF = mybir.ActivationFunctionType
OP = mybir.AluOpType
AX = mybir.AxisListType

S = 4096
DM = 2048
NH = 16
HD = 128
BLK = 64
NBLK = S // BLK
IN = 8192
NCI = DM // 128
NCO_I = IN // 128
TA = 1024
TF = 512
NCA = S // TA
NCF = S // TF
NQT = S // 128
NQT_B = NQT

SM_SCALE = float(HD) ** -0.5
S1 = 1.0 / (np.sqrt(2048.0) * 64.0)
S2 = 1.0 / (np.sqrt(8192.0) * 64.0)
EPS = 1e-5
NEG = -1.0e9
BIG = 1.0e30

ENGS = ("sync", "gpsimd", "vector", "scalar", "tensor")


OWN_SEM = {"vector": "vv", "scalar": "sc"}


class Prog:
    def __init__(self):
        self.ops = {e: [] for e in ENGS}
        self.cnt = {}

    def op(self, eng, fname, *a, wait=(), inc=(), small=False, **k):
        own = OWN_SEM.get(eng)
        ws = list(wait)
        if own is not None and fname not in ("nop", "engine_nop", "drain", "dma_start"):
            ws.append((own, self.cnt.get(own, 0)))
            if not any(s == own for s, _ in inc):
                inc = tuple(inc) + ((own, 1),)
        incs = []
        for sem, d in inc:
            self.cnt[sem] = self.cnt.get(sem, 0) + d
            incs.append((sem, d))
        ws = [(s, v) for (s, v) in ws if v > 0]
        if small:
            self.ops[eng].append((fname, a, k, ws, []))
            self.ops[eng].append(("drain", (), {}, [], incs))
        else:
            self.ops[eng].append((fname, a, k, ws, incs))
        return (eng, len(self.ops[eng]) - 1)

    def add_wait(self, handle, sem, val):
        eng, i = handle
        fname, a, k, ws, incs = self.ops[eng][i]
        if val > 0:
            self.ops[eng][i] = (fname, a, k, ws + [(sem, val)], incs)

    POOL = {"sync": ["wS%d" % i for i in range(16)],
            "gpsimd": ["wG%d" % i for i in range(8)],
            "scalar": ["wA%d" % i for i in range(8)]}

    def dma(self, eng, out, in_, sem=None, wait=()):
        pool = self.POOL[eng]
        depth = len(pool) // 2
        hist = self.cnt.setdefault("_hist_" + eng, [])
        i = len(hist)
        s = pool[i % len(pool)]
        w = tuple(wait)
        if i >= depth:
            w = w + (hist[i - depth],)
        self.op(eng, "dma_start", wait=w, inc=((s, 16),), out=out, in_=in_)
        tok = (s, self.cnt[s])
        hist.append(tok)
        return tok

    def c(self, sem):
        return self.cnt.get(sem, 0)

    def all_sems(self):
        return (["tm", "vv", "sc"] + self.POOL["sync"] + self.POOL["gpsimd"]
                + self.POOL["scalar"])

    def barrier(self):
        snap = [(s, self.c(s)) for s in self.all_sems()]
        for e in ENGS:
            nopname = {"sync": "nop", "gpsimd": "engine_nop", "vector": "engine_nop",
                       "scalar": "nop", "tensor": "nop"}[e]
            self.op(e, nopname, wait=snap)

    def check_waits(self):
        bad = []
        for e in ENGS:
            for i, (fn, a, k, ws, incs) in enumerate(self.ops[e]):
                for s, v in ws:
                    if v > self.cnt.get(s, 0):
                        bad.append((e, i, fn, s, v, self.cnt.get(s, 0)))
        return bad

    def replay(self, engname, engobj, sems):
        for fname, a, k, waits, incs in self.ops[engname]:
            for s, v in waits:
                engobj.wait_ge(sems[s], v)
            instr = getattr(engobj, fname)(*a, **k)
            for s, d in incs:
                instr.then_inc(sems[s], d)


def build_nc(dbg=False, phases="AFBC"):
    nc = bass.Bass("TRN2", target_bir_lowering=False)

    x_d = nc.dram_tensor("x", [S, DM], F32, kind="ExternalInput").ap()
    wq_d = nc.dram_tensor("wq", [NCI, 128, DM], F32, kind="ExternalInput").ap()
    wk_d = nc.dram_tensor("wk", [NCI, 128, DM], F32, kind="ExternalInput").ap()
    wv_d = nc.dram_tensor("wv", [NCI, 128, DM], F32, kind="ExternalInput").ap()
    wo_d = nc.dram_tensor("wo", [NCI, 128, DM], F32, kind="ExternalInput").ap()
    wup_d = nc.dram_tensor("wup", [NCO_I, 128, DM], F32, kind="ExternalInput").ap()
    wgt_d = nc.dram_tensor("wgt", [NCO_I, 128, DM], F32, kind="ExternalInput").ap()
    wdn_d = nc.dram_tensor("wdn", [NCI, 128, IN], F32, kind="ExternalInput").ap()
    su_ug_d = nc.dram_tensor("su_ug", [128, NCI], F32, kind="ExternalInput").ap()
    su_dp_d = nc.dram_tensor("su_dp", [128, NCO_I], F32, kind="ExternalInput").ap()
    wsu_d = nc.dram_tensor("wsu", [128, NCO_I], F32, kind="ExternalInput").ap()
    wsg_d = nc.dram_tensor("wsg", [128, NCO_I], F32, kind="ExternalInput").ap()
    wdp_d = nc.dram_tensor("wdp", [128, NCI], F32, kind="ExternalInput").ap()
    h128_d = nc.dram_tensor("h128", [128, 128], F32, kind="ExternalInput").ap()
    ident_d = nc.dram_tensor("ident", [128, 128], F32, kind="ExternalInput").ap()
    causal_d = nc.dram_tensor("causal", [128, 128], F32, kind="ExternalInput").ap()
    gmask_d = nc.dram_tensor("gmask", [128, NQT * NBLK], F32, kind="ExternalInput").ap()

    out_d = nc.dram_tensor("out", [S, DM], F32, kind="ExternalOutput").ap()
    skind = "ExternalOutput" if dbg else "Internal"
    qt_d = nc.dram_tensor("qt_s", [DM, S], F32, kind=skind).ap()
    kt_d = nc.dram_tensor("kt_s", [DM, S], F32, kind=skind).ap()
    vt_d = nc.dram_tensor("vt_s", [DM, S], F32, kind=skind).ap()
    ot_d = nc.dram_tensor("ot_s", [DM, S], F32, kind=skind).ap()
    ft_d = nc.dram_tensor("ft_s", [DM, S], F32, kind=skind).ap()
    kg_d = nc.dram_tensor("kg_s", [128, NCI * NBLK], F32, kind=skind).ap()
    wsp = {}
    for nm, nco_, wid_ in (("wq", NCI, DM), ("wk", NCI, DM), ("wv", NCI, DM),
                           ("wo", NCI, DM), ("wup", NCO_I, DM), ("wgt", NCO_I, DM),
                           ("wdn", NCI, IN)):
        wsp[nm] = (nc.dram_tensor(nm + "_hi", [nco_, 128, wid_], BF16, kind="Internal").ap(),
                   nc.dram_tensor(nm + "_lo", [nco_, 128, wid_], BF16, kind="Internal").ap())
    ot_hi_d = nc.dram_tensor("ot_hi", [DM, S], BF16, kind="Internal").ap()
    ot_lo_d = nc.dram_tensor("ot_lo", [DM, S], BF16, kind="Internal").ap()
    qhi_d = nc.dram_tensor("q_hi", [DM, S], BF16, kind="Internal").ap()
    qlo_d = nc.dram_tensor("q_lo", [DM, S], BF16, kind="Internal").ap()
    khi_d = nc.dram_tensor("k_hi", [DM, S], BF16, kind="Internal").ap()
    klo_d = nc.dram_tensor("k_lo", [DM, S], BF16, kind="Internal").ap()
    xn_d = nc.dram_tensor("xn_s", [128, DM], F32, kind=skind).ap()
    st_d = nc.dram_tensor("st_s", [128, 64], F32, kind=skind).ap()
    xnt_d = nc.dram_tensor("xnt_s", [128, TA], F32, kind=skind).ap()

    p = Prog()

    import contextlib
    with contextlib.ExitStack() as stack:
        en = stack.enter_context
        sb_id = en(nc.sbuf_tensor("sb_ident", [128, 128], F32))
        sb_h = en(nc.sbuf_tensor("sb_h128", [128, 128], F32))
        sb_ca = en(nc.sbuf_tensor("sb_causal", [128, 128], F32))
        sb_suug = en(nc.sbuf_tensor("sb_su_ug", [128, NCI], F32))
        sb_sudp = en(nc.sbuf_tensor("sb_su_dp", [128, NCO_I], F32))
        sb_wsu = en(nc.sbuf_tensor("wsu64su", [128, NCO_I], F32))
        sb_wsg = en(nc.sbuf_tensor("wsg64", [128, NCO_I], F32))
        sb_wdp = en(nc.sbuf_tensor("wdp64", [128, NCI], F32))
        sb_kg = en(nc.sbuf_tensor("kgT", [128, NCI * NBLK], F32))
        sb_xng = en(nc.sbuf_tensor("xng", [128, 768], F32))
        sb_st = en(nc.sbuf_tensor("stats", [128, 64], F32))
        A_ACT = en(nc.sbuf_tensor("ACT", [128, 32768], F32))
        A_W = en(nc.sbuf_tensor("WST", [128, 4096], F32))
        A_Z = en(nc.sbuf_tensor("ZB", [128, 8192], F32))
        A_X = en(nc.sbuf_tensor("XB", [128, 4096], F32))
        A_S = en(nc.sbuf_tensor("STG", [128, 1536], F32))
        PB = [en(nc.psum_tensor(f"pb{i}", [128, 512], F32)) for i in range(6)]
        TRASH = en(nc.psum_tensor("trash", [128, 1024], F32))
        GATE_PS = TRASH[:, 0:512]
        sems = {s: en(nc.semaphore(s)) for s in p.all_sems()}
        block = en(nc.Block())
        bankv = {i: ("vv", 0) for i in range(8)}

        def stc(o, c, n=1):
            return sb_st[:, o * 24 + c: o * 24 + c + n]
        C_SUM, C_MEAN, C_VEPS, C_STD, C_RSTD = 0, 1, 2, 3, 4
        C_SQ0 = 8
        def stb(c, n=1, ofs=0):
            return sb_st[:, 48 + c + ofs: 48 + c + ofs + n]
        C_NMAX, C_SSUM, C_RS, C_T8 = 0, 2, 4, 8

        pre_toks = [
            p.dma("sync", sb_id[:], ident_d[:, :]),
            p.dma("sync", sb_h[:], h128_d[:, :]),
            p.dma("sync", sb_ca[:], causal_d[:, :]),
            p.dma("sync", sb_suug[:], su_ug_d[:, :]),
            p.dma("sync", sb_sudp[:], su_dp_d[:, :]),
            p.dma("sync", sb_wsu[:], wsu_d[:, :]),
            p.dma("sync", sb_wsg[:], wsg_d[:, :]),
            p.dma("sync", sb_wdp[:], wdp_d[:, :]),
        ]
        p.op("vector", "memset", ap=sb_st[:], constant=0.0, inc=(("vv", 1),))
        stats_init_vv = p.c("vv")
        p.op("vector", "tensor_scalar", wait=tuple(pre_toks), inc=(("vv", 1),),
             out=sb_wsu[:], in0=sb_wsu[:], scalar1=64.0, scalar2=None, op0=OP.mult)
        p.op("vector", "tensor_tensor", inc=(("vv", 1),),
             out=sb_wsu[:], in0=sb_wsu[:], in1=sb_sudp[:], op=OP.mult)
        p.op("vector", "tensor_scalar", inc=(("vv", 1),),
             out=sb_wsg[:], in0=sb_wsg[:], scalar1=64.0, scalar2=None, op0=OP.mult)
        p.op("vector", "tensor_scalar", inc=(("vv", 1),), small=True,
             out=sb_wdp[:], in0=sb_wdp[:], scalar1=64.0, scalar2=None, op0=OP.mult)

        ln_prev_vv = [stats_init_vv, stats_init_vv]

        def layernorm(xa, wx, par):
            p.op("scalar", "activation",
                 wait=(wx, ("vv", ln_prev_vv[par]),),
                 inc=(("sc", 1),),
                 out=TRASH[:, :], in_=xa[:, 0:1024], func=AF.Square,
                 accum_out=stc(par, C_SQ0))
            p.op("scalar", "activation", inc=(("sc", 1),),
                 out=TRASH[:, :], in_=xa[:, 1024:2048], func=AF.Square,
                 accum_out=stc(par, C_SQ0 + 1))
            p.op("scalar", "activation", inc=(("sc", 1),),
                 out=TRASH[:, :], in_=xa[:, 0:1024], func=AF.Copy,
                 accum_out=stc(par, C_SUM))
            p.op("scalar", "activation", inc=(("sc", 1),), small=True,
                 out=TRASH[:, :], in_=xa[:, 1024:2048], func=AF.Copy,
                 accum_out=stc(par, C_SQ0 + 2))
            sq_done = p.c("sc")
            p.op("vector", "tensor_tensor", wait=(("sc", sq_done),),
                 inc=(("vv", 1),), small=True,
                 out=stc(par, C_SUM), in0=stc(par, C_SUM),
                 in1=stc(par, C_SQ0 + 2), op=OP.add)
            p.op("vector", "tensor_tensor", inc=(("vv", 1),), small=True,
                 out=stc(par, C_SQ0), in0=stc(par, C_SQ0),
                 in1=stc(par, C_SQ0 + 1), op=OP.add)
            p.op("vector", "tensor_scalar", inc=(("vv", 1),), small=True,
                 out=stc(par, C_MEAN), in0=stc(par, C_SUM), scalar1=1.0 / DM,
                 scalar2=None, op0=OP.mult)
            p.op("vector", "tensor_scalar", inc=(("vv", 1),), small=True,
                 out=stc(par, C_SQ0), in0=stc(par, C_SQ0), scalar1=1.0 / DM,
                 scalar2=EPS, op0=OP.mult, op1=OP.add)
            p.op("vector", "tensor_scalar", inc=(("vv", 1),), small=True,
                 out=stc(par, C_VEPS), in0=stc(par, C_MEAN),
                 scalar1=stc(par, C_MEAN), scalar2=-1.0, op0=OP.mult, op1=OP.mult)
            p.op("vector", "tensor_tensor", inc=(("vv", 1),), small=True,
                 out=stc(par, C_VEPS), in0=stc(par, C_SQ0), in1=stc(par, C_VEPS),
                 op=OP.add)
            veps_done = p.c("vv")
            p.op("scalar", "activation", wait=(("vv", veps_done),), inc=(("sc", 1),),
                 small=True, out=stc(par, C_STD), in_=stc(par, C_VEPS), func=AF.Sqrt)
            p.op("vector", "reciprocal", wait=(("sc", p.c("sc")),), inc=(("vv", 1),),
                 small=True, out=stc(par, C_RSTD), in_=stc(par, C_STD))
            p.op("vector", "tensor_scalar", inc=(("vv", 1),),
                 out=xa[:], in0=xa[:], scalar1=stc(par, C_MEAN),
                 scalar2=stc(par, C_RSTD), op0=OP.subtract, op1=OP.mult)
            ln_prev_vv[par] = p.c("vv")
            return p.c("vv")


        def transpose_tiles(srcs, dests, ln_done, banks, sus=None, scale=None,
                            group_dests=None, group_dests_lo=None, extra_wait=(),
                            ready=None):
            n = len(srcs)
            for g in range((n + 3) // 4):
                b = banks[g % len(banks)]
                n_in_g = min(4, n - g * 4)
                for j in range(n_in_g):
                    i = g * 4 + j
                    w = ()
                    if j == 0:
                        r = ready if ready is not None else ("vv", ln_done)
                        w = (r, (bankv[b][0], bankv[b][1])) + tuple(extra_wait)
                    p.op("tensor", "transpose", wait=w,
                         inc=((("tm", 1),) if j == n_in_g - 1 else ()),
                         out=PB[b][:, j * 128:(j + 1) * 128], in_=srcs[i],
                         identity=sb_id[:])
                tmc = p.c("tm")
                if group_dests is not None:
                    p.op("vector", "tensor_copy", wait=(("tm", tmc),),
                         inc=(("vv", 1),),
                         out=group_dests[g], in_=PB[b][:, 0:n_in_g * 128])
                    if group_dests_lo is not None:
                        gsh = group_dests[g].shape
                        p.op("vector", "tensor_tensor", inc=(("vv", 1),),
                             out=group_dests_lo[g],
                             in0=PB[b][:, 0:n_in_g * 128].rearrange(
                                 "p (a b) -> p a b", b=gsh[-1]),
                             in1=group_dests[g], op=OP.subtract)
                else:
                    for j in range(n_in_g):
                        i = g * 4 + j
                        if sus is None and scale is None:
                            p.op("vector", "tensor_copy",
                                 wait=(("tm", tmc),) if j == 0 else (), inc=(("vv", 1),),
                                 out=dests[i], in_=PB[b][:, j * 128:(j + 1) * 128])
                        else:
                            s1 = sus[i] if sus is not None else 1.0
                            p.op("vector", "tensor_scalar",
                                 wait=(("tm", tmc),) if j == 0 else (), inc=(("vv", 1),),
                                 out=dests[i], in0=PB[b][:, j * 128:(j + 1) * 128],
                                 scalar1=s1, scalar2=scale,
                                 op0=OP.mult, op1=(OP.bypass if scale is None else OP.mult))
                bankv[b] = ("vv", p.c("vv"))
            return p.c("vv")

        NSTG = 3
        stg_slots = [A_S[:, i * 512:(i + 1) * 512] for i in range(NSTG)]
        stg_free = [()] * NSTG
        stg_i = [0]

        def stage_out(bank, cols, dram_slice, mm_wait, scale_col=None):
            si = stg_i[0] % NSTG
            stg_i[0] += 1
            slot = stg_slots[si][:, 0:cols]
            if scale_col is None:
                p.op("vector", "tensor_copy",
                     wait=(("tm", mm_wait),) + tuple(stg_free[si]),
                     inc=(("vv", 1),), out=slot, in_=PB[bank][:, 0:cols])
            else:
                p.op("vector", "tensor_scalar",
                     wait=(("tm", mm_wait),) + tuple(stg_free[si]),
                     inc=(("vv", 1),), out=slot, in0=PB[bank][:, 0:cols],
                     scalar1=scale_col, scalar2=None, op0=OP.mult)
            ev = p.c("vv")
            tok = p.dma("gpsimd", dram_slice, slot, wait=(("vv", ev),))
            stg_free[si] = (tok,)
            bankv[bank] = ("vv", ev)
            return ev

        def stage_out_split(bank, hi_slice, lo_slice, mm_wait):
            si = stg_i[0] % NSTG
            stg_i[0] += 1
            s_hi = stg_slots[si][:, 0:256].bitcast(BF16)
            s_lo = stg_slots[si][:, 256:512].bitcast(BF16)
            p.op("vector", "tensor_copy",
                 wait=(("tm", mm_wait),) + tuple(stg_free[si]),
                 inc=(("vv", 1),), out=s_hi, in_=PB[bank][:])
            p.op("vector", "tensor_tensor", inc=(("vv", 1),),
                 out=s_lo, in0=PB[bank][:], in1=s_hi, op=OP.subtract)
            ev = p.c("vv")
            t1 = p.dma("gpsimd", hi_slice, s_hi, wait=(("vv", ev),))
            t2 = p.dma("gpsimd", lo_slice, s_lo, wait=(("vv", ev),))
            stg_free[si] = (t1, t2)
            bankv[bank] = ("vv", ev)
            return ev

        w_slots = [A_W[:, i * 2048:(i + 1) * 2048] for i in range(2)]
        w_hi = [A_W[:, i * 2048: i * 2048 + 1024].bitcast(BF16) for i in range(2)]
        w_lo = [A_W[:, i * 2048 + 1024:(i + 1) * 2048].bitcast(BF16) for i in range(2)]
        w_free = [("tm", 0)] * 2
        w_i = [0]

        def w_load(dram_slice, extra_wait=()):
            si = w_i[0] % 2
            w_i[0] += 1
            tok = p.dma("sync", w_slots[si], dram_slice,
                        wait=((w_free[si][0], w_free[si][1]),) + tuple(extra_wait))
            return si, tok

        def w_load_pair(whi_slice, wlo_slice, extra_wait=()):
            si = w_i[0] % 2
            w_i[0] += 1
            t1 = p.dma("sync", w_hi[si], whi_slice,
                       wait=((w_free[si][0], w_free[si][1]),) + tuple(extra_wait))
            t2 = p.dma("sync", w_lo[si], wlo_slice)
            return si, t1, t2

        def mm3(bank_ap, si, rhs_hi, rhs_lo, nci, first_wait, width=128):
            first = True
            for wsel, xsel in ((w_hi[si], rhs_hi), (w_hi[si], rhs_lo),
                               (w_lo[si], rhs_hi)):
                for ci in range(nci):
                    w = first_wait if first else ()
                    last = (wsel is w_lo[si]) and ci == nci - 1
                    p.op("tensor", "matmul", wait=w,
                         inc=((("tm", 1),) if last else ()),
                         out=bank_ap,
                         lhsT=wsel[:, ci * width:(ci + 1) * width],
                         rhs=xsel(ci),
                         start=first, stop=last)
                    first = False
            return p.c("tm")

        xa_slots = [A_X[:, 0:2048], A_X[:, 2048:4096]]
        xa_free = [("vv", 0), ("vv", 0)]

        land = [A_Z[:, i * 2048:(i + 1) * 2048] for i in range(2)]
        phi = [A_Z[:, 4096 + i * 2048: 4096 + i * 2048 + 1024].bitcast(BF16)
               for i in range(2)]
        plo = [A_Z[:, 4096 + i * 2048 + 1024: 4096 + (i + 1) * 2048].bitcast(BF16)
               for i in range(2)]
        land_free = [("vv", 0), ("vv", 0)]
        pair_free = [[], []]
        ji = [0]
        prol_done = {}

        def split_chunk(fp32_slice, hi_slice, lo_slice):
            sl = ji[0] % 2
            ji[0] += 1
            t_in = p.dma("scalar", land[sl], fp32_slice,
                         wait=((land_free[sl][0], land_free[sl][1]),))
            p.op("vector", "tensor_copy", wait=(t_in,) + tuple(pair_free[sl]),
                 inc=(("vv", 1),), out=phi[sl], in_=land[sl])
            p.op("vector", "tensor_tensor", inc=(("vv", 1),),
                 out=plo[sl], in0=land[sl], in1=phi[sl], op=OP.subtract)
            land_free[sl] = ("vv", p.c("vv"))
            ev = p.c("vv")
            t1 = p.dma("scalar", hi_slice, phi[sl], wait=(("vv", ev),))
            t2 = p.dma("scalar", lo_slice, plo[sl], wait=(("vv", ev),))
            pair_free[sl] = [t1, t2]
            return (t1, t2)

        for nm, dram, nco_, wid_ in (("wq", wq_d, NCI, DM), ("wk", wk_d, NCI, DM),
                                     ("wv", wv_d, NCI, DM), ("wo", wo_d, NCI, DM),
                                     ("wup", wup_d, NCO_I, DM),
                                     ("wgt", wgt_d, NCO_I, DM),
                                     ("wdn", wdn_d, NCI, IN)):
            whi_t, wlo_t = wsp[nm]
            toks = None
            for co in range(nco_):
                for sub in range(wid_ // 2048):
                    sl_ = slice(sub * 2048, (sub + 1) * 2048)
                    toks = split_chunk(dram[co, :, sl_], whi_t[co, :, sl_],
                                       wlo_t[co, :, sl_])
            prol_done[nm] = tuple((s_, p.c(s_)) for s_ in
                                  {toks[0][0], toks[1][0]})
        prol_all = tuple((s_, p.c(s_)) for s_ in p.POOL["scalar"])

        if "A" in phases:
            xnT = A_ACT
            xnT_hi = A_ACT[:, 0:8192].bitcast(BF16)
            xnT_lo = A_ACT[:, 8192:16384].bitcast(BF16)
            xnT_free = ("tm", 0)
            for tc in range(NCA):
                t0 = tc * TA
                for s in range(8):
                    sl = s % 2
                    xa = xa_slots[sl]
                    wx = p.dma("sync", xa, x_d[t0 + s * 128: t0 + s * 128 + 128, :],
                               wait=((xa_free[sl][0], xa_free[sl][1]),))
                    ln_done = layernorm(xa, wx, s % 2)
                    srcs = [xa[:, ec * 128:(ec + 1) * 128] for ec in range(NCI)]
                    ghi = [xnT_hi.rearrange("p (c t) -> p c t", t=TA)
                           [:, g * 4:g * 4 + 4, s * 128:s * 128 + 128]
                           for g in range(4)]
                    glo = [xnT_lo.rearrange("p (c t) -> p c t", t=TA)
                           [:, g * 4:g * 4 + 4, s * 128:s * 128 + 128]
                           for g in range(4)]
                    if dbg and tc == 0 and s == 0:
                        p.dma("gpsimd", xn_d[:, :], xa[:], wait=(("vv", ln_done),))
                        p.dma("gpsimd", st_d[:, :], sb_st[:, 0:64], wait=(("vv", ln_done),))
                        p.barrier()
                    td = transpose_tiles(srcs, None, ln_done, (0, 1),
                                         group_dests=ghi, group_dests_lo=glo,
                                         extra_wait=((xnT_free[0], xnT_free[1]),)
                                         if s == 0 else ())
                    xa_free[sl] = ("vv", td)
                if dbg and tc == 0:
                    p.dma("gpsimd", xnt_d[:, :], xnT[:, 0:TA],
                          wait=(("vv", p.c("vv")),))
                xnT_done = p.c("vv")
                for ci in range(NCI):
                    p.op("vector", "reduce_sum", wait=(("vv", xnT_done),),
                         inc=(("vv", 1),),
                         out=sb_xng[:, ci * 16:(ci + 1) * 16].unsqueeze(2),
                         in_=xnT_hi[:, ci * TA:(ci + 1) * TA].rearrange(
                             "p (b t) -> p b t", t=BLK),
                         axis=AX.X)
                    p.op("vector", "reduce_sum",
                         inc=(("vv", 1),), small=(ci == NCI - 1),
                         out=sb_xng[:, 256 + ci * 16: 256 + ci * 16 + 16].unsqueeze(2),
                         in_=xnT_lo[:, ci * TA:(ci + 1) * TA].rearrange(
                             "p (b t) -> p b t", t=BLK),
                         axis=AX.X)
                p.op("vector", "tensor_tensor", inc=(("vv", 1),),
                     out=sb_xng[:, 0:256], in0=sb_xng[:, 0:256],
                     in1=sb_xng[:, 256:512], op=OP.add)
                p.op("vector", "tensor_scalar", inc=(("vv", 1),),
                     out=sb_xng[:, 0:256], in0=sb_xng[:, 0:256],
                     scalar1=1.0 / BLK, scalar2=None, op0=OP.mult)
                xng_hi = sb_xng[:, 512:640].bitcast(BF16)
                xng_lo = sb_xng[:, 640:768].bitcast(BF16)
                p.op("vector", "tensor_copy", inc=(("vv", 1),),
                     out=xng_hi, in_=sb_xng[:, 0:256])
                p.op("vector", "tensor_tensor", inc=(("vv", 1),), small=True,
                     out=xng_lo, in0=sb_xng[:, 0:256], in1=xng_hi, op=OP.subtract)
                xng_done = p.c("vv")

                for wnm, spill, iskg in (("wq", (qhi_d, qlo_d), False),
                                         ("wk", (khi_d, klo_d), True),
                                         ("wv", vt_d, False)):
                    whi_t, wlo_t = wsp[wnm]
                    for co in range(NCI):
                        si, wd1, wd2 = w_load_pair(whi_t[co, :, :], wlo_t[co, :, :],
                                                   extra_wait=prol_done[wnm]
                                                   if tc == 0 and co == 0 else ())
                        for th in range(2):
                            b = 2 + th
                            fw = (wd1, wd2, ("vv", xnT_done),
                                  (bankv[b][0], bankv[b][1]))
                            mm3(PB[b][:], si,
                                lambda ci, th=th: xnT_hi[:, ci * TA + th * 512:
                                                         ci * TA + th * 512 + 512],
                                lambda ci, th=th: xnT_lo[:, ci * TA + th * 512:
                                                         ci * TA + th * 512 + 512],
                                NCI, fw)
                            if wnm == "wv":
                                stage_out(b, 512,
                                          spill[co * 128:(co + 1) * 128,
                                                t0 + th * 512: t0 + th * 512 + 512],
                                          p.c("tm"))
                            else:
                                stage_out_split(
                                    b,
                                    spill[0][co * 128:(co + 1) * 128,
                                             t0 + th * 512: t0 + th * 512 + 512],
                                    spill[1][co * 128:(co + 1) * 128,
                                             t0 + th * 512: t0 + th * 512 + 512],
                                    p.c("tm"))
                        if iskg:
                            b = 4
                            fw = (("vv", xng_done), (bankv[b][0], bankv[b][1]))
                            mm3(PB[b][:, 0:16], si,
                                lambda ci: xng_hi[:, ci * 16:(ci + 1) * 16],
                                lambda ci: xng_lo[:, ci * 16:(ci + 1) * 16],
                                NCI, fw)
                            p.op("vector", "tensor_copy", wait=(("tm", p.c("tm")),),
                                 inc=(("vv", 1),), small=True,
                                 out=sb_kg[:, co * NBLK + tc * 16:
                                           co * NBLK + tc * 16 + 16],
                                 in_=PB[b][:, 0:16])
                            bankv[b] = ("vv", p.c("vv"))
                        w_free[si] = ("tm", p.c("tm"))
                xnT_free = ("tm", p.c("tm"))
            if dbg:
                p.dma("gpsimd", kg_d[:, :], sb_kg[:], wait=(("vv", p.c("vv")),))
            p.barrier()

        if "F" in phases:
            ACT = A_ACT
            Z1 = A_Z
            actb = A_ACT[:, :].bitcast(BF16)
            z1b = A_Z[:, :].bitcast(BF16)
            z1hi = lambda ci: z1b[:, ci * 1024: ci * 1024 + 512]
            z1lo = lambda ci: z1b[:, ci * 1024 + 512: ci * 1024 + 1024]
            act_free = ("tm", 0)
            z1_free = ("tm", 0)
            z1_prol = prol_all

            def f_front_pieces(tc):
                t0 = tc * TF

                def mk_ln_tr(s):
                    def go():
                        sl = s % 2
                        xa = xa_slots[sl]
                        wx = p.dma("sync", xa,
                                   x_d[t0 + s * 128: t0 + s * 128 + 128, :],
                                   wait=((xa_free[sl][0], xa_free[sl][1]),))
                        ln_done = layernorm(xa, wx, s % 2)
                        srcs = [xa[:, ec * 128:(ec + 1) * 128] for ec in range(NCI)]
                        dsts = [Z1[:, ec * TF + s * 128: ec * TF + s * 128 + 128]
                                for ec in range(NCI)]
                        sus = [sb_suug[:, ec:ec + 1] for ec in range(NCI)]
                        ew = ((z1_free[0], z1_free[1]),)
                        if tc == 0:
                            ew = ew + z1_prol
                        td = transpose_tiles(srcs, dsts, ln_done, (0, 1), sus=sus,
                                             extra_wait=ew if s == 0 else ())
                        xa_free[sl] = ("vv", td)
                    return go

                def h16():
                    hh = 1
                    while hh < NCI:
                        for base in range(0, NCI, 2 * hh):
                            for j in range(hh):
                                a = (base + j) * TF
                                bcol = (base + j + hh) * TF
                                p.op("vector", "tensor_tensor", inc=(("vv", 1),),
                                     out=Z1[:, a:a + TF], in0=Z1[:, a:a + TF],
                                     in1=Z1[:, bcol:bcol + TF], op=OP.add)
                                p.op("vector", "scalar_tensor_tensor",
                                     inc=(("vv", 1),),
                                     out=Z1[:, bcol:bcol + TF],
                                     in0=Z1[:, bcol:bcol + TF],
                                     scalar=-2.0, in1=Z1[:, a:a + TF],
                                     op0=OP.mult, op1=OP.add)
                        hh *= 2

                def h128_1():
                    bf1_done = p.c("vv")
                    for ci in range(NCI):
                        b = 2 + ci % 2
                        p.op("tensor", "matmul",
                             wait=(("vv", bf1_done), (bankv[b][0], bankv[b][1]),),
                             inc=(("tm", 1),),
                             out=PB[b][:], lhsT=sb_h[:],
                             rhs=Z1[:, ci * TF:(ci + 1) * TF],
                             start=True, stop=True)
                        zh = z1b[:, ci * 1024: ci * 1024 + 512]
                        zl = z1b[:, ci * 1024 + 512: ci * 1024 + 1024]
                        p.op("vector", "tensor_scalar", wait=(("tm", p.c("tm")),),
                             inc=(("vv", 1),),
                             out=zh, in0=PB[b][:], scalar1=S1,
                             scalar2=None, op0=OP.mult)
                        p.op("vector", "scalar_tensor_tensor", inc=(("vv", 1),),
                             out=zl, in0=PB[b][:], scalar=S1, in1=zh,
                             op0=OP.mult, op1=OP.subtract)
                        bankv[b] = ("vv", p.c("vv"))
                return [mk_ln_tr(0), mk_ln_tr(1), mk_ln_tr(2), mk_ln_tr(3),
                        h16, h128_1]

            def f_upgate_h64(tc):
                nonlocal act_free

                def bfly(a, bcol):
                    p.op("vector", "tensor_tensor", inc=(("vv", 1),),
                         out=ACT[:, a * TF:a * TF + TF], in0=ACT[:, a * TF:a * TF + TF],
                         in1=ACT[:, bcol * TF:bcol * TF + TF], op=OP.add)
                    p.op("vector", "scalar_tensor_tensor", inc=(("vv", 1),),
                         out=ACT[:, bcol * TF:bcol * TF + TF],
                         in0=ACT[:, bcol * TF:bcol * TF + TF],
                         scalar=-2.0, in1=ACT[:, a * TF:a * TF + TF],
                         op0=OP.mult, op1=OP.add)

                z1_done = p.c("vv")
                for co in range(NCO_I):
                    first_use = tc == 0 and co == 0
                    whi_t, wlo_t = wsp["wup"]
                    siu, wu1, wu2 = w_load_pair(whi_t[co, :, :], wlo_t[co, :, :],
                                                extra_wait=prol_done["wup"]
                                                if first_use else ())
                    whi_t, wlo_t = wsp["wgt"]
                    sig, wg1, wg2 = w_load_pair(whi_t[co, :, :], wlo_t[co, :, :],
                                                extra_wait=prol_done["wgt"]
                                                if first_use else ())
                    fw = (wu1, wu2, ("vv", z1_done), (bankv[4][0], bankv[4][1]))
                    mm_u = mm3(PB[4][:], siu, z1hi, z1lo, NCI, fw)
                    fw = (wg1, wg2, (bankv[5][0], bankv[5][1]))
                    if co == 0:
                        fw = fw + ((act_free[0], act_free[1]),)
                    mm_g = mm3(PB[5][:], sig, z1hi, z1lo, NCI, fw)
                    w_free[siu] = ("tm", mm_u)
                    w_free[sig] = ("tm", mm_g)
                    si = stg_i[0] % NSTG
                    stg_i[0] += 1
                    p.op("vector", "tensor_scalar",
                         wait=(("tm", mm_u),) + tuple(stg_free[si]),
                         inc=(("vv", 1),),
                         out=stg_slots[si], in0=PB[4][:],
                         scalar1=sb_wsu[:, co:co + 1], scalar2=None, op0=OP.mult)
                    bankv[4] = ("vv", p.c("vv"))
                    p.op("scalar", "activation",
                         wait=(("tm", mm_g),) + (((act_free[0], act_free[1]),)
                                                 if co == 0 else ()),
                         inc=(("sc", 1),),
                         out=ACT[:, co * TF:(co + 1) * TF], in_=PB[5][:],
                         func=AF.Silu, scale=sb_wsg[:, co:co + 1])
                    sg_done = p.c("sc")
                    bankv[5] = ("sc", sg_done)
                    p.op("vector", "tensor_tensor", wait=(("sc", sg_done),),
                         inc=(("vv", 1),),
                         out=ACT[:, co * TF:(co + 1) * TF], in0=stg_slots[si],
                         in1=ACT[:, co * TF:(co + 1) * TF], op=OP.mult)
                    stg_free[si] = (("vv", p.c("vv")),)
                    for s in range(6):
                        blk_sz = 2 << s
                        if (co + 1) % blk_sz == 0:
                            base = co + 1 - blk_sz
                            for j in range(1 << s):
                                bfly(base + j, base + j + (1 << s))
                return p.c("vv")

            def f_h128_2(tc, bf_done):
                qvv = []
                for co in range(NCO_I):
                    b = 2 + co % 2
                    p.op("tensor", "matmul",
                         wait=((("vv", bf_done),) if co == 0 else ())
                         + ((bankv[b][0], bankv[b][1]),),
                         inc=(("tm", 1),),
                         out=PB[b][:], lhsT=sb_h[:],
                         rhs=ACT[:, co * TF:(co + 1) * TF],
                         start=True, stop=True)
                    ahi = actb[:, co * 1024: co * 1024 + 512]
                    alo = actb[:, co * 1024 + 512: co * 1024 + 1024]
                    p.op("vector", "tensor_scalar", wait=(("tm", p.c("tm")),),
                         inc=(("vv", 1),),
                         out=ahi, in0=PB[b][:], scalar1=S2, scalar2=None,
                         op0=OP.mult)
                    p.op("vector", "scalar_tensor_tensor", inc=(("vv", 1),),
                         out=alo, in0=PB[b][:], scalar=S2, in1=ahi,
                         op0=OP.mult, op1=OP.subtract)
                    bankv[b] = ("vv", p.c("vv"))
                    if co % 16 == 15:
                        qvv.append(p.c("vv"))
                return qvv

            def f_down(tc, qvv, pieces):
                t0 = tc * TF
                whi_t, wlo_t = wsp["wdn"]
                for oc in range(NCI):
                    b = 4 + oc % 2
                    T0 = p.c("tm")
                    qinfo = []
                    for q in range(4):
                        ew = (("tm", T0 + q - 1),) if q >= 2 else ()
                        if tc == 0 and oc == 0 and q < 2:
                            ew = ew + prol_done["wdn"]
                        sq, w1, w2 = w_load_pair(
                            whi_t[oc, :, q * 2048:(q + 1) * 2048],
                            wlo_t[oc, :, q * 2048:(q + 1) * 2048], extra_wait=ew)
                        qinfo.append((sq, w1, w2))
                    first = True
                    for q in range(4):
                        sq, w1, w2 = qinfo[q]
                        for wsel, xoff in ((w_hi[sq], 0), (w_hi[sq], 512),
                                           (w_lo[sq], 0)):
                            for c16 in range(16):
                                ci = q * 16 + c16
                                w = ()
                                if c16 == 0 and xoff == 0:
                                    w = (w1, w2, ("vv", qvv[q]))
                                    if first:
                                        w = w + ((bankv[b][0], bankv[b][1]),)
                                last = (q == 3 and wsel is w_lo[sq]
                                        and c16 == 15)
                                p.op("tensor", "matmul", wait=w,
                                     inc=((("tm", 1),)
                                          if (wsel is w_lo[sq] and c16 == 15)
                                          else ()),
                                     out=PB[b][:],
                                     lhsT=wsel[:, c16 * 128: c16 * 128 + 128],
                                     rhs=actb[:, ci * 1024 + xoff:
                                              ci * 1024 + xoff + 512],
                                     start=first, stop=last)
                                first = False
                    mm_d = p.c("tm")
                    for sq, _, _ in qinfo:
                        w_free[sq] = ("tm", mm_d)
                    stage_out(b, 512, ft_d[oc * 128:(oc + 1) * 128, t0:t0 + TF],
                              mm_d, scale_col=sb_wdp[:, oc:oc + 1])
                    if pieces and oc % 2 == 1 and oc // 2 < len(pieces):
                        pieces[oc // 2]()

            pieces = f_front_pieces(0)
            for piece in pieces:
                piece()
            for tc in range(NCF):
                bf_done = f_upgate_h64(tc)
                z1_free = ("tm", p.c("tm"))
                qvv = f_h128_2(tc, bf_done)
                nxt = f_front_pieces(tc + 1) if tc + 1 < NCF else []
                f_down(tc, qvv, nxt)
                act_free = ("tm", p.c("tm"))
            p.barrier()

        if "B" in phases:
            KB = [A_ACT[:, 0:4096], A_ACT[:, 4096:8192]]
            VTB = A_ACT[:, 8192:12288]
            VB = A_ACT[:, 16384:20480]
            SCRP = [A_ACT[:, 12288:16384], A_ACT[:, 20480:24576]]
            PT = A_ACT[:, 24576:28672]
            QH = [A_ACT[:, 28672:29696], A_ACT[:, 29696:30720]]
            GM8 = [A_ACT[:, 30720:31232], A_ACT[:, 31232:31744]]
            GT8 = A_ACT[:, 31744:32256]
            BIA8 = A_ACT[:, 32256:32768]
            OS = [A_S[:, 0:128], A_S[:, 128:256]]
            KGH = A_S[:, 256:768].bitcast(BF16)
            KGL = A_S[:, 768:1280].bitcast(BF16)
            T88 = A_S[:, 1280:1344]

            p.op("vector", "tensor_copy", wait=(("vv", p.c("vv")),),
                 inc=(("vv", 1),), out=KGH, in_=sb_kg[:])
            p.op("vector", "tensor_tensor", inc=(("vv", 1),), small=True,
                 out=KGL, in0=sb_kg[:], in1=KGH, op=OP.subtract)
            kg_split = p.c("vv")

            kb_free = [("tm", 0), ("tm", 0)]
            qh_free = [("tm", 0), ("tm", 0)]
            gm_free = [("vv", 0), ("vv", 0)]
            os_free = [(), ()]
            vtb_free = ("tm", 0)
            vb_free = ("tm", 0)
            scr_free = [("tm", 0), ("tm", 0)]
            pt_free = ("tm", 0)
            SBK = (2, 3, 4)
            TBK = (0, 1)
            NG = NQT // 8

            def emit_gate_group(h, g):
                gp = g % 2
                tq1 = p.dma("sync", QH[gp][:, 0:512].bitcast(BF16),
                            qhi_d[h * 128:(h + 1) * 128,
                                  g * 1024:(g + 1) * 1024],
                            wait=((qh_free[gp][0], qh_free[gp][1]),))
                tq2 = p.dma("sync", QH[gp][:, 512:1024].bitcast(BF16),
                            qlo_d[h * 128:(h + 1) * 128,
                                  g * 1024:(g + 1) * 1024])
                tgm = p.dma("sync", GM8[gp],
                            gmask_d[:, (h * 0 + g * 8) * 64:(g * 8 + 8) * 64],
                            wait=((gm_free[gp][0], gm_free[gp][1]),))
                qhi8 = QH[gp][:, 0:512].bitcast(BF16)
                qlo8 = QH[gp][:, 512:1024].bitcast(BF16)
                first = True
                for j in range(8):
                    for lh, rh in ((qhi8, KGH), (qhi8, KGL), (qlo8, KGH)):
                        w = ()
                        if first:
                            w = (tq1, tq2, ("vv", kg_split),
                                 (bankv[6][0], bankv[6][1]))
                            first = False
                        p.op("tensor", "matmul", wait=w,
                             inc=((("tm", 1),) if (j == 7 and lh is qlo8)
                                  else ()),
                             out=GATE_PS[:, j * 64:(j + 1) * 64],
                             lhsT=lh[:, j * 128:(j + 1) * 128],
                             rhs=rh[:, h * 64:(h + 1) * 64],
                             start=(lh is qhi8 and rh is KGH),
                             stop=(lh is qlo8))
                gmm = p.c("tm")
                p.op("vector", "scalar_tensor_tensor",
                     wait=(("tm", gmm), tgm), inc=(("vv", 1),),
                     out=GT8[:, :], in0=GATE_PS[:, 0:512], scalar=1.0,
                     in1=GM8[gp], op0=OP.mult, op1=OP.add)
                bankv[6] = ("vv", p.c("vv"))
                for j in range(8):
                    p.op("vector", "max", inc=(("vv", 1),), small=(j == 7),
                         out=T88[:, j * 8: j * 8 + 8],
                         in_=GT8[:, j * 64:(j + 1) * 64])
                for j in range(8):
                    p.op("vector", "tensor_scalar", inc=(("vv", 1),),
                         small=(j == 7),
                         out=GM8[gp][:, j * 64:(j + 1) * 64],
                         in0=GT8[:, j * 64:(j + 1) * 64],
                         scalar1=T88[:, j * 8 + 7: j * 8 + 8],
                         scalar2=NEG, op0=OP.is_lt, op1=OP.mult)
                return p.c("vv"), (tq1, tq2), gp

            def emit_tail_tr(h, qt, p_ready):
                nonlocal pt_free
                njt = qt + 1
                par = qt % 2
                srcs = [SCRP[par][:, k * 128:(k + 1) * 128] for k in range(njt)]
                gdsts = [PT[:, g2 * 512: g2 * 512 + min(4, njt - g2 * 4) * 128]
                         for g2 in range((njt + 3) // 4)]
                pt_done = transpose_tiles(srcs, None, 0, TBK, group_dests=gdsts,
                                          extra_wait=(("tm", pt_free[1]),),
                                          ready=p_ready)
                return pt_done

            def emit_tail_pv(h, qt, pt_done):
                nonlocal pt_free
                njt = qt + 1
                par = qt % 2
                for j in range(njt):
                    w = ()
                    if j == 0:
                        w = (("vv", pt_done), ("vv", v_done[0]),
                             (bankv[5][0], bankv[5][1]))
                    p.op("tensor", "matmul", wait=w,
                         inc=((("tm", 1),) if j == njt - 1 else ()),
                         out=PB[5][:, 0:128], lhsT=VB[:, j * 128:(j + 1) * 128],
                         rhs=PT[:, j * 128:(j + 1) * 128],
                         start=(j == 0), stop=(j == njt - 1))
                pv_mm = p.c("tm")
                pt_free = ("tm", pv_mm)
                scr_free[par] = ("tm", pv_mm)
                oslot = qt % 2
                os_hi = OS[oslot][:, 0:64].bitcast(BF16)
                os_lo = OS[oslot][:, 64:128].bitcast(BF16)
                p.op("vector", "tensor_copy",
                     wait=(("tm", pv_mm),) + tuple(os_free[oslot]),
                     inc=(("vv", 1),),
                     out=os_hi, in_=PB[5][:, 0:128])
                p.op("vector", "tensor_tensor", inc=(("vv", 1),),
                     out=os_lo, in0=PB[5][:, 0:128], in1=os_hi, op=OP.subtract)
                bankv[5] = ("vv", p.c("vv"))
                t_hi = p.dma("gpsimd",
                             ot_hi_d[h * 128:(h + 1) * 128,
                                     qt * 128:(qt + 1) * 128],
                             os_hi, wait=(("vv", p.c("vv")),))
                t_lo = p.dma("gpsimd",
                             ot_lo_d[h * 128:(h + 1) * 128,
                                     qt * 128:(qt + 1) * 128],
                             os_lo)
                os_free[oslot] = (t_hi, t_lo)

            v_done = [0]
            for h in range(NH):
                hs = h % 2
                khi = KB[hs][:, 0:2048].bitcast(BF16)
                klo = KB[hs][:, 2048:4096].bitcast(BF16)
                tk1 = p.dma("sync", khi, khi_d[h * 128:(h + 1) * 128, :],
                            wait=((kb_free[hs][0], kb_free[hs][1]),))
                tk2 = p.dma("sync", klo, klo_d[h * 128:(h + 1) * 128, :])
                wv_ld = p.dma("sync", VTB, vt_d[h * 128:(h + 1) * 128, :],
                              wait=((vtb_free[0], vtb_free[1]),))
                srcs = [VTB[:, k * 128:(k + 1) * 128] for k in range(32)]
                gdsts = [VB[:, g2 * 512:(g2 + 1) * 512] for g2 in range(8)]
                v_done[0] = transpose_tiles(srcs, None, 0, TBK, group_dests=gdsts,
                                            extra_wait=(wv_ld,
                                                        (vb_free[0], vb_free[1]),))
                vtb_free = ("tm", p.c("tm"))
                bias_done, _, _ = emit_gate_group(h, 0)
                bias_vv = {0: bias_done}
                prev1 = None
                prev2 = None
                for qt in range(NQT):
                    g = qt // 8
                    par = qt % 2
                    kvlen = (qt + 1) * 128
                    nkc = (kvlen + 511) // 512
                    gp = g % 2
                    qhi8 = QH[gp][:, 0:512].bitcast(BF16)
                    qlo8 = QH[gp][:, 512:1024].bitcast(BF16)
                    jq = qt % 8
                    tail2 = None
                    if prev2 is not None:
                        tail2 = (prev2[0], emit_tail_tr(h, prev2[0], prev2[1]))
                        prev2 = None
                    if jq == 4 and g + 1 < NG:
                        bias_vv[g + 1], _, _ = emit_gate_group(h, g + 1)
                    qk_mms = []
                    qk_handles = []
                    for kc in range(nkc):
                        kcols = min(512, kvlen - kc * 512)
                        b = SBK[kc % 3]
                        first = True
                        for lh, rh in ((qhi8, khi), (qhi8, klo), (qlo8, khi)):
                            w = ()
                            if first:
                                w = (tk1, tk2, (bankv[b][0], bankv[b][1]))
                            hnd = p.op("tensor", "matmul", wait=w,
                                 inc=((("tm", 1),) if lh is qlo8 else ()),
                                 out=PB[b][:, 0:kcols],
                                 lhsT=lh[:, jq * 128:(jq + 1) * 128],
                                 rhs=rh[:, kc * 512: kc * 512 + kcols],
                                 start=first, stop=(lh is qlo8))
                            if first:
                                qk_handles.append(hnd)
                                first = False
                        qk_mms.append((b, p.c("tm")))
                    if jq == 7:
                        qh_free[gp] = ("tm", p.c("tm"))
                    for kc in range(nkc):
                        kcols = min(512, kvlen - kc * 512)
                        b, s_mm = qk_mms[kc]
                        nb = kcols // 64
                        p.op("vector", "scalar_tensor_tensor",
                             wait=(("tm", s_mm), ("vv", bias_vv[g]),)
                             + ((("tm", scr_free[par][1]),) if kc == 0 else ()),
                             inc=(("vv", 1),),
                             out=SCRP[par][:, kc * 512: kc * 512 + kcols].rearrange(
                                 "p (a b) -> p a b", b=64),
                             in0=PB[b][:, 0:kcols].rearrange("p (a b) -> p a b", b=64),
                             scalar=SM_SCALE,
                             in1=GM8[g % 2][:, jq * 64 + kc * 8:
                                            jq * 64 + kc * 8 + nb]
                             .unsqueeze(2).broadcast_to([128, nb, 64]),
                             op0=OP.mult, op1=OP.add)
                        bankv[b] = ("vv", p.c("vv"))
                        if kc + 3 < nkc:
                            p.add_wait(qk_handles[kc + 3], "vv", p.c("vv"))
                    if jq == 7:
                        gm_free[g % 2] = ("vv", p.c("vv"))
                    p.op("vector", "tensor_tensor", inc=(("vv", 1),),
                         out=SCRP[par][:, kvlen - 128: kvlen],
                         in0=SCRP[par][:, kvlen - 128: kvlen], in1=sb_ca[:],
                         op=OP.add)
                    if tail2 is not None:
                        emit_tail_pv(h, tail2[0], tail2[1])
                    p.op("vector", "reduce_max", inc=(("vv", 1),), small=True,
                         out=stb(C_NMAX, ofs=par), in_=SCRP[par][:, 0:kvlen],
                         axis=AX.X, negate=True)
                    sm_stats = p.c("vv")
                    p.op("scalar", "activation", wait=(("vv", sm_stats),),
                         inc=(("sc", 1),), small=True,
                         out=SCRP[par][:, 0:kvlen], in_=SCRP[par][:, 0:kvlen],
                         func=AF.Exp, bias=stb(C_NMAX, ofs=par), scale=1.0,
                         accum_out=stb(C_SSUM, ofs=par))
                    exp_done = p.c("sc")
                    if prev1 is not None:
                        q1, e1 = prev1
                        par1 = q1 % 2
                        p.op("vector", "reciprocal", wait=(e1,),
                             inc=(("vv", 1),), small=True,
                             out=stb(C_RS, ofs=par1), in_=stb(C_SSUM, ofs=par1))
                        p.op("scalar", "activation", wait=(("vv", p.c("vv")),),
                             inc=(("sc", 1),),
                             out=SCRP[par1][:, 0:(q1 + 1) * 128],
                             in_=SCRP[par1][:, 0:(q1 + 1) * 128],
                             func=AF.Copy, scale=stb(C_RS, ofs=par1))
                        prev2 = (q1, ("sc", p.c("sc")))
                    prev1 = (qt, ("sc", exp_done))
                q1, e1 = prev1
                par1 = q1 % 2
                p.op("vector", "reciprocal", wait=(e1,), inc=(("vv", 1),),
                     small=True, out=stb(C_RS, ofs=par1), in_=stb(C_SSUM, ofs=par1))
                p.op("scalar", "activation", wait=(("vv", p.c("vv")),),
                     inc=(("sc", 1),),
                     out=SCRP[par1][:, 0:(q1 + 1) * 128],
                     in_=SCRP[par1][:, 0:(q1 + 1) * 128],
                     func=AF.Copy, scale=stb(C_RS, ofs=par1))
                emit_tail_pv(h, prev2[0], emit_tail_tr(h, prev2[0], prev2[1]))
                sc_last = ("sc", p.c("sc"))
                emit_tail_pv(h, q1, emit_tail_tr(h, q1, sc_last))
                kb_free[hs] = ("tm", p.c("tm"))
                vb_free = ("tm", p.c("tm"))
            p.barrier()

        if "C" in phases:
            OTB_hi = A_ACT[:, 0:8192].bitcast(BF16)
            OTB_lo = A_ACT[:, 8192:16384].bitcast(BF16)
            FTB = A_ACT[:, 16384:32768]
            OUTB = A_Z
            outb_frees = ()
            for tc in range(NCA):
                t0 = tc * TA
                otfb_free = ("tm", p.c("tm"))
                otft_toks = []
                for ci in range(NCI):
                    otft_toks.append(p.dma("sync",
                          OTB_hi[:, ci * TA:(ci + 1) * TA],
                          ot_hi_d[ci * 128:(ci + 1) * 128, t0:t0 + TA],
                          wait=(otfb_free,)))
                    otft_toks.append(p.dma("sync",
                          OTB_lo[:, ci * TA:(ci + 1) * TA],
                          ot_lo_d[ci * 128:(ci + 1) * 128, t0:t0 + TA],
                          wait=(otfb_free,)))
                    otft_toks.append(p.dma("sync", FTB[:, ci * TA:(ci + 1) * TA],
                          ft_d[ci * 128:(ci + 1) * 128, t0:t0 + TA],
                          wait=(otfb_free,)))
                seen = {}
                for s_, v_ in otft_toks:
                    seen[s_] = max(v_, seen.get(s_, 0))
                otft_ld = tuple(seen.items())
                for th in range(2):
                    out_toks = []
                    for co in range(NCI):
                        whi_t, wlo_t = wsp["wo"]
                        si, wo1, wo2 = w_load_pair(whi_t[co, :, :], wlo_t[co, :, :],
                                                   extra_wait=prol_done["wo"]
                                                   if tc == 0 and th == 0 and co == 0
                                                   else ())
                        b = 2 + co % 2
                        fw = (wo1, wo2) + otft_ld + ((bankv[b][0], bankv[b][1]),)
                        mm_o = mm3(PB[b][:], si,
                                   lambda ci, th=th: OTB_hi[:, ci * TA + th * 512:
                                                            ci * TA + th * 512 + 512],
                                   lambda ci, th=th: OTB_lo[:, ci * TA + th * 512:
                                                            ci * TA + th * 512 + 512],
                                   NCI, fw)
                        w_free[si] = ("tm", mm_o)
                        si2 = stg_i[0] % NSTG
                        stg_i[0] += 1
                        p.op("vector", "tensor_tensor",
                             wait=(("tm", mm_o),) + tuple(stg_free[si2]),
                             inc=(("vv", 1),),
                             out=stg_slots[si2], in0=PB[b][:],
                             in1=FTB[:, co * TA + th * 512: co * TA + th * 512 + 512],
                             op=OP.add)
                        st_done = p.c("vv")
                        bankv[b] = ("vv", st_done)
                        tb = 4 + co % 2
                        for j in range(4):
                            w = ()
                            if j == 0:
                                w = (("vv", st_done), (bankv[tb][0], bankv[tb][1]))
                                if co == 0:
                                    w = w + outb_frees
                            p.op("tensor", "transpose", wait=w,
                                 inc=((("tm", 1),) if j == 3 else ()),
                                 out=PB[tb][:, j * 128:(j + 1) * 128],
                                 in_=stg_slots[si2][:, j * 128:(j + 1) * 128],
                                 identity=sb_id[:])
                        tr_done = p.c("tm")
                        stg_free[si2] = (("tm", tr_done),)
                        p.op("vector", "tensor_copy",
                             wait=(("tm", tr_done),), inc=(("vv", 1),),
                             out=bass.AP(A_Z, co * 128,
                                         [[8192, 128], [2048, 4], [1, 128]]),
                             in_=PB[tb][:, 0:512])
                        bankv[tb] = ("vv", p.c("vv"))
                    outb_ready = p.c("vv")
                    for s4 in range(4):
                        s = th * 4 + s4
                        sl = s % 2
                        xa = xa_slots[sl]
                        wx = p.dma("sync", xa, x_d[t0 + s * 128: t0 + s * 128 + 128, :],
                              wait=((xa_free[sl][0], xa_free[sl][1]),))
                        p.op("vector", "tensor_tensor",
                             wait=(wx, ("vv", outb_ready),), inc=(("vv", 1),),
                             out=OUTB[:, s4 * 2048:(s4 + 1) * 2048],
                             in0=OUTB[:, s4 * 2048:(s4 + 1) * 2048], in1=xa[:],
                             op=OP.add)
                        xa_free[sl] = ("vv", p.c("vv"))
                        out_toks.append(
                            p.dma("gpsimd", out_d[t0 + s * 128: t0 + s * 128 + 128, :],
                                  OUTB[:, s4 * 2048:(s4 + 1) * 2048],
                                  wait=(("vv", p.c("vv")),)))
                    seen = {}
                    for s_, v_ in out_toks:
                        seen[s_] = max(v_, seen.get(s_, 0))
                    outb_frees = tuple(seen.items())

        final = tuple((s, p.c(s)) for s in p.POOL["gpsimd"] + p.POOL["sync"])
        p.op("gpsimd", "engine_nop", wait=final)
        p.op("sync", "nop", wait=final)

        @block.sync
        def _(sync):
            p.replay("sync", sync, sems)

        @block.gpsimd
        def _(gpsimd):
            p.replay("gpsimd", gpsimd, sems)

        @block.vector
        def _(vector):
            p.replay("vector", vector, sems)

        @block.scalar
        def _(scalar):
            p.replay("scalar", scalar, sems)

        @block.tensor
        def _(tensor):
            p.replay("tensor", tensor, sems)

    return nc


def _host_inputs(xb, wq, wk, wv, wo, su_ug, su_dp, wscale_ug, wscale_dp,
                 wup, wgt, wdn):
    def blk_in_out(w):
        i_, o_ = w.shape
        nci, nco = i_ // 128, o_ // 128
        return np.ascontiguousarray(
            w.reshape(nci, 128, nco, 128).transpose(2, 1, 0, 3).reshape(nco, 128, i_))

    def blk_out_in(w):
        return blk_in_out(np.ascontiguousarray(w.T))

    q_global = np.arange(S).reshape(NQT, 128)
    qb = q_global // BLK
    n_idx = np.arange(NBLK)
    gm = np.zeros((NQT, 128, NBLK), np.float32)
    gm[qb[:, :, None] == n_idx[None, None, :]] = BIG
    gm[qb[:, :, None] < n_idx[None, None, :]] = -BIG
    gmask = np.ascontiguousarray(gm.transpose(1, 0, 2).reshape(128, NQT * NBLK))

    i = np.arange(128)
    causal = np.where(i[:, None] >= i[None, :], 0.0, NEG).astype(np.float32)

    hm = np.ones((1, 1), np.float32)
    while hm.shape[0] < 128:
        hm = np.block([[hm, hm], [hm, -hm]])

    def fm(v):
        return np.ascontiguousarray(np.asarray(v, np.float32).reshape(-1, 128).T)

    return {
        "x": np.ascontiguousarray(xb),
        "wq": blk_in_out(wq), "wk": blk_in_out(wk), "wv": blk_in_out(wv),
        "wo": blk_in_out(wo),
        "wup": blk_out_in(wup), "wgt": blk_out_in(wgt), "wdn": blk_out_in(wdn),
        "su_ug": fm(su_ug), "su_dp": fm(su_dp),
        "wsu": fm(wscale_ug[:IN]), "wsg": fm(wscale_ug[IN:]),
        "wdp": fm(wscale_dp),
        "h128": np.ascontiguousarray(hm.astype(np.float32)),
        "ident": np.eye(128, dtype=np.float32),
        "causal": causal,
        "gmask": gmask,
    }


def kernel(L0_attn_hidden_states, L0_attn_Wq, L0_attn_Wk, L0_attn_Wv, L0_attn_Wo,
           L0_ffn_SU_ug, L0_ffn_SU_dp, L0_ffn_Wscale_ug, L0_ffn_Wscale_dp,
           L0_ffn_W_up, L0_ffn_W_gate, L0_ffn_W_down):
    x = np.asarray(L0_attn_hidden_states, np.float32)
    args = [np.asarray(a, np.float32) for a in
            (L0_attn_Wq, L0_attn_Wk, L0_attn_Wv, L0_attn_Wo, L0_ffn_SU_ug,
             L0_ffn_SU_dp, L0_ffn_Wscale_ug, L0_ffn_Wscale_dp, L0_ffn_W_up,
             L0_ffn_W_gate, L0_ffn_W_down)]
    nc = build_nc(dbg=False)
    in_maps = [_host_inputs(x[b], *args) for b in range(8)]
    res = run_bass_kernel_spmd(nc, in_maps, core_ids=list(range(8)))
    return np.stack([res.results[b]["out"] for b in range(8)], axis=0)


# version 13 (final)
# speedup vs baseline: 1.0252x; 1.0242x over previous
import numpy as np
import concourse.bass as bass
import concourse.mybir as mybir
from concourse.bass_utils import run_bass_kernel_spmd

F32 = mybir.dt.float32
BF16 = mybir.dt.bfloat16
AF = mybir.ActivationFunctionType
OP = mybir.AluOpType
AX = mybir.AxisListType

S = 4096
DM = 2048
NH = 16
HD = 128
BLK = 64
NBLK = S // BLK
IN = 8192
NCI = DM // 128
NCO_I = IN // 128
TA = 1024
TF = 512
NCA = S // TA
NCF = S // TF
NQT = S // 128
NQT_B = NQT

SM_SCALE = float(HD) ** -0.5
S1 = 1.0 / (np.sqrt(2048.0) * 64.0)
S2 = 1.0 / (np.sqrt(8192.0) * 64.0)
EPS = 1e-5
NEG = -1.0e9
BIG = 1.0e30

ENGS = ("sync", "gpsimd", "vector", "scalar", "tensor")


OWN_SEM = {"vector": "vv", "scalar": "sc"}


class Prog:
    def __init__(self):
        self.ops = {e: [] for e in ENGS}
        self.cnt = {}

    def op(self, eng, fname, *a, wait=(), inc=(), small=False, **k):
        own = OWN_SEM.get(eng)
        ws = list(wait)
        if own is not None and fname not in ("nop", "engine_nop", "drain", "dma_start"):
            ws.append((own, self.cnt.get(own, 0)))
            if not any(s == own for s, _ in inc):
                inc = tuple(inc) + ((own, 1),)
        incs = []
        for sem, d in inc:
            self.cnt[sem] = self.cnt.get(sem, 0) + d
            incs.append((sem, d))
        ws = [(s, v) for (s, v) in ws if v > 0]
        if small:
            self.ops[eng].append((fname, a, k, ws, []))
            self.ops[eng].append(("drain", (), {}, [], incs))
        else:
            self.ops[eng].append((fname, a, k, ws, incs))
        return (eng, len(self.ops[eng]) - 1)

    def add_wait(self, handle, sem, val):
        eng, i = handle
        fname, a, k, ws, incs = self.ops[eng][i]
        if val > 0:
            self.ops[eng][i] = (fname, a, k, ws + [(sem, val)], incs)

    POOL = {"sync": ["wS%d" % i for i in range(16)],
            "gpsimd": ["wG%d" % i for i in range(8)],
            "scalar": ["wA%d" % i for i in range(8)]}

    def dma(self, eng, out, in_, sem=None, wait=()):
        pool = self.POOL[eng]
        depth = len(pool) // 2
        hist = self.cnt.setdefault("_hist_" + eng, [])
        i = len(hist)
        s = pool[i % len(pool)]
        w = tuple(wait)
        if i >= depth:
            w = w + (hist[i - depth],)
        self.op(eng, "dma_start", wait=w, inc=((s, 16),), out=out, in_=in_)
        tok = (s, self.cnt[s])
        hist.append(tok)
        return tok

    def c(self, sem):
        return self.cnt.get(sem, 0)

    def all_sems(self):
        return (["tm", "vv", "sc"] + self.POOL["sync"] + self.POOL["gpsimd"]
                + self.POOL["scalar"])

    def barrier(self):
        snap = [(s, self.c(s)) for s in self.all_sems()]
        for e in ENGS:
            nopname = {"sync": "nop", "gpsimd": "engine_nop", "vector": "engine_nop",
                       "scalar": "nop", "tensor": "nop"}[e]
            self.op(e, nopname, wait=snap)

    def check_waits(self):
        bad = []
        for e in ENGS:
            for i, (fn, a, k, ws, incs) in enumerate(self.ops[e]):
                for s, v in ws:
                    if v > self.cnt.get(s, 0):
                        bad.append((e, i, fn, s, v, self.cnt.get(s, 0)))
        return bad

    def replay(self, engname, engobj, sems):
        for fname, a, k, waits, incs in self.ops[engname]:
            for s, v in waits:
                engobj.wait_ge(sems[s], v)
            instr = getattr(engobj, fname)(*a, **k)
            for s, d in incs:
                instr.then_inc(sems[s], d)


def build_nc(dbg=False, phases="AFBC"):
    nc = bass.Bass("TRN2", target_bir_lowering=False)

    x_d = nc.dram_tensor("x", [S, DM], F32, kind="ExternalInput").ap()
    wq_d = nc.dram_tensor("wq", [NCI, 128, DM], F32, kind="ExternalInput").ap()
    wk_d = nc.dram_tensor("wk", [NCI, 128, DM], F32, kind="ExternalInput").ap()
    wv_d = nc.dram_tensor("wv", [NCI, 128, DM], F32, kind="ExternalInput").ap()
    wo_d = nc.dram_tensor("wo", [NCI, 128, DM], F32, kind="ExternalInput").ap()
    wup_d = nc.dram_tensor("wup", [NCO_I, 128, DM], F32, kind="ExternalInput").ap()
    wgt_d = nc.dram_tensor("wgt", [NCO_I, 128, DM], F32, kind="ExternalInput").ap()
    wdn_d = nc.dram_tensor("wdn", [NCI, 128, IN], F32, kind="ExternalInput").ap()
    su_ug_d = nc.dram_tensor("su_ug", [128, NCI], F32, kind="ExternalInput").ap()
    su_dp_d = nc.dram_tensor("su_dp", [128, NCO_I], F32, kind="ExternalInput").ap()
    wsu_d = nc.dram_tensor("wsu", [128, NCO_I], F32, kind="ExternalInput").ap()
    wsg_d = nc.dram_tensor("wsg", [128, NCO_I], F32, kind="ExternalInput").ap()
    wdp_d = nc.dram_tensor("wdp", [128, NCI], F32, kind="ExternalInput").ap()
    h128_d = nc.dram_tensor("h128", [128, 128], F32, kind="ExternalInput").ap()
    ident_d = nc.dram_tensor("ident", [128, 128], F32, kind="ExternalInput").ap()
    causal_d = nc.dram_tensor("causal", [128, 128], F32, kind="ExternalInput").ap()
    gmask_d = nc.dram_tensor("gmask", [128, NQT * NBLK], F32, kind="ExternalInput").ap()

    out_d = nc.dram_tensor("out", [S, DM], F32, kind="ExternalOutput").ap()
    skind = "ExternalOutput" if dbg else "Internal"
    qt_d = nc.dram_tensor("qt_s", [DM, S], F32, kind=skind).ap()
    kt_d = nc.dram_tensor("kt_s", [DM, S], F32, kind=skind).ap()
    vt_d = nc.dram_tensor("vt_s", [DM, S], F32, kind=skind).ap()
    ot_d = nc.dram_tensor("ot_s", [DM, S], F32, kind=skind).ap()
    ft_d = nc.dram_tensor("ft_s", [DM, S], F32, kind=skind).ap()
    kg_d = nc.dram_tensor("kg_s", [128, NCI * NBLK], F32, kind=skind).ap()
    wsp = {}
    for nm, nco_, wid_ in (("wq", NCI, DM), ("wk", NCI, DM), ("wv", NCI, DM),
                           ("wo", NCI, DM), ("wup", NCO_I, DM), ("wgt", NCO_I, DM),
                           ("wdn", NCI, IN)):
        wsp[nm] = (nc.dram_tensor(nm + "_hi", [nco_, 128, wid_], BF16, kind="Internal").ap(),
                   nc.dram_tensor(nm + "_lo", [nco_, 128, wid_], BF16, kind="Internal").ap())
    ot_hi_d = nc.dram_tensor("ot_hi", [DM, S], BF16, kind="Internal").ap()
    ot_lo_d = nc.dram_tensor("ot_lo", [DM, S], BF16, kind="Internal").ap()
    qhi_d = nc.dram_tensor("q_hi", [DM, S], BF16, kind="Internal").ap()
    qlo_d = nc.dram_tensor("q_lo", [DM, S], BF16, kind="Internal").ap()
    khi_d = nc.dram_tensor("k_hi", [DM, S], BF16, kind="Internal").ap()
    klo_d = nc.dram_tensor("k_lo", [DM, S], BF16, kind="Internal").ap()
    xn_d = nc.dram_tensor("xn_s", [128, DM], F32, kind=skind).ap()
    st_d = nc.dram_tensor("st_s", [128, 64], F32, kind=skind).ap()
    xnt_d = nc.dram_tensor("xnt_s", [128, TA], F32, kind=skind).ap()

    p = Prog()

    import contextlib
    with contextlib.ExitStack() as stack:
        en = stack.enter_context
        sb_id = en(nc.sbuf_tensor("sb_ident", [128, 128], F32))
        sb_h = en(nc.sbuf_tensor("sb_h128", [128, 128], F32))
        sb_ca = en(nc.sbuf_tensor("sb_causal", [128, 128], F32))
        sb_suug = en(nc.sbuf_tensor("sb_su_ug", [128, NCI], F32))
        sb_sudp = en(nc.sbuf_tensor("sb_su_dp", [128, NCO_I], F32))
        sb_wsu = en(nc.sbuf_tensor("wsu64su", [128, NCO_I], F32))
        sb_wsg = en(nc.sbuf_tensor("wsg64", [128, NCO_I], F32))
        sb_wdp = en(nc.sbuf_tensor("wdp64", [128, NCI], F32))
        sb_kg = en(nc.sbuf_tensor("kgT", [128, NCI * NBLK], F32))
        sb_xng = en(nc.sbuf_tensor("xng", [128, 768], F32))
        sb_st = en(nc.sbuf_tensor("stats", [128, 64], F32))
        A_ACT = en(nc.sbuf_tensor("ACT", [128, 32768], F32))
        A_W = en(nc.sbuf_tensor("WST", [128, 4096], F32))
        A_Z = en(nc.sbuf_tensor("ZB", [128, 8192], F32))
        A_X = en(nc.sbuf_tensor("XB", [128, 4096], F32))
        A_S = en(nc.sbuf_tensor("STG", [128, 1536], F32))
        PB = [en(nc.psum_tensor(f"pb{i}", [128, 512], F32)) for i in range(6)]
        TRASH = en(nc.psum_tensor("trash", [128, 1024], F32))
        GATE_PS = TRASH[:, 0:512]
        sems = {s: en(nc.semaphore(s)) for s in p.all_sems()}
        block = en(nc.Block())
        bankv = {i: ("vv", 0) for i in range(8)}

        def stc(o, c, n=1):
            return sb_st[:, o * 24 + c: o * 24 + c + n]
        C_SUM, C_MEAN, C_VEPS, C_STD, C_RSTD = 0, 1, 2, 3, 4
        C_SQ0 = 8
        def stb(c, n=1, ofs=0):
            return sb_st[:, 48 + c + ofs: 48 + c + ofs + n]
        C_NMAX, C_SSUM, C_RS, C_T8 = 0, 2, 4, 8

        pre_toks = [
            p.dma("sync", sb_id[:], ident_d[:, :]),
            p.dma("sync", sb_h[:], h128_d[:, :]),
            p.dma("sync", sb_ca[:], causal_d[:, :]),
            p.dma("sync", sb_suug[:], su_ug_d[:, :]),
            p.dma("sync", sb_sudp[:], su_dp_d[:, :]),
            p.dma("sync", sb_wsu[:], wsu_d[:, :]),
            p.dma("sync", sb_wsg[:], wsg_d[:, :]),
            p.dma("sync", sb_wdp[:], wdp_d[:, :]),
        ]
        p.op("vector", "memset", ap=sb_st[:], constant=0.0, inc=(("vv", 1),))
        stats_init_vv = p.c("vv")
        p.op("vector", "tensor_scalar", wait=tuple(pre_toks), inc=(("vv", 1),),
             out=sb_wsu[:], in0=sb_wsu[:], scalar1=64.0, scalar2=None, op0=OP.mult)
        p.op("vector", "tensor_tensor", inc=(("vv", 1),),
             out=sb_wsu[:], in0=sb_wsu[:], in1=sb_sudp[:], op=OP.mult)
        p.op("vector", "tensor_scalar", inc=(("vv", 1),),
             out=sb_wsg[:], in0=sb_wsg[:], scalar1=64.0, scalar2=None, op0=OP.mult)
        p.op("vector", "tensor_scalar", inc=(("vv", 1),), small=True,
             out=sb_wdp[:], in0=sb_wdp[:], scalar1=64.0, scalar2=None, op0=OP.mult)

        ln_prev_vv = [stats_init_vv, stats_init_vv]

        def layernorm(xa, wx, par):
            p.op("scalar", "activation",
                 wait=(wx, ("vv", ln_prev_vv[par]),),
                 inc=(("sc", 1),),
                 out=TRASH[:, :], in_=xa[:, 0:1024], func=AF.Square,
                 accum_out=stc(par, C_SQ0))
            p.op("scalar", "activation", inc=(("sc", 1),),
                 out=TRASH[:, :], in_=xa[:, 1024:2048], func=AF.Square,
                 accum_out=stc(par, C_SQ0 + 1))
            p.op("scalar", "activation", inc=(("sc", 1),),
                 out=TRASH[:, :], in_=xa[:, 0:1024], func=AF.Copy,
                 accum_out=stc(par, C_SUM))
            p.op("scalar", "activation", inc=(("sc", 1),), small=True,
                 out=TRASH[:, :], in_=xa[:, 1024:2048], func=AF.Copy,
                 accum_out=stc(par, C_SQ0 + 2))
            sq_done = p.c("sc")
            p.op("vector", "tensor_tensor", wait=(("sc", sq_done),),
                 inc=(("vv", 1),), small=True,
                 out=stc(par, C_SUM), in0=stc(par, C_SUM),
                 in1=stc(par, C_SQ0 + 2), op=OP.add)
            p.op("vector", "tensor_tensor", inc=(("vv", 1),), small=True,
                 out=stc(par, C_SQ0), in0=stc(par, C_SQ0),
                 in1=stc(par, C_SQ0 + 1), op=OP.add)
            p.op("vector", "tensor_scalar", inc=(("vv", 1),), small=True,
                 out=stc(par, C_MEAN), in0=stc(par, C_SUM), scalar1=1.0 / DM,
                 scalar2=None, op0=OP.mult)
            p.op("vector", "tensor_scalar", inc=(("vv", 1),), small=True,
                 out=stc(par, C_SQ0), in0=stc(par, C_SQ0), scalar1=1.0 / DM,
                 scalar2=EPS, op0=OP.mult, op1=OP.add)
            p.op("vector", "tensor_scalar", inc=(("vv", 1),), small=True,
                 out=stc(par, C_VEPS), in0=stc(par, C_MEAN),
                 scalar1=stc(par, C_MEAN), scalar2=-1.0, op0=OP.mult, op1=OP.mult)
            p.op("vector", "tensor_tensor", inc=(("vv", 1),), small=True,
                 out=stc(par, C_VEPS), in0=stc(par, C_SQ0), in1=stc(par, C_VEPS),
                 op=OP.add)
            veps_done = p.c("vv")
            p.op("scalar", "activation", wait=(("vv", veps_done),), inc=(("sc", 1),),
                 small=True, out=stc(par, C_STD), in_=stc(par, C_VEPS), func=AF.Sqrt)
            p.op("vector", "reciprocal", wait=(("sc", p.c("sc")),), inc=(("vv", 1),),
                 small=True, out=stc(par, C_RSTD), in_=stc(par, C_STD))
            p.op("vector", "tensor_scalar", inc=(("vv", 1),),
                 out=xa[:], in0=xa[:], scalar1=stc(par, C_MEAN),
                 scalar2=stc(par, C_RSTD), op0=OP.subtract, op1=OP.mult)
            ln_prev_vv[par] = p.c("vv")
            return p.c("vv")


        def transpose_tiles(srcs, dests, ln_done, banks, sus=None, scale=None,
                            group_dests=None, group_dests_lo=None, extra_wait=(),
                            ready=None):
            n = len(srcs)
            for g in range((n + 3) // 4):
                b = banks[g % len(banks)]
                n_in_g = min(4, n - g * 4)
                for j in range(n_in_g):
                    i = g * 4 + j
                    w = ()
                    if j == 0:
                        r = ready if ready is not None else ("vv", ln_done)
                        w = (r, (bankv[b][0], bankv[b][1])) + tuple(extra_wait)
                    p.op("tensor", "transpose", wait=w,
                         inc=((("tm", 1),) if j == n_in_g - 1 else ()),
                         out=PB[b][:, j * 128:(j + 1) * 128], in_=srcs[i],
                         identity=sb_id[:])
                tmc = p.c("tm")
                if group_dests is not None:
                    p.op("vector", "tensor_copy", wait=(("tm", tmc),),
                         inc=(("vv", 1),),
                         out=group_dests[g], in_=PB[b][:, 0:n_in_g * 128])
                    if group_dests_lo is not None:
                        gsh = group_dests[g].shape
                        p.op("vector", "tensor_tensor", inc=(("vv", 1),),
                             out=group_dests_lo[g],
                             in0=PB[b][:, 0:n_in_g * 128].rearrange(
                                 "p (a b) -> p a b", b=gsh[-1]),
                             in1=group_dests[g], op=OP.subtract)
                else:
                    for j in range(n_in_g):
                        i = g * 4 + j
                        if sus is None and scale is None:
                            p.op("vector", "tensor_copy",
                                 wait=(("tm", tmc),) if j == 0 else (), inc=(("vv", 1),),
                                 out=dests[i], in_=PB[b][:, j * 128:(j + 1) * 128])
                        else:
                            s1 = sus[i] if sus is not None else 1.0
                            p.op("vector", "tensor_scalar",
                                 wait=(("tm", tmc),) if j == 0 else (), inc=(("vv", 1),),
                                 out=dests[i], in0=PB[b][:, j * 128:(j + 1) * 128],
                                 scalar1=s1, scalar2=scale,
                                 op0=OP.mult, op1=(OP.bypass if scale is None else OP.mult))
                bankv[b] = ("vv", p.c("vv"))
            return p.c("vv")

        NSTG = 3
        stg_slots = [A_S[:, i * 512:(i + 1) * 512] for i in range(NSTG)]
        stg_free = [()] * NSTG
        stg_i = [0]

        def stage_out(bank, cols, dram_slice, mm_wait, scale_col=None):
            si = stg_i[0] % NSTG
            stg_i[0] += 1
            slot = stg_slots[si][:, 0:cols]
            if scale_col is None:
                p.op("vector", "tensor_copy",
                     wait=(("tm", mm_wait),) + tuple(stg_free[si]),
                     inc=(("vv", 1),), out=slot, in_=PB[bank][:, 0:cols])
            else:
                p.op("vector", "tensor_scalar",
                     wait=(("tm", mm_wait),) + tuple(stg_free[si]),
                     inc=(("vv", 1),), out=slot, in0=PB[bank][:, 0:cols],
                     scalar1=scale_col, scalar2=None, op0=OP.mult)
            ev = p.c("vv")
            tok = p.dma("gpsimd", dram_slice, slot, wait=(("vv", ev),))
            stg_free[si] = (tok,)
            bankv[bank] = ("vv", ev)
            return ev

        def stage_out_split(bank, hi_slice, lo_slice, mm_wait):
            si = stg_i[0] % NSTG
            stg_i[0] += 1
            s_hi = stg_slots[si][:, 0:256].bitcast(BF16)
            s_lo = stg_slots[si][:, 256:512].bitcast(BF16)
            p.op("vector", "tensor_copy",
                 wait=(("tm", mm_wait),) + tuple(stg_free[si]),
                 inc=(("vv", 1),), out=s_hi, in_=PB[bank][:])
            p.op("vector", "tensor_tensor", inc=(("vv", 1),),
                 out=s_lo, in0=PB[bank][:], in1=s_hi, op=OP.subtract)
            ev = p.c("vv")
            t1 = p.dma("gpsimd", hi_slice, s_hi, wait=(("vv", ev),))
            t2 = p.dma("gpsimd", lo_slice, s_lo, wait=(("vv", ev),))
            stg_free[si] = (t1, t2)
            bankv[bank] = ("vv", ev)
            return ev

        w_slots = [A_W[:, i * 2048:(i + 1) * 2048] for i in range(2)]
        w_hi = [A_W[:, i * 2048: i * 2048 + 1024].bitcast(BF16) for i in range(2)]
        w_lo = [A_W[:, i * 2048 + 1024:(i + 1) * 2048].bitcast(BF16) for i in range(2)]
        w_free = [("tm", 0)] * 2
        w_i = [0]

        def w_load(dram_slice, extra_wait=()):
            si = w_i[0] % 2
            w_i[0] += 1
            tok = p.dma("sync", w_slots[si], dram_slice,
                        wait=((w_free[si][0], w_free[si][1]),) + tuple(extra_wait))
            return si, tok

        def w_load_pair(whi_slice, wlo_slice, extra_wait=()):
            si = w_i[0] % 2
            w_i[0] += 1
            t1 = p.dma("sync", w_hi[si], whi_slice,
                       wait=((w_free[si][0], w_free[si][1]),) + tuple(extra_wait))
            t2 = p.dma("sync", w_lo[si], wlo_slice)
            return si, t1, t2

        def mm3(bank_ap, si, rhs_hi, rhs_lo, nci, first_wait, width=128):
            first = True
            for wsel, xsel in ((w_hi[si], rhs_hi), (w_hi[si], rhs_lo),
                               (w_lo[si], rhs_hi)):
                for ci in range(nci):
                    w = first_wait if first else ()
                    last = (wsel is w_lo[si]) and ci == nci - 1
                    p.op("tensor", "matmul", wait=w,
                         inc=((("tm", 1),) if last else ()),
                         out=bank_ap,
                         lhsT=wsel[:, ci * width:(ci + 1) * width],
                         rhs=xsel(ci),
                         start=first, stop=last)
                    first = False
            return p.c("tm")

        xa_slots = [A_X[:, 0:2048], A_X[:, 2048:4096]]
        xa_free = [("vv", 0), ("vv", 0)]

        land = [A_Z[:, i * 2048:(i + 1) * 2048] for i in range(2)]
        phi = [A_Z[:, 4096 + i * 2048: 4096 + i * 2048 + 1024].bitcast(BF16)
               for i in range(2)]
        plo = [A_Z[:, 4096 + i * 2048 + 1024: 4096 + (i + 1) * 2048].bitcast(BF16)
               for i in range(2)]
        land_free = [("vv", 0), ("vv", 0)]
        pair_free = [[], []]
        ji = [0]
        prol_done = {}

        def split_chunk(fp32_slice, hi_slice, lo_slice):
            sl = ji[0] % 2
            ji[0] += 1
            t_in = p.dma("scalar", land[sl], fp32_slice,
                         wait=((land_free[sl][0], land_free[sl][1]),))
            p.op("vector", "tensor_copy", wait=(t_in,) + tuple(pair_free[sl]),
                 inc=(("vv", 1),), out=phi[sl], in_=land[sl])
            p.op("vector", "tensor_tensor", inc=(("vv", 1),),
                 out=plo[sl], in0=land[sl], in1=phi[sl], op=OP.subtract)
            land_free[sl] = ("vv", p.c("vv"))
            ev = p.c("vv")
            t1 = p.dma("scalar", hi_slice, phi[sl], wait=(("vv", ev),))
            t2 = p.dma("scalar", lo_slice, plo[sl], wait=(("vv", ev),))
            pair_free[sl] = [t1, t2]
            return (t1, t2)

        for nm, dram, nco_, wid_ in (("wq", wq_d, NCI, DM), ("wk", wk_d, NCI, DM),
                                     ("wv", wv_d, NCI, DM), ("wo", wo_d, NCI, DM),
                                     ("wup", wup_d, NCO_I, DM),
                                     ("wgt", wgt_d, NCO_I, DM),
                                     ("wdn", wdn_d, NCI, IN)):
            whi_t, wlo_t = wsp[nm]
            toks = None
            for co in range(nco_):
                for sub in range(wid_ // 2048):
                    sl_ = slice(sub * 2048, (sub + 1) * 2048)
                    toks = split_chunk(dram[co, :, sl_], whi_t[co, :, sl_],
                                       wlo_t[co, :, sl_])
            prol_done[nm] = tuple((s_, p.c(s_)) for s_ in
                                  {toks[0][0], toks[1][0]})
        prol_all = tuple((s_, p.c(s_)) for s_ in p.POOL["scalar"])

        if "A" in phases:
            xnT = A_ACT
            xnT_hi = A_ACT[:, 0:8192].bitcast(BF16)
            xnT_lo = A_ACT[:, 8192:16384].bitcast(BF16)
            xnT_free = ("tm", 0)
            for tc in range(NCA):
                t0 = tc * TA
                for s in range(8):
                    sl = s % 2
                    xa = xa_slots[sl]
                    wx = p.dma("sync", xa, x_d[t0 + s * 128: t0 + s * 128 + 128, :],
                               wait=((xa_free[sl][0], xa_free[sl][1]),))
                    ln_done = layernorm(xa, wx, s % 2)
                    srcs = [xa[:, ec * 128:(ec + 1) * 128] for ec in range(NCI)]
                    ghi = [xnT_hi.rearrange("p (c t) -> p c t", t=TA)
                           [:, g * 4:g * 4 + 4, s * 128:s * 128 + 128]
                           for g in range(4)]
                    glo = [xnT_lo.rearrange("p (c t) -> p c t", t=TA)
                           [:, g * 4:g * 4 + 4, s * 128:s * 128 + 128]
                           for g in range(4)]
                    if dbg and tc == 0 and s == 0:
                        p.dma("gpsimd", xn_d[:, :], xa[:], wait=(("vv", ln_done),))
                        p.dma("gpsimd", st_d[:, :], sb_st[:, 0:64], wait=(("vv", ln_done),))
                        p.barrier()
                    td = transpose_tiles(srcs, None, ln_done, (0, 1),
                                         group_dests=ghi, group_dests_lo=glo,
                                         extra_wait=((xnT_free[0], xnT_free[1]),)
                                         if s == 0 else ())
                    xa_free[sl] = ("vv", td)
                if dbg and tc == 0:
                    p.dma("gpsimd", xnt_d[:, :], xnT[:, 0:TA],
                          wait=(("vv", p.c("vv")),))
                xnT_done = p.c("vv")
                for ci in range(NCI):
                    p.op("vector", "reduce_sum", wait=(("vv", xnT_done),),
                         inc=(("vv", 1),),
                         out=sb_xng[:, ci * 16:(ci + 1) * 16].unsqueeze(2),
                         in_=xnT_hi[:, ci * TA:(ci + 1) * TA].rearrange(
                             "p (b t) -> p b t", t=BLK),
                         axis=AX.X)
                    p.op("vector", "reduce_sum",
                         inc=(("vv", 1),), small=(ci == NCI - 1),
                         out=sb_xng[:, 256 + ci * 16: 256 + ci * 16 + 16].unsqueeze(2),
                         in_=xnT_lo[:, ci * TA:(ci + 1) * TA].rearrange(
                             "p (b t) -> p b t", t=BLK),
                         axis=AX.X)
                p.op("vector", "tensor_tensor", inc=(("vv", 1),),
                     out=sb_xng[:, 0:256], in0=sb_xng[:, 0:256],
                     in1=sb_xng[:, 256:512], op=OP.add)
                p.op("vector", "tensor_scalar", inc=(("vv", 1),),
                     out=sb_xng[:, 0:256], in0=sb_xng[:, 0:256],
                     scalar1=1.0 / BLK, scalar2=None, op0=OP.mult)
                xng_hi = sb_xng[:, 512:640].bitcast(BF16)
                xng_lo = sb_xng[:, 640:768].bitcast(BF16)
                p.op("vector", "tensor_copy", inc=(("vv", 1),),
                     out=xng_hi, in_=sb_xng[:, 0:256])
                p.op("vector", "tensor_tensor", inc=(("vv", 1),), small=True,
                     out=xng_lo, in0=sb_xng[:, 0:256], in1=xng_hi, op=OP.subtract)
                xng_done = p.c("vv")

                for wnm, spill, iskg in (("wq", (qhi_d, qlo_d), False),
                                         ("wk", (khi_d, klo_d), True),
                                         ("wv", vt_d, False)):
                    whi_t, wlo_t = wsp[wnm]
                    for co in range(NCI):
                        si, wd1, wd2 = w_load_pair(whi_t[co, :, :], wlo_t[co, :, :],
                                                   extra_wait=prol_done[wnm]
                                                   if tc == 0 and co == 0 else ())
                        for th in range(2):
                            b = 2 + th
                            fw = (wd1, wd2, ("vv", xnT_done),
                                  (bankv[b][0], bankv[b][1]))
                            mm3(PB[b][:], si,
                                lambda ci, th=th: xnT_hi[:, ci * TA + th * 512:
                                                         ci * TA + th * 512 + 512],
                                lambda ci, th=th: xnT_lo[:, ci * TA + th * 512:
                                                         ci * TA + th * 512 + 512],
                                NCI, fw)
                            if wnm == "wv":
                                stage_out(b, 512,
                                          spill[co * 128:(co + 1) * 128,
                                                t0 + th * 512: t0 + th * 512 + 512],
                                          p.c("tm"))
                            else:
                                stage_out_split(
                                    b,
                                    spill[0][co * 128:(co + 1) * 128,
                                             t0 + th * 512: t0 + th * 512 + 512],
                                    spill[1][co * 128:(co + 1) * 128,
                                             t0 + th * 512: t0 + th * 512 + 512],
                                    p.c("tm"))
                        if iskg:
                            b = 4
                            fw = (("vv", xng_done), (bankv[b][0], bankv[b][1]))
                            mm3(PB[b][:, 0:16], si,
                                lambda ci: xng_hi[:, ci * 16:(ci + 1) * 16],
                                lambda ci: xng_lo[:, ci * 16:(ci + 1) * 16],
                                NCI, fw)
                            p.op("vector", "tensor_copy", wait=(("tm", p.c("tm")),),
                                 inc=(("vv", 1),), small=True,
                                 out=sb_kg[:, co * NBLK + tc * 16:
                                           co * NBLK + tc * 16 + 16],
                                 in_=PB[b][:, 0:16])
                            bankv[b] = ("vv", p.c("vv"))
                        w_free[si] = ("tm", p.c("tm"))
                xnT_free = ("tm", p.c("tm"))
            if dbg:
                p.dma("gpsimd", kg_d[:, :], sb_kg[:], wait=(("vv", p.c("vv")),))
            p.barrier()

        if "F" in phases:
            ACT = A_ACT
            Z1 = A_Z
            actb = A_ACT[:, :].bitcast(BF16)
            z1b = A_Z[:, :].bitcast(BF16)
            z1hi = lambda ci: z1b[:, ci * 1024: ci * 1024 + 512]
            z1lo = lambda ci: z1b[:, ci * 1024 + 512: ci * 1024 + 1024]
            act_free = ("tm", 0)
            z1_free = ("tm", 0)
            z1_prol = prol_all

            def f_front_pieces(tc):
                t0 = tc * TF

                def mk_ln_tr(s):
                    def go():
                        sl = s % 2
                        xa = xa_slots[sl]
                        wx = p.dma("sync", xa,
                                   x_d[t0 + s * 128: t0 + s * 128 + 128, :],
                                   wait=((xa_free[sl][0], xa_free[sl][1]),))
                        ln_done = layernorm(xa, wx, s % 2)
                        srcs = [xa[:, ec * 128:(ec + 1) * 128] for ec in range(NCI)]
                        dsts = [Z1[:, ec * TF + s * 128: ec * TF + s * 128 + 128]
                                for ec in range(NCI)]
                        sus = [sb_suug[:, ec:ec + 1] for ec in range(NCI)]
                        ew = ((z1_free[0], z1_free[1]),)
                        if tc == 0:
                            ew = ew + z1_prol
                        td = transpose_tiles(srcs, dsts, ln_done, (0, 1), sus=sus,
                                             extra_wait=ew if s == 0 else ())
                        xa_free[sl] = ("vv", td)
                    return go

                def h16():
                    hh = 1
                    while hh < NCI:
                        for base in range(0, NCI, 2 * hh):
                            for j in range(hh):
                                a = (base + j) * TF
                                bcol = (base + j + hh) * TF
                                p.op("vector", "tensor_tensor", inc=(("vv", 1),),
                                     out=Z1[:, a:a + TF], in0=Z1[:, a:a + TF],
                                     in1=Z1[:, bcol:bcol + TF], op=OP.add)
                                p.op("vector", "scalar_tensor_tensor",
                                     inc=(("vv", 1),),
                                     out=Z1[:, bcol:bcol + TF],
                                     in0=Z1[:, bcol:bcol + TF],
                                     scalar=-2.0, in1=Z1[:, a:a + TF],
                                     op0=OP.mult, op1=OP.add)
                        hh *= 2

                def h128_1():
                    bf1_done = p.c("vv")
                    for ci in range(NCI):
                        b = 2 + ci % 2
                        p.op("tensor", "matmul",
                             wait=(("vv", bf1_done), (bankv[b][0], bankv[b][1]),),
                             inc=(("tm", 1),),
                             out=PB[b][:], lhsT=sb_h[:],
                             rhs=Z1[:, ci * TF:(ci + 1) * TF],
                             start=True, stop=True)
                        zh = z1b[:, ci * 1024: ci * 1024 + 512]
                        zl = z1b[:, ci * 1024 + 512: ci * 1024 + 1024]
                        p.op("vector", "tensor_scalar", wait=(("tm", p.c("tm")),),
                             inc=(("vv", 1),),
                             out=zh, in0=PB[b][:], scalar1=S1,
                             scalar2=None, op0=OP.mult)
                        p.op("vector", "scalar_tensor_tensor", inc=(("vv", 1),),
                             out=zl, in0=PB[b][:], scalar=S1, in1=zh,
                             op0=OP.mult, op1=OP.subtract)
                        bankv[b] = ("vv", p.c("vv"))
                return [mk_ln_tr(0), mk_ln_tr(1), mk_ln_tr(2), mk_ln_tr(3),
                        h16, h128_1]

            def f_upgate_h64(tc):
                nonlocal act_free

                def bfly(a, bcol):
                    p.op("vector", "tensor_tensor", inc=(("vv", 1),),
                         out=ACT[:, a * TF:a * TF + TF], in0=ACT[:, a * TF:a * TF + TF],
                         in1=ACT[:, bcol * TF:bcol * TF + TF], op=OP.add)
                    p.op("vector", "scalar_tensor_tensor", inc=(("vv", 1),),
                         out=ACT[:, bcol * TF:bcol * TF + TF],
                         in0=ACT[:, bcol * TF:bcol * TF + TF],
                         scalar=-2.0, in1=ACT[:, a * TF:a * TF + TF],
                         op0=OP.mult, op1=OP.add)

                z1_done = p.c("vv")
                for co in range(NCO_I):
                    first_use = tc == 0 and co == 0
                    bu = 4 if co % 2 == 0 else 2
                    bg = 5 if co % 2 == 0 else 3
                    whi_t, wlo_t = wsp["wup"]
                    siu, wu1, wu2 = w_load_pair(whi_t[co, :, :], wlo_t[co, :, :],
                                                extra_wait=prol_done["wup"]
                                                if first_use else ())
                    whi_t, wlo_t = wsp["wgt"]
                    sig, wg1, wg2 = w_load_pair(whi_t[co, :, :], wlo_t[co, :, :],
                                                extra_wait=prol_done["wgt"]
                                                if first_use else ())
                    fw = (wu1, wu2, ("vv", z1_done), (bankv[bu][0], bankv[bu][1]))
                    mm_u = mm3(PB[bu][:], siu, z1hi, z1lo, NCI, fw)
                    fw = (wg1, wg2, (bankv[bg][0], bankv[bg][1]))
                    if co == 0:
                        fw = fw + ((act_free[0], act_free[1]),)
                    mm_g = mm3(PB[bg][:], sig, z1hi, z1lo, NCI, fw)
                    w_free[siu] = ("tm", mm_u)
                    w_free[sig] = ("tm", mm_g)
                    si = stg_i[0] % NSTG
                    stg_i[0] += 1
                    p.op("vector", "tensor_scalar",
                         wait=(("tm", mm_u),) + tuple(stg_free[si]),
                         inc=(("vv", 1),),
                         out=stg_slots[si], in0=PB[bu][:],
                         scalar1=sb_wsu[:, co:co + 1], scalar2=None, op0=OP.mult)
                    bankv[bu] = ("vv", p.c("vv"))
                    p.op("scalar", "activation",
                         wait=(("tm", mm_g),) + (((act_free[0], act_free[1]),)
                                                 if co == 0 else ()),
                         inc=(("sc", 1),),
                         out=ACT[:, co * TF:(co + 1) * TF], in_=PB[bg][:],
                         func=AF.Silu, scale=sb_wsg[:, co:co + 1])
                    sg_done = p.c("sc")
                    bankv[bg] = ("sc", sg_done)
                    p.op("vector", "tensor_tensor", wait=(("sc", sg_done),),
                         inc=(("vv", 1),),
                         out=ACT[:, co * TF:(co + 1) * TF], in0=stg_slots[si],
                         in1=ACT[:, co * TF:(co + 1) * TF], op=OP.mult)
                    stg_free[si] = (("vv", p.c("vv")),)
                    for s in range(6):
                        blk_sz = 2 << s
                        if (co + 1) % blk_sz == 0:
                            base = co + 1 - blk_sz
                            for j in range(1 << s):
                                bfly(base + j, base + j + (1 << s))
                return p.c("vv")

            def f_h128_2(tc, bf_done):
                qvv = []
                for co in range(NCO_I):
                    b = 2 + co % 2
                    p.op("tensor", "matmul",
                         wait=((("vv", bf_done),) if co == 0 else ())
                         + ((bankv[b][0], bankv[b][1]),),
                         inc=(("tm", 1),),
                         out=PB[b][:], lhsT=sb_h[:],
                         rhs=ACT[:, co * TF:(co + 1) * TF],
                         start=True, stop=True)
                    ahi = actb[:, co * 1024: co * 1024 + 512]
                    alo = actb[:, co * 1024 + 512: co * 1024 + 1024]
                    p.op("vector", "tensor_scalar", wait=(("tm", p.c("tm")),),
                         inc=(("vv", 1),),
                         out=ahi, in0=PB[b][:], scalar1=S2, scalar2=None,
                         op0=OP.mult)
                    p.op("vector", "scalar_tensor_tensor", inc=(("vv", 1),),
                         out=alo, in0=PB[b][:], scalar=S2, in1=ahi,
                         op0=OP.mult, op1=OP.subtract)
                    bankv[b] = ("vv", p.c("vv"))
                    if co % 16 == 15:
                        qvv.append(p.c("vv"))
                return qvv

            def f_down(tc, qvv, pieces):
                t0 = tc * TF
                whi_t, wlo_t = wsp["wdn"]
                for oc in range(NCI):
                    b = 4 + oc % 2
                    T0 = p.c("tm")
                    qinfo = []
                    for q in range(4):
                        ew = (("tm", T0 + q - 1),) if q >= 2 else ()
                        if tc == 0 and oc == 0 and q < 2:
                            ew = ew + prol_done["wdn"]
                        sq, w1, w2 = w_load_pair(
                            whi_t[oc, :, q * 2048:(q + 1) * 2048],
                            wlo_t[oc, :, q * 2048:(q + 1) * 2048], extra_wait=ew)
                        qinfo.append((sq, w1, w2))
                    first = True
                    for q in range(4):
                        sq, w1, w2 = qinfo[q]
                        for wsel, xoff in ((w_hi[sq], 0), (w_hi[sq], 512),
                                           (w_lo[sq], 0)):
                            for c16 in range(16):
                                ci = q * 16 + c16
                                w = ()
                                if c16 == 0 and xoff == 0:
                                    w = (w1, w2, ("vv", qvv[q]))
                                    if first:
                                        w = w + ((bankv[b][0], bankv[b][1]),)
                                last = (q == 3 and wsel is w_lo[sq]
                                        and c16 == 15)
                                p.op("tensor", "matmul", wait=w,
                                     inc=((("tm", 1),)
                                          if (wsel is w_lo[sq] and c16 == 15)
                                          else ()),
                                     out=PB[b][:],
                                     lhsT=wsel[:, c16 * 128: c16 * 128 + 128],
                                     rhs=actb[:, ci * 1024 + xoff:
                                              ci * 1024 + xoff + 512],
                                     start=first, stop=last)
                                first = False
                    mm_d = p.c("tm")
                    for sq, _, _ in qinfo:
                        w_free[sq] = ("tm", mm_d)
                    stage_out(b, 512, ft_d[oc * 128:(oc + 1) * 128, t0:t0 + TF],
                              mm_d, scale_col=sb_wdp[:, oc:oc + 1])
                    if pieces and oc % 2 == 1 and oc // 2 < len(pieces):
                        pieces[oc // 2]()

            pieces = f_front_pieces(0)
            for piece in pieces:
                piece()
            for tc in range(NCF):
                bf_done = f_upgate_h64(tc)
                z1_free = ("tm", p.c("tm"))
                qvv = f_h128_2(tc, bf_done)
                nxt = f_front_pieces(tc + 1) if tc + 1 < NCF else []
                f_down(tc, qvv, nxt)
                act_free = ("tm", p.c("tm"))
            p.barrier()

        if "B" in phases:
            KB = [A_ACT[:, 0:4096], A_ACT[:, 4096:8192]]
            VTB = A_ACT[:, 8192:12288]
            VB = A_ACT[:, 16384:20480]
            SCRP = [A_ACT[:, 12288:16384], A_ACT[:, 20480:24576]]
            PT = A_ACT[:, 24576:28672]
            QH = [A_ACT[:, 28672:29696], A_ACT[:, 29696:30720]]
            GM8 = [A_ACT[:, 30720:31232], A_ACT[:, 31232:31744]]
            GT8 = A_ACT[:, 31744:32256]
            BIA8 = A_ACT[:, 32256:32768]
            OS = [A_S[:, 0:128], A_S[:, 128:256]]
            KGH = A_S[:, 256:768].bitcast(BF16)
            KGL = A_S[:, 768:1280].bitcast(BF16)
            T88 = A_S[:, 1280:1344]

            p.op("vector", "tensor_copy", wait=(("vv", p.c("vv")),),
                 inc=(("vv", 1),), out=KGH, in_=sb_kg[:])
            p.op("vector", "tensor_tensor", inc=(("vv", 1),), small=True,
                 out=KGL, in0=sb_kg[:], in1=KGH, op=OP.subtract)
            kg_split = p.c("vv")

            kb_free = [("tm", 0), ("tm", 0)]
            qh_free = [("tm", 0), ("tm", 0)]
            gm_free = [("vv", 0), ("vv", 0)]
            os_free = [(), ()]
            vtb_free = ("tm", 0)
            vb_free = ("tm", 0)
            scr_free = [("tm", 0), ("tm", 0)]
            pt_free = ("tm", 0)
            SBK = (2, 3, 4)
            TBK = (0, 1)
            NG = NQT // 8

            def emit_gate_group(h, g):
                gp = g % 2
                tq1 = p.dma("sync", QH[gp][:, 0:512].bitcast(BF16),
                            qhi_d[h * 128:(h + 1) * 128,
                                  g * 1024:(g + 1) * 1024],
                            wait=((qh_free[gp][0], qh_free[gp][1]),))
                tq2 = p.dma("sync", QH[gp][:, 512:1024].bitcast(BF16),
                            qlo_d[h * 128:(h + 1) * 128,
                                  g * 1024:(g + 1) * 1024])
                tgm = p.dma("sync", GM8[gp],
                            gmask_d[:, (h * 0 + g * 8) * 64:(g * 8 + 8) * 64],
                            wait=((gm_free[gp][0], gm_free[gp][1]),))
                qhi8 = QH[gp][:, 0:512].bitcast(BF16)
                qlo8 = QH[gp][:, 512:1024].bitcast(BF16)
                first = True
                for j in range(8):
                    for lh, rh in ((qhi8, KGH), (qhi8, KGL), (qlo8, KGH)):
                        w = ()
                        if first:
                            w = (tq1, tq2, ("vv", kg_split),
                                 (bankv[6][0], bankv[6][1]))
                            first = False
                        p.op("tensor", "matmul", wait=w,
                             inc=((("tm", 1),) if (j == 7 and lh is qlo8)
                                  else ()),
                             out=GATE_PS[:, j * 64:(j + 1) * 64],
                             lhsT=lh[:, j * 128:(j + 1) * 128],
                             rhs=rh[:, h * 64:(h + 1) * 64],
                             start=(lh is qhi8 and rh is KGH),
                             stop=(lh is qlo8))
                gmm = p.c("tm")
                p.op("vector", "scalar_tensor_tensor",
                     wait=(("tm", gmm), tgm), inc=(("vv", 1),),
                     out=GT8[:, :], in0=GATE_PS[:, 0:512], scalar=1.0,
                     in1=GM8[gp], op0=OP.mult, op1=OP.add)
                bankv[6] = ("vv", p.c("vv"))
                for j in range(8):
                    p.op("vector", "max", inc=(("vv", 1),), small=(j == 7),
                         out=T88[:, j * 8: j * 8 + 8],
                         in_=GT8[:, j * 64:(j + 1) * 64])
                for j in range(8):
                    p.op("vector", "tensor_scalar", inc=(("vv", 1),),
                         small=(j == 7),
                         out=GM8[gp][:, j * 64:(j + 1) * 64],
                         in0=GT8[:, j * 64:(j + 1) * 64],
                         scalar1=T88[:, j * 8 + 7: j * 8 + 8],
                         scalar2=NEG, op0=OP.is_lt, op1=OP.mult)
                return p.c("vv"), (tq1, tq2), gp

            def emit_tail_tr(h, qt, p_ready):
                nonlocal pt_free
                njt = qt + 1
                par = qt % 2
                srcs = [SCRP[par][:, k * 128:(k + 1) * 128] for k in range(njt)]
                gdsts = [PT[:, g2 * 512: g2 * 512 + min(4, njt - g2 * 4) * 128]
                         for g2 in range((njt + 3) // 4)]
                pt_done = transpose_tiles(srcs, None, 0, TBK, group_dests=gdsts,
                                          extra_wait=(("tm", pt_free[1]),),
                                          ready=p_ready)
                return pt_done

            def emit_tail_pv(h, qt, pt_done):
                nonlocal pt_free
                njt = qt + 1
                par = qt % 2
                bo = 5 if qt % 2 == 0 else 6
                out_ap = PB[5][:, 0:128] if bo == 5 else GATE_PS[:, 256:384]
                for j in range(njt):
                    w = ()
                    if j == 0:
                        w = (("vv", pt_done), ("vv", v_done[0]),
                             (bankv[bo][0], bankv[bo][1]))
                    p.op("tensor", "matmul", wait=w,
                         inc=((("tm", 1),) if j == njt - 1 else ()),
                         out=out_ap, lhsT=VB[:, j * 128:(j + 1) * 128],
                         rhs=PT[:, j * 128:(j + 1) * 128],
                         start=(j == 0), stop=(j == njt - 1))
                pv_mm = p.c("tm")
                pt_free = ("tm", pv_mm)
                scr_free[par] = ("tm", pv_mm)
                oslot = qt % 2
                os_hi = OS[oslot][:, 0:64].bitcast(BF16)
                os_lo = OS[oslot][:, 64:128].bitcast(BF16)
                p.op("vector", "tensor_copy",
                     wait=(("tm", pv_mm),) + tuple(os_free[oslot]),
                     inc=(("vv", 1),),
                     out=os_hi, in_=out_ap)
                p.op("vector", "tensor_tensor", inc=(("vv", 1),),
                     out=os_lo, in0=out_ap, in1=os_hi, op=OP.subtract)
                bankv[bo] = ("vv", p.c("vv"))
                t_hi = p.dma("gpsimd",
                             ot_hi_d[h * 128:(h + 1) * 128,
                                     qt * 128:(qt + 1) * 128],
                             os_hi, wait=(("vv", p.c("vv")),))
                t_lo = p.dma("gpsimd",
                             ot_lo_d[h * 128:(h + 1) * 128,
                                     qt * 128:(qt + 1) * 128],
                             os_lo)
                os_free[oslot] = (t_hi, t_lo)

            v_done = [0]
            for h in range(NH):
                hs = h % 2
                khi = KB[hs][:, 0:2048].bitcast(BF16)
                klo = KB[hs][:, 2048:4096].bitcast(BF16)
                tk1 = p.dma("sync", khi, khi_d[h * 128:(h + 1) * 128, :],
                            wait=((kb_free[hs][0], kb_free[hs][1]),))
                tk2 = p.dma("sync", klo, klo_d[h * 128:(h + 1) * 128, :])
                wv_ld = p.dma("sync", VTB, vt_d[h * 128:(h + 1) * 128, :],
                              wait=((vtb_free[0], vtb_free[1]),))
                srcs = [VTB[:, k * 128:(k + 1) * 128] for k in range(32)]
                gdsts = [VB[:, g2 * 512:(g2 + 1) * 512] for g2 in range(8)]
                v_done[0] = transpose_tiles(srcs, None, 0, TBK, group_dests=gdsts,
                                            extra_wait=(wv_ld,
                                                        (vb_free[0], vb_free[1]),))
                vtb_free = ("tm", p.c("tm"))
                bias_done, _, _ = emit_gate_group(h, 0)
                bias_vv = {0: bias_done}
                prev1 = None
                prev2 = None
                for qt in range(NQT):
                    g = qt // 8
                    par = qt % 2
                    kvlen = (qt + 1) * 128
                    nkc = (kvlen + 511) // 512
                    gp = g % 2
                    qhi8 = QH[gp][:, 0:512].bitcast(BF16)
                    qlo8 = QH[gp][:, 512:1024].bitcast(BF16)
                    jq = qt % 8
                    tail2 = None
                    if prev2 is not None:
                        tail2 = (prev2[0], emit_tail_tr(h, prev2[0], prev2[1]))
                        prev2 = None
                    if jq == 4 and g + 1 < NG:
                        bias_vv[g + 1], _, _ = emit_gate_group(h, g + 1)
                    qk_mms = []
                    qk_handles = []
                    for kc in range(nkc):
                        kcols = min(512, kvlen - kc * 512)
                        b = SBK[kc % 3]
                        first = True
                        for lh, rh in ((qhi8, khi), (qhi8, klo), (qlo8, khi)):
                            w = ()
                            if first:
                                w = (tk1, tk2, (bankv[b][0], bankv[b][1]))
                            hnd = p.op("tensor", "matmul", wait=w,
                                 inc=((("tm", 1),) if lh is qlo8 else ()),
                                 out=PB[b][:, 0:kcols],
                                 lhsT=lh[:, jq * 128:(jq + 1) * 128],
                                 rhs=rh[:, kc * 512: kc * 512 + kcols],
                                 start=first, stop=(lh is qlo8))
                            if first:
                                qk_handles.append(hnd)
                                first = False
                        qk_mms.append((b, p.c("tm")))
                    if jq == 7:
                        qh_free[gp] = ("tm", p.c("tm"))
                    for kc in range(nkc):
                        kcols = min(512, kvlen - kc * 512)
                        b, s_mm = qk_mms[kc]
                        nb = kcols // 64
                        p.op("vector", "scalar_tensor_tensor",
                             wait=(("tm", s_mm), ("vv", bias_vv[g]),)
                             + ((("tm", scr_free[par][1]),) if kc == 0 else ()),
                             inc=(("vv", 1),),
                             out=SCRP[par][:, kc * 512: kc * 512 + kcols].rearrange(
                                 "p (a b) -> p a b", b=64),
                             in0=PB[b][:, 0:kcols].rearrange("p (a b) -> p a b", b=64),
                             scalar=SM_SCALE,
                             in1=GM8[g % 2][:, jq * 64 + kc * 8:
                                            jq * 64 + kc * 8 + nb]
                             .unsqueeze(2).broadcast_to([128, nb, 64]),
                             op0=OP.mult, op1=OP.add)
                        bankv[b] = ("vv", p.c("vv"))
                        if kc + 3 < nkc:
                            p.add_wait(qk_handles[kc + 3], "vv", p.c("vv"))
                    if jq == 7:
                        gm_free[g % 2] = ("vv", p.c("vv"))
                    p.op("vector", "tensor_tensor", inc=(("vv", 1),),
                         out=SCRP[par][:, kvlen - 128: kvlen],
                         in0=SCRP[par][:, kvlen - 128: kvlen], in1=sb_ca[:],
                         op=OP.add)
                    if tail2 is not None:
                        emit_tail_pv(h, tail2[0], tail2[1])
                    p.op("vector", "reduce_max", inc=(("vv", 1),), small=True,
                         out=stb(C_NMAX, ofs=par), in_=SCRP[par][:, 0:kvlen],
                         axis=AX.X, negate=True)
                    sm_stats = p.c("vv")
                    p.op("scalar", "activation", wait=(("vv", sm_stats),),
                         inc=(("sc", 1),), small=True,
                         out=SCRP[par][:, 0:kvlen], in_=SCRP[par][:, 0:kvlen],
                         func=AF.Exp, bias=stb(C_NMAX, ofs=par), scale=1.0,
                         accum_out=stb(C_SSUM, ofs=par))
                    exp_done = p.c("sc")
                    if prev1 is not None:
                        q1, e1 = prev1
                        par1 = q1 % 2
                        p.op("vector", "reciprocal", wait=(e1,),
                             inc=(("vv", 1),), small=True,
                             out=stb(C_RS, ofs=par1), in_=stb(C_SSUM, ofs=par1))
                        p.op("scalar", "activation", wait=(("vv", p.c("vv")),),
                             inc=(("sc", 1),),
                             out=SCRP[par1][:, 0:(q1 + 1) * 128],
                             in_=SCRP[par1][:, 0:(q1 + 1) * 128],
                             func=AF.Copy, scale=stb(C_RS, ofs=par1))
                        prev2 = (q1, ("sc", p.c("sc")))
                    prev1 = (qt, ("sc", exp_done))
                q1, e1 = prev1
                par1 = q1 % 2
                p.op("vector", "reciprocal", wait=(e1,), inc=(("vv", 1),),
                     small=True, out=stb(C_RS, ofs=par1), in_=stb(C_SSUM, ofs=par1))
                p.op("scalar", "activation", wait=(("vv", p.c("vv")),),
                     inc=(("sc", 1),),
                     out=SCRP[par1][:, 0:(q1 + 1) * 128],
                     in_=SCRP[par1][:, 0:(q1 + 1) * 128],
                     func=AF.Copy, scale=stb(C_RS, ofs=par1))
                emit_tail_pv(h, prev2[0], emit_tail_tr(h, prev2[0], prev2[1]))
                sc_last = ("sc", p.c("sc"))
                emit_tail_pv(h, q1, emit_tail_tr(h, q1, sc_last))
                kb_free[hs] = ("tm", p.c("tm"))
                vb_free = ("tm", p.c("tm"))
            p.barrier()

        if "C" in phases:
            OTB_hi = A_ACT[:, 0:8192].bitcast(BF16)
            OTB_lo = A_ACT[:, 8192:16384].bitcast(BF16)
            FTB = A_ACT[:, 16384:32768]
            OUTB = A_Z
            outb_frees = ()
            for tc in range(NCA):
                t0 = tc * TA
                otfb_free = ("tm", p.c("tm"))
                otft_toks = []
                for ci in range(NCI):
                    otft_toks.append(p.dma("sync",
                          OTB_hi[:, ci * TA:(ci + 1) * TA],
                          ot_hi_d[ci * 128:(ci + 1) * 128, t0:t0 + TA],
                          wait=(otfb_free,)))
                    otft_toks.append(p.dma("sync",
                          OTB_lo[:, ci * TA:(ci + 1) * TA],
                          ot_lo_d[ci * 128:(ci + 1) * 128, t0:t0 + TA],
                          wait=(otfb_free,)))
                    otft_toks.append(p.dma("sync", FTB[:, ci * TA:(ci + 1) * TA],
                          ft_d[ci * 128:(ci + 1) * 128, t0:t0 + TA],
                          wait=(otfb_free,)))
                seen = {}
                for s_, v_ in otft_toks:
                    seen[s_] = max(v_, seen.get(s_, 0))
                otft_ld = tuple(seen.items())
                for th in range(2):
                    out_toks = []
                    for co in range(NCI):
                        whi_t, wlo_t = wsp["wo"]
                        si, wo1, wo2 = w_load_pair(whi_t[co, :, :], wlo_t[co, :, :],
                                                   extra_wait=prol_done["wo"]
                                                   if tc == 0 and th == 0 and co == 0
                                                   else ())
                        b = 2 + co % 2
                        fw = (wo1, wo2) + otft_ld + ((bankv[b][0], bankv[b][1]),)
                        mm_o = mm3(PB[b][:], si,
                                   lambda ci, th=th: OTB_hi[:, ci * TA + th * 512:
                                                            ci * TA + th * 512 + 512],
                                   lambda ci, th=th: OTB_lo[:, ci * TA + th * 512:
                                                            ci * TA + th * 512 + 512],
                                   NCI, fw)
                        w_free[si] = ("tm", mm_o)
                        si2 = stg_i[0] % NSTG
                        stg_i[0] += 1
                        p.op("vector", "tensor_tensor",
                             wait=(("tm", mm_o),) + tuple(stg_free[si2]),
                             inc=(("vv", 1),),
                             out=stg_slots[si2], in0=PB[b][:],
                             in1=FTB[:, co * TA + th * 512: co * TA + th * 512 + 512],
                             op=OP.add)
                        st_done = p.c("vv")
                        bankv[b] = ("vv", st_done)
                        tb = 4 + co % 2
                        for j in range(4):
                            w = ()
                            if j == 0:
                                w = (("vv", st_done), (bankv[tb][0], bankv[tb][1]))
                                if co == 0:
                                    w = w + outb_frees
                            p.op("tensor", "transpose", wait=w,
                                 inc=((("tm", 1),) if j == 3 else ()),
                                 out=PB[tb][:, j * 128:(j + 1) * 128],
                                 in_=stg_slots[si2][:, j * 128:(j + 1) * 128],
                                 identity=sb_id[:])
                        tr_done = p.c("tm")
                        stg_free[si2] = (("tm", tr_done),)
                        p.op("vector", "tensor_copy",
                             wait=(("tm", tr_done),), inc=(("vv", 1),),
                             out=bass.AP(A_Z, co * 128,
                                         [[8192, 128], [2048, 4], [1, 128]]),
                             in_=PB[tb][:, 0:512])
                        bankv[tb] = ("vv", p.c("vv"))
                    outb_ready = p.c("vv")
                    for s4 in range(4):
                        s = th * 4 + s4
                        sl = s % 2
                        xa = xa_slots[sl]
                        wx = p.dma("sync", xa, x_d[t0 + s * 128: t0 + s * 128 + 128, :],
                              wait=((xa_free[sl][0], xa_free[sl][1]),))
                        p.op("vector", "tensor_tensor",
                             wait=(wx, ("vv", outb_ready),), inc=(("vv", 1),),
                             out=OUTB[:, s4 * 2048:(s4 + 1) * 2048],
                             in0=OUTB[:, s4 * 2048:(s4 + 1) * 2048], in1=xa[:],
                             op=OP.add)
                        xa_free[sl] = ("vv", p.c("vv"))
                        out_toks.append(
                            p.dma("gpsimd", out_d[t0 + s * 128: t0 + s * 128 + 128, :],
                                  OUTB[:, s4 * 2048:(s4 + 1) * 2048],
                                  wait=(("vv", p.c("vv")),)))
                    seen = {}
                    for s_, v_ in out_toks:
                        seen[s_] = max(v_, seen.get(s_, 0))
                    outb_frees = tuple(seen.items())

        final = tuple((s, p.c(s)) for s in p.POOL["gpsimd"] + p.POOL["sync"])
        p.op("gpsimd", "engine_nop", wait=final)
        p.op("sync", "nop", wait=final)

        @block.sync
        def _(sync):
            p.replay("sync", sync, sems)

        @block.gpsimd
        def _(gpsimd):
            p.replay("gpsimd", gpsimd, sems)

        @block.vector
        def _(vector):
            p.replay("vector", vector, sems)

        @block.scalar
        def _(scalar):
            p.replay("scalar", scalar, sems)

        @block.tensor
        def _(tensor):
            p.replay("tensor", tensor, sems)

    return nc


def _host_inputs(xb, wq, wk, wv, wo, su_ug, su_dp, wscale_ug, wscale_dp,
                 wup, wgt, wdn):
    def blk_in_out(w):
        i_, o_ = w.shape
        nci, nco = i_ // 128, o_ // 128
        return np.ascontiguousarray(
            w.reshape(nci, 128, nco, 128).transpose(2, 1, 0, 3).reshape(nco, 128, i_))

    def blk_out_in(w):
        return blk_in_out(np.ascontiguousarray(w.T))

    q_global = np.arange(S).reshape(NQT, 128)
    qb = q_global // BLK
    n_idx = np.arange(NBLK)
    gm = np.zeros((NQT, 128, NBLK), np.float32)
    gm[qb[:, :, None] == n_idx[None, None, :]] = BIG
    gm[qb[:, :, None] < n_idx[None, None, :]] = -BIG
    gmask = np.ascontiguousarray(gm.transpose(1, 0, 2).reshape(128, NQT * NBLK))

    i = np.arange(128)
    causal = np.where(i[:, None] >= i[None, :], 0.0, NEG).astype(np.float32)

    hm = np.ones((1, 1), np.float32)
    while hm.shape[0] < 128:
        hm = np.block([[hm, hm], [hm, -hm]])

    def fm(v):
        return np.ascontiguousarray(np.asarray(v, np.float32).reshape(-1, 128).T)

    return {
        "x": np.ascontiguousarray(xb),
        "wq": blk_in_out(wq), "wk": blk_in_out(wk), "wv": blk_in_out(wv),
        "wo": blk_in_out(wo),
        "wup": blk_out_in(wup), "wgt": blk_out_in(wgt), "wdn": blk_out_in(wdn),
        "su_ug": fm(su_ug), "su_dp": fm(su_dp),
        "wsu": fm(wscale_ug[:IN]), "wsg": fm(wscale_ug[IN:]),
        "wdp": fm(wscale_dp),
        "h128": np.ascontiguousarray(hm.astype(np.float32)),
        "ident": np.eye(128, dtype=np.float32),
        "causal": causal,
        "gmask": gmask,
    }


def kernel(L0_attn_hidden_states, L0_attn_Wq, L0_attn_Wk, L0_attn_Wv, L0_attn_Wo,
           L0_ffn_SU_ug, L0_ffn_SU_dp, L0_ffn_Wscale_ug, L0_ffn_Wscale_dp,
           L0_ffn_W_up, L0_ffn_W_gate, L0_ffn_W_down):
    x = np.asarray(L0_attn_hidden_states, np.float32)
    args = [np.asarray(a, np.float32) for a in
            (L0_attn_Wq, L0_attn_Wk, L0_attn_Wv, L0_attn_Wo, L0_ffn_SU_ug,
             L0_ffn_SU_dp, L0_ffn_Wscale_ug, L0_ffn_Wscale_dp, L0_ffn_W_up,
             L0_ffn_W_gate, L0_ffn_W_down)]
    nc = build_nc(dbg=False)
    in_maps = [_host_inputs(x[b], *args) for b in range(8)]
    res = run_bass_kernel_spmd(nc, in_maps, core_ids=list(range(8)))
    return np.stack([res.results[b]["out"] for b in range(8)], axis=0)


# version 14
# speedup vs baseline: 1.0440x; 1.0183x over previous
import numpy as np
import concourse.bass as bass
import concourse.mybir as mybir
from concourse.bass_utils import run_bass_kernel_spmd

F32 = mybir.dt.float32
BF16 = mybir.dt.bfloat16
AF = mybir.ActivationFunctionType
OP = mybir.AluOpType
AX = mybir.AxisListType

S = 4096
DM = 2048
NH = 16
HD = 128
BLK = 64
NBLK = S // BLK
IN = 8192
NCI = DM // 128
NCO_I = IN // 128
TA = 1024
TF = 512
NCA = S // TA
NCF = S // TF
NQT = S // 128
NQT_B = NQT

SM_SCALE = float(HD) ** -0.5
S1 = 1.0 / (np.sqrt(2048.0) * 64.0)
S2 = 1.0 / (np.sqrt(8192.0) * 64.0)
EPS = 1e-5
NEG = -1.0e9
BIG = 1.0e30

ENGS = ("sync", "gpsimd", "vector", "scalar", "tensor")


OWN_SEM = {"vector": "vv", "scalar": "sc"}


class Prog:
    def __init__(self):
        self.ops = {e: [] for e in ENGS}
        self.cnt = {}

    def op(self, eng, fname, *a, wait=(), inc=(), small=False, **k):
        own = OWN_SEM.get(eng)
        ws = list(wait)
        if own is not None and fname not in ("nop", "engine_nop", "drain", "dma_start"):
            ws.append((own, self.cnt.get(own, 0)))
            if not any(s == own for s, _ in inc):
                inc = tuple(inc) + ((own, 1),)
        incs = []
        for sem, d in inc:
            self.cnt[sem] = self.cnt.get(sem, 0) + d
            incs.append((sem, d))
        ws = [(s, v) for (s, v) in ws if v > 0]
        if small:
            self.ops[eng].append((fname, a, k, ws, []))
            self.ops[eng].append(("drain", (), {}, [], incs))
        else:
            self.ops[eng].append((fname, a, k, ws, incs))
        return (eng, len(self.ops[eng]) - 1)

    def add_wait(self, handle, sem, val):
        eng, i = handle
        fname, a, k, ws, incs = self.ops[eng][i]
        if val > 0:
            self.ops[eng][i] = (fname, a, k, ws + [(sem, val)], incs)

    POOL = {"sync": ["wS%d" % i for i in range(16)],
            "gpsimd": ["wG%d" % i for i in range(8)],
            "scalar": ["wA%d" % i for i in range(8)]}

    def dma(self, eng, out, in_, sem=None, wait=()):
        pool = self.POOL[eng]
        depth = len(pool) // 2
        hist = self.cnt.setdefault("_hist_" + eng, [])
        i = len(hist)
        s = pool[i % len(pool)]
        w = tuple(wait)
        if i >= depth:
            w = w + (hist[i - depth],)
        self.op(eng, "dma_start", wait=w, inc=((s, 16),), out=out, in_=in_)
        tok = (s, self.cnt[s])
        hist.append(tok)
        return tok

    def c(self, sem):
        return self.cnt.get(sem, 0)

    def all_sems(self):
        return (["tm", "vv", "sc"] + self.POOL["sync"] + self.POOL["gpsimd"]
                + self.POOL["scalar"])

    def barrier(self):
        snap = [(s, self.c(s)) for s in self.all_sems()]
        for e in ENGS:
            nopname = {"sync": "nop", "gpsimd": "engine_nop", "vector": "engine_nop",
                       "scalar": "nop", "tensor": "nop"}[e]
            self.op(e, nopname, wait=snap)

    def check_waits(self):
        bad = []
        for e in ENGS:
            for i, (fn, a, k, ws, incs) in enumerate(self.ops[e]):
                for s, v in ws:
                    if v > self.cnt.get(s, 0):
                        bad.append((e, i, fn, s, v, self.cnt.get(s, 0)))
        return bad

    def replay(self, engname, engobj, sems):
        for fname, a, k, waits, incs in self.ops[engname]:
            for s, v in waits:
                engobj.wait_ge(sems[s], v)
            instr = getattr(engobj, fname)(*a, **k)
            for s, d in incs:
                instr.then_inc(sems[s], d)


def build_nc(dbg=False, phases="AFBC"):
    nc = bass.Bass("TRN2", target_bir_lowering=False)

    x_d = nc.dram_tensor("x", [S, DM], F32, kind="ExternalInput").ap()
    wq_d = nc.dram_tensor("wq", [NCI, 128, DM], F32, kind="ExternalInput").ap()
    wk_d = nc.dram_tensor("wk", [NCI, 128, DM], F32, kind="ExternalInput").ap()
    wv_d = nc.dram_tensor("wv", [NCI, 128, DM], F32, kind="ExternalInput").ap()
    wo_d = nc.dram_tensor("wo", [NCI, 128, DM], F32, kind="ExternalInput").ap()
    wup_d = nc.dram_tensor("wup", [NCO_I, 128, DM], F32, kind="ExternalInput").ap()
    wgt_d = nc.dram_tensor("wgt", [NCO_I, 128, DM], F32, kind="ExternalInput").ap()
    wdn_d = nc.dram_tensor("wdn", [NCI, 128, IN], F32, kind="ExternalInput").ap()
    su_ug_d = nc.dram_tensor("su_ug", [128, NCI], F32, kind="ExternalInput").ap()
    su_dp_d = nc.dram_tensor("su_dp", [128, NCO_I], F32, kind="ExternalInput").ap()
    wsu_d = nc.dram_tensor("wsu", [128, NCO_I], F32, kind="ExternalInput").ap()
    wsg_d = nc.dram_tensor("wsg", [128, NCO_I], F32, kind="ExternalInput").ap()
    wdp_d = nc.dram_tensor("wdp", [128, NCI], F32, kind="ExternalInput").ap()
    h128_d = nc.dram_tensor("h128", [128, 128], F32, kind="ExternalInput").ap()
    ident_d = nc.dram_tensor("ident", [128, 128], F32, kind="ExternalInput").ap()
    causal_d = nc.dram_tensor("causal", [128, 128], F32, kind="ExternalInput").ap()
    gmask_d = nc.dram_tensor("gmask", [128, NQT * NBLK], F32, kind="ExternalInput").ap()

    out_d = nc.dram_tensor("out", [S, DM], F32, kind="ExternalOutput").ap()
    skind = "ExternalOutput" if dbg else "Internal"
    qt_d = nc.dram_tensor("qt_s", [DM, S], F32, kind=skind).ap()
    kt_d = nc.dram_tensor("kt_s", [DM, S], F32, kind=skind).ap()
    vt_d = nc.dram_tensor("vt_s", [DM, S], F32, kind=skind).ap()
    ot_d = nc.dram_tensor("ot_s", [DM, S], F32, kind=skind).ap()
    ft_d = nc.dram_tensor("ft_s", [DM, S], F32, kind=skind).ap()
    kg_d = nc.dram_tensor("kg_s", [128, NCI * NBLK], F32, kind=skind).ap()
    wsp = {}
    for nm, nco_, wid_ in (("wq", NCI, DM), ("wk", NCI, DM), ("wv", NCI, DM),
                           ("wo", NCI, DM), ("wup", NCO_I, DM), ("wgt", NCO_I, DM),
                           ("wdn", NCI, IN)):
        wsp[nm] = (nc.dram_tensor(nm + "_hi", [nco_, 128, wid_], BF16, kind="Internal").ap(),
                   nc.dram_tensor(nm + "_lo", [nco_, 128, wid_], BF16, kind="Internal").ap())
    ot_hi_d = nc.dram_tensor("ot_hi", [DM, S], BF16, kind="Internal").ap()
    ot_lo_d = nc.dram_tensor("ot_lo", [DM, S], BF16, kind="Internal").ap()
    qhi_d = nc.dram_tensor("q_hi", [DM, S], BF16, kind="Internal").ap()
    qlo_d = nc.dram_tensor("q_lo", [DM, S], BF16, kind="Internal").ap()
    khi_d = nc.dram_tensor("k_hi", [DM, S], BF16, kind="Internal").ap()
    klo_d = nc.dram_tensor("k_lo", [DM, S], BF16, kind="Internal").ap()
    xn_d = nc.dram_tensor("xn_s", [128, DM], F32, kind=skind).ap()
    st_d = nc.dram_tensor("st_s", [128, 64], F32, kind=skind).ap()
    xnt_d = nc.dram_tensor("xnt_s", [128, TA], F32, kind=skind).ap()

    p = Prog()

    import contextlib
    with contextlib.ExitStack() as stack:
        en = stack.enter_context
        sb_id = en(nc.sbuf_tensor("sb_ident", [128, 128], F32))
        sb_h = en(nc.sbuf_tensor("sb_h128", [128, 128], F32))
        sb_ca = en(nc.sbuf_tensor("sb_causal", [128, 128], F32))
        sb_suug = en(nc.sbuf_tensor("sb_su_ug", [128, NCI], F32))
        sb_sudp = en(nc.sbuf_tensor("sb_su_dp", [128, NCO_I], F32))
        sb_wsu = en(nc.sbuf_tensor("wsu64su", [128, NCO_I], F32))
        sb_wsg = en(nc.sbuf_tensor("wsg64", [128, NCO_I], F32))
        sb_wdp = en(nc.sbuf_tensor("wdp64", [128, NCI], F32))
        sb_kg = en(nc.sbuf_tensor("kgT", [128, NCI * NBLK], F32))
        sb_xng = en(nc.sbuf_tensor("xng", [128, 768], F32))
        sb_st = en(nc.sbuf_tensor("stats", [128, 64], F32))
        A_ACT = en(nc.sbuf_tensor("ACT", [128, 32768], F32))
        A_W = en(nc.sbuf_tensor("WST", [128, 4096], F32))
        A_Z = en(nc.sbuf_tensor("ZB", [128, 8192], F32))
        A_X = en(nc.sbuf_tensor("XB", [128, 4096], F32))
        A_S = en(nc.sbuf_tensor("STG", [128, 1536], F32))
        PB = [en(nc.psum_tensor(f"pb{i}", [128, 512], F32)) for i in range(6)]
        TRASH = en(nc.psum_tensor("trash", [128, 1024], F32))
        GATE_PS = TRASH[:, 0:512]
        sems = {s: en(nc.semaphore(s)) for s in p.all_sems()}
        block = en(nc.Block())
        bankv = {i: ("vv", 0) for i in range(8)}

        def stc(o, c, n=1):
            return sb_st[:, o * 24 + c: o * 24 + c + n]
        C_SUM, C_MEAN, C_VEPS, C_STD, C_RSTD = 0, 1, 2, 3, 4
        C_SQ0 = 8
        def stb(c, n=1, ofs=0):
            return sb_st[:, 48 + c + ofs: 48 + c + ofs + n]
        C_NMAX, C_SSUM, C_RS, C_T8 = 0, 2, 4, 8

        pre_toks = [
            p.dma("sync", sb_id[:], ident_d[:, :]),
            p.dma("sync", sb_h[:], h128_d[:, :]),
            p.dma("sync", sb_ca[:], causal_d[:, :]),
            p.dma("sync", sb_suug[:], su_ug_d[:, :]),
            p.dma("sync", sb_sudp[:], su_dp_d[:, :]),
            p.dma("sync", sb_wsu[:], wsu_d[:, :]),
            p.dma("sync", sb_wsg[:], wsg_d[:, :]),
            p.dma("sync", sb_wdp[:], wdp_d[:, :]),
        ]
        p.op("vector", "memset", ap=sb_st[:], constant=0.0, inc=(("vv", 1),))
        stats_init_vv = p.c("vv")
        p.op("vector", "tensor_scalar", wait=tuple(pre_toks), inc=(("vv", 1),),
             out=sb_wsu[:], in0=sb_wsu[:], scalar1=64.0, scalar2=None, op0=OP.mult)
        p.op("vector", "tensor_tensor", inc=(("vv", 1),),
             out=sb_wsu[:], in0=sb_wsu[:], in1=sb_sudp[:], op=OP.mult)
        p.op("vector", "tensor_scalar", inc=(("vv", 1),),
             out=sb_wsg[:], in0=sb_wsg[:], scalar1=64.0, scalar2=None, op0=OP.mult)
        p.op("vector", "tensor_scalar", inc=(("vv", 1),), small=True,
             out=sb_wdp[:], in0=sb_wdp[:], scalar1=64.0, scalar2=None, op0=OP.mult)

        ln_prev_vv = [stats_init_vv, stats_init_vv]

        def layernorm(xa, wx, par):
            p.op("scalar", "activation",
                 wait=(wx, ("vv", ln_prev_vv[par]),),
                 inc=(("sc", 1),),
                 out=TRASH[:, :], in_=xa[:, 0:1024], func=AF.Square,
                 accum_out=stc(par, C_SQ0))
            p.op("scalar", "activation", inc=(("sc", 1),),
                 out=TRASH[:, :], in_=xa[:, 1024:2048], func=AF.Square,
                 accum_out=stc(par, C_SQ0 + 1))
            p.op("scalar", "activation", inc=(("sc", 1),),
                 out=TRASH[:, :], in_=xa[:, 0:1024], func=AF.Copy,
                 accum_out=stc(par, C_SUM))
            p.op("scalar", "activation", inc=(("sc", 1),), small=True,
                 out=TRASH[:, :], in_=xa[:, 1024:2048], func=AF.Copy,
                 accum_out=stc(par, C_SQ0 + 2))
            sq_done = p.c("sc")
            p.op("vector", "tensor_tensor", wait=(("sc", sq_done),),
                 inc=(("vv", 1),), small=True,
                 out=stc(par, C_SUM), in0=stc(par, C_SUM),
                 in1=stc(par, C_SQ0 + 2), op=OP.add)
            p.op("vector", "tensor_tensor", inc=(("vv", 1),), small=True,
                 out=stc(par, C_SQ0), in0=stc(par, C_SQ0),
                 in1=stc(par, C_SQ0 + 1), op=OP.add)
            p.op("vector", "tensor_scalar", inc=(("vv", 1),), small=True,
                 out=stc(par, C_MEAN), in0=stc(par, C_SUM), scalar1=1.0 / DM,
                 scalar2=None, op0=OP.mult)
            p.op("vector", "tensor_scalar", inc=(("vv", 1),), small=True,
                 out=stc(par, C_SQ0), in0=stc(par, C_SQ0), scalar1=1.0 / DM,
                 scalar2=EPS, op0=OP.mult, op1=OP.add)
            p.op("vector", "tensor_scalar", inc=(("vv", 1),), small=True,
                 out=stc(par, C_VEPS), in0=stc(par, C_MEAN),
                 scalar1=stc(par, C_MEAN), scalar2=-1.0, op0=OP.mult, op1=OP.mult)
            p.op("vector", "tensor_tensor", inc=(("vv", 1),), small=True,
                 out=stc(par, C_VEPS), in0=stc(par, C_SQ0), in1=stc(par, C_VEPS),
                 op=OP.add)
            veps_done = p.c("vv")
            p.op("scalar", "activation", wait=(("vv", veps_done),), inc=(("sc", 1),),
                 small=True, out=stc(par, C_STD), in_=stc(par, C_VEPS), func=AF.Sqrt)
            p.op("vector", "reciprocal", wait=(("sc", p.c("sc")),), inc=(("vv", 1),),
                 small=True, out=stc(par, C_RSTD), in_=stc(par, C_STD))
            p.op("vector", "tensor_scalar", inc=(("vv", 1),),
                 out=xa[:], in0=xa[:], scalar1=stc(par, C_MEAN),
                 scalar2=stc(par, C_RSTD), op0=OP.subtract, op1=OP.mult)
            ln_prev_vv[par] = p.c("vv")
            return p.c("vv")


        def transpose_tiles(srcs, dests, ln_done, banks, sus=None, scale=None,
                            group_dests=None, group_dests_lo=None, extra_wait=(),
                            ready=None):
            n = len(srcs)
            for g in range((n + 3) // 4):
                b = banks[g % len(banks)]
                n_in_g = min(4, n - g * 4)
                for j in range(n_in_g):
                    i = g * 4 + j
                    w = ()
                    if j == 0:
                        r = ready if ready is not None else ("vv", ln_done)
                        w = (r, (bankv[b][0], bankv[b][1])) + tuple(extra_wait)
                    p.op("tensor", "transpose", wait=w,
                         inc=((("tm", 1),) if j == n_in_g - 1 else ()),
                         out=PB[b][:, j * 128:(j + 1) * 128], in_=srcs[i],
                         identity=sb_id[:])
                tmc = p.c("tm")
                if group_dests is not None:
                    p.op("vector", "tensor_copy", wait=(("tm", tmc),),
                         inc=(("vv", 1),),
                         out=group_dests[g], in_=PB[b][:, 0:n_in_g * 128])
                    if group_dests_lo is not None:
                        gsh = group_dests[g].shape
                        p.op("vector", "tensor_tensor", inc=(("vv", 1),),
                             out=group_dests_lo[g],
                             in0=PB[b][:, 0:n_in_g * 128].rearrange(
                                 "p (a b) -> p a b", b=gsh[-1]),
                             in1=group_dests[g], op=OP.subtract)
                else:
                    for j in range(n_in_g):
                        i = g * 4 + j
                        if sus is None and scale is None:
                            p.op("vector", "tensor_copy",
                                 wait=(("tm", tmc),) if j == 0 else (), inc=(("vv", 1),),
                                 out=dests[i], in_=PB[b][:, j * 128:(j + 1) * 128])
                        else:
                            s1 = sus[i] if sus is not None else 1.0
                            p.op("vector", "tensor_scalar",
                                 wait=(("tm", tmc),) if j == 0 else (), inc=(("vv", 1),),
                                 out=dests[i], in0=PB[b][:, j * 128:(j + 1) * 128],
                                 scalar1=s1, scalar2=scale,
                                 op0=OP.mult, op1=(OP.bypass if scale is None else OP.mult))
                bankv[b] = ("vv", p.c("vv"))
            return p.c("vv")

        NSTG = 3
        stg_slots = [A_S[:, i * 512:(i + 1) * 512] for i in range(NSTG)]
        stg_free = [()] * NSTG
        stg_i = [0]

        def stage_out(bank, cols, dram_slice, mm_wait, scale_col=None):
            si = stg_i[0] % NSTG
            stg_i[0] += 1
            slot = stg_slots[si][:, 0:cols]
            if scale_col is None:
                p.op("vector", "tensor_copy",
                     wait=(("tm", mm_wait),) + tuple(stg_free[si]),
                     inc=(("vv", 1),), out=slot, in_=PB[bank][:, 0:cols])
            else:
                p.op("vector", "tensor_scalar",
                     wait=(("tm", mm_wait),) + tuple(stg_free[si]),
                     inc=(("vv", 1),), out=slot, in0=PB[bank][:, 0:cols],
                     scalar1=scale_col, scalar2=None, op0=OP.mult)
            ev = p.c("vv")
            tok = p.dma("gpsimd", dram_slice, slot, wait=(("vv", ev),))
            stg_free[si] = (tok,)
            bankv[bank] = ("vv", ev)
            return ev

        def stage_out_split(bank, hi_slice, lo_slice, mm_wait):
            si = stg_i[0] % NSTG
            stg_i[0] += 1
            s_hi = stg_slots[si][:, 0:256].bitcast(BF16)
            s_lo = stg_slots[si][:, 256:512].bitcast(BF16)
            p.op("vector", "tensor_copy",
                 wait=(("tm", mm_wait),) + tuple(stg_free[si]),
                 inc=(("vv", 1),), out=s_hi, in_=PB[bank][:])
            p.op("vector", "tensor_tensor", inc=(("vv", 1),),
                 out=s_lo, in0=PB[bank][:], in1=s_hi, op=OP.subtract)
            ev = p.c("vv")
            t1 = p.dma("gpsimd", hi_slice, s_hi, wait=(("vv", ev),))
            t2 = p.dma("gpsimd", lo_slice, s_lo, wait=(("vv", ev),))
            stg_free[si] = (t1, t2)
            bankv[bank] = ("vv", ev)
            return ev

        w_slots = [A_W[:, i * 2048:(i + 1) * 2048] for i in range(2)]
        w_hi = [A_W[:, i * 2048: i * 2048 + 1024].bitcast(BF16) for i in range(2)]
        w_lo = [A_W[:, i * 2048 + 1024:(i + 1) * 2048].bitcast(BF16) for i in range(2)]
        w_free = [("tm", 0)] * 2
        w_i = [0]

        def w_load(dram_slice, extra_wait=()):
            si = w_i[0] % 2
            w_i[0] += 1
            tok = p.dma("sync", w_slots[si], dram_slice,
                        wait=((w_free[si][0], w_free[si][1]),) + tuple(extra_wait))
            return si, tok

        def w_load_pair(whi_slice, wlo_slice, extra_wait=()):
            si = w_i[0] % 2
            w_i[0] += 1
            t1 = p.dma("sync", w_hi[si], whi_slice,
                       wait=((w_free[si][0], w_free[si][1]),) + tuple(extra_wait))
            t2 = p.dma("sync", w_lo[si], wlo_slice)
            return si, t1, t2

        def mm3(bank_ap, si, rhs_hi, rhs_lo, nci, first_wait, width=128):
            first = True
            for wsel, xsel in ((w_hi[si], rhs_hi), (w_hi[si], rhs_lo),
                               (w_lo[si], rhs_hi)):
                for ci in range(nci):
                    w = first_wait if first else ()
                    last = (wsel is w_lo[si]) and ci == nci - 1
                    p.op("tensor", "matmul", wait=w,
                         inc=((("tm", 1),) if last else ()),
                         out=bank_ap,
                         lhsT=wsel[:, ci * width:(ci + 1) * width],
                         rhs=xsel(ci),
                         start=first, stop=last)
                    first = False
            return p.c("tm")

        xa_slots = [A_X[:, 0:2048], A_X[:, 2048:4096]]
        xa_free = [("vv", 0), ("vv", 0)]

        land = [A_Z[:, i * 2048:(i + 1) * 2048] for i in range(2)]
        phi = [A_Z[:, 4096 + i * 2048: 4096 + i * 2048 + 1024].bitcast(BF16)
               for i in range(2)]
        plo = [A_Z[:, 4096 + i * 2048 + 1024: 4096 + (i + 1) * 2048].bitcast(BF16)
               for i in range(2)]
        land_free = [("vv", 0), ("vv", 0)]
        pair_free = [[], []]
        ji = [0]
        prol_done = {}

        def split_chunk(fp32_slice, hi_slice, lo_slice):
            sl = ji[0] % 2
            ji[0] += 1
            t_in = p.dma("scalar", land[sl], fp32_slice,
                         wait=((land_free[sl][0], land_free[sl][1]),))
            p.op("vector", "tensor_copy", wait=(t_in,) + tuple(pair_free[sl]),
                 inc=(("vv", 1),), out=phi[sl], in_=land[sl])
            p.op("vector", "tensor_tensor", inc=(("vv", 1),),
                 out=plo[sl], in0=land[sl], in1=phi[sl], op=OP.subtract)
            land_free[sl] = ("vv", p.c("vv"))
            ev = p.c("vv")
            t1 = p.dma("scalar", hi_slice, phi[sl], wait=(("vv", ev),))
            t2 = p.dma("scalar", lo_slice, plo[sl], wait=(("vv", ev),))
            pair_free[sl] = [t1, t2]
            return (t1, t2)

        for nm, dram, nco_, wid_ in (("wq", wq_d, NCI, DM), ("wk", wk_d, NCI, DM),
                                     ("wv", wv_d, NCI, DM), ("wo", wo_d, NCI, DM),
                                     ("wup", wup_d, NCO_I, DM),
                                     ("wgt", wgt_d, NCO_I, DM),
                                     ("wdn", wdn_d, NCI, IN)):
            whi_t, wlo_t = wsp[nm]
            toks = None
            for co in range(nco_):
                for sub in range(wid_ // 2048):
                    sl_ = slice(sub * 2048, (sub + 1) * 2048)
                    toks = split_chunk(dram[co, :, sl_], whi_t[co, :, sl_],
                                       wlo_t[co, :, sl_])
            prol_done[nm] = tuple((s_, p.c(s_)) for s_ in
                                  {toks[0][0], toks[1][0]})
        prol_all = tuple((s_, p.c(s_)) for s_ in p.POOL["scalar"])

        if "A" in phases:
            xnT = A_ACT
            xnT_hi = A_ACT[:, 0:8192].bitcast(BF16)
            xnT_lo = A_ACT[:, 8192:16384].bitcast(BF16)
            xnT_free = ("tm", 0)
            for tc in range(NCA):
                t0 = tc * TA
                for s in range(8):
                    sl = s % 2
                    xa = xa_slots[sl]
                    wx = p.dma("sync", xa, x_d[t0 + s * 128: t0 + s * 128 + 128, :],
                               wait=((xa_free[sl][0], xa_free[sl][1]),))
                    ln_done = layernorm(xa, wx, s % 2)
                    srcs = [xa[:, ec * 128:(ec + 1) * 128] for ec in range(NCI)]
                    ghi = [xnT_hi.rearrange("p (c t) -> p c t", t=TA)
                           [:, g * 4:g * 4 + 4, s * 128:s * 128 + 128]
                           for g in range(4)]
                    glo = [xnT_lo.rearrange("p (c t) -> p c t", t=TA)
                           [:, g * 4:g * 4 + 4, s * 128:s * 128 + 128]
                           for g in range(4)]
                    if dbg and tc == 0 and s == 0:
                        p.dma("gpsimd", xn_d[:, :], xa[:], wait=(("vv", ln_done),))
                        p.dma("gpsimd", st_d[:, :], sb_st[:, 0:64], wait=(("vv", ln_done),))
                        p.barrier()
                    td = transpose_tiles(srcs, None, ln_done, (0, 1),
                                         group_dests=ghi, group_dests_lo=glo,
                                         extra_wait=((xnT_free[0], xnT_free[1]),)
                                         if s == 0 else ())
                    xa_free[sl] = ("vv", td)
                if dbg and tc == 0:
                    p.dma("gpsimd", xnt_d[:, :], xnT[:, 0:TA],
                          wait=(("vv", p.c("vv")),))
                xnT_done = p.c("vv")
                for ci in range(NCI):
                    p.op("vector", "reduce_sum", wait=(("vv", xnT_done),),
                         inc=(("vv", 1),),
                         out=sb_xng[:, ci * 16:(ci + 1) * 16].unsqueeze(2),
                         in_=xnT_hi[:, ci * TA:(ci + 1) * TA].rearrange(
                             "p (b t) -> p b t", t=BLK),
                         axis=AX.X)
                    p.op("vector", "reduce_sum",
                         inc=(("vv", 1),), small=(ci == NCI - 1),
                         out=sb_xng[:, 256 + ci * 16: 256 + ci * 16 + 16].unsqueeze(2),
                         in_=xnT_lo[:, ci * TA:(ci + 1) * TA].rearrange(
                             "p (b t) -> p b t", t=BLK),
                         axis=AX.X)
                p.op("vector", "tensor_tensor", inc=(("vv", 1),),
                     out=sb_xng[:, 0:256], in0=sb_xng[:, 0:256],
                     in1=sb_xng[:, 256:512], op=OP.add)
                p.op("vector", "tensor_scalar", inc=(("vv", 1),),
                     out=sb_xng[:, 0:256], in0=sb_xng[:, 0:256],
                     scalar1=1.0 / BLK, scalar2=None, op0=OP.mult)
                xng_hi = sb_xng[:, 512:640].bitcast(BF16)
                xng_lo = sb_xng[:, 640:768].bitcast(BF16)
                p.op("vector", "tensor_copy", inc=(("vv", 1),),
                     out=xng_hi, in_=sb_xng[:, 0:256])
                p.op("vector", "tensor_tensor", inc=(("vv", 1),), small=True,
                     out=xng_lo, in0=sb_xng[:, 0:256], in1=xng_hi, op=OP.subtract)
                xng_done = p.c("vv")

                for wnm, spill, iskg in (("wq", (qhi_d, qlo_d), False),
                                         ("wk", (khi_d, klo_d), True),
                                         ("wv", vt_d, False)):
                    whi_t, wlo_t = wsp[wnm]
                    for co in range(NCI):
                        si, wd1, wd2 = w_load_pair(whi_t[co, :, :], wlo_t[co, :, :],
                                                   extra_wait=prol_done[wnm]
                                                   if tc == 0 and co == 0 else ())
                        for th in range(2):
                            b = 2 + th
                            fw = (wd1, wd2, ("vv", xnT_done),
                                  (bankv[b][0], bankv[b][1]))
                            mm3(PB[b][:], si,
                                lambda ci, th=th: xnT_hi[:, ci * TA + th * 512:
                                                         ci * TA + th * 512 + 512],
                                lambda ci, th=th: xnT_lo[:, ci * TA + th * 512:
                                                         ci * TA + th * 512 + 512],
                                NCI, fw)
                            if wnm == "wv":
                                stage_out(b, 512,
                                          spill[co * 128:(co + 1) * 128,
                                                t0 + th * 512: t0 + th * 512 + 512],
                                          p.c("tm"))
                            else:
                                stage_out_split(
                                    b,
                                    spill[0][co * 128:(co + 1) * 128,
                                             t0 + th * 512: t0 + th * 512 + 512],
                                    spill[1][co * 128:(co + 1) * 128,
                                             t0 + th * 512: t0 + th * 512 + 512],
                                    p.c("tm"))
                        if iskg:
                            b = 4
                            fw = (("vv", xng_done), (bankv[b][0], bankv[b][1]))
                            mm3(PB[b][:, 0:16], si,
                                lambda ci: xng_hi[:, ci * 16:(ci + 1) * 16],
                                lambda ci: xng_lo[:, ci * 16:(ci + 1) * 16],
                                NCI, fw)
                            p.op("vector", "tensor_copy", wait=(("tm", p.c("tm")),),
                                 inc=(("vv", 1),), small=True,
                                 out=sb_kg[:, co * NBLK + tc * 16:
                                           co * NBLK + tc * 16 + 16],
                                 in_=PB[b][:, 0:16])
                            bankv[b] = ("vv", p.c("vv"))
                        w_free[si] = ("tm", p.c("tm"))
                xnT_free = ("tm", p.c("tm"))
            if dbg:
                p.dma("gpsimd", kg_d[:, :], sb_kg[:], wait=(("vv", p.c("vv")),))
            p.barrier()

        if "F" in phases:
            ACT = A_ACT
            Z1 = A_Z
            actb = A_ACT[:, :].bitcast(BF16)
            z1b = A_Z[:, :].bitcast(BF16)
            z1hi = lambda ci: z1b[:, ci * 1024: ci * 1024 + 512]
            z1lo = lambda ci: z1b[:, ci * 1024 + 512: ci * 1024 + 1024]
            act_free = ("tm", 0)
            z1_free = ("tm", 0)
            z1_prol = prol_all

            def f_front_pieces(tc):
                t0 = tc * TF

                def mk_ln_tr(s):
                    def go():
                        sl = s % 2
                        xa = xa_slots[sl]
                        wx = p.dma("sync", xa,
                                   x_d[t0 + s * 128: t0 + s * 128 + 128, :],
                                   wait=((xa_free[sl][0], xa_free[sl][1]),))
                        ln_done = layernorm(xa, wx, s % 2)
                        srcs = [xa[:, ec * 128:(ec + 1) * 128] for ec in range(NCI)]
                        dsts = [Z1[:, ec * TF + s * 128: ec * TF + s * 128 + 128]
                                for ec in range(NCI)]
                        sus = [sb_suug[:, ec:ec + 1] for ec in range(NCI)]
                        ew = ((z1_free[0], z1_free[1]),)
                        if tc == 0:
                            ew = ew + z1_prol
                        td = transpose_tiles(srcs, dsts, ln_done, (0, 1), sus=sus,
                                             extra_wait=ew if s == 0 else ())
                        xa_free[sl] = ("vv", td)
                    return go

                def h16():
                    hh = 1
                    while hh < NCI:
                        for base in range(0, NCI, 2 * hh):
                            for j in range(hh):
                                a = (base + j) * TF
                                bcol = (base + j + hh) * TF
                                p.op("vector", "tensor_tensor", inc=(("vv", 1),),
                                     out=Z1[:, a:a + TF], in0=Z1[:, a:a + TF],
                                     in1=Z1[:, bcol:bcol + TF], op=OP.add)
                                p.op("vector", "scalar_tensor_tensor",
                                     inc=(("vv", 1),),
                                     out=Z1[:, bcol:bcol + TF],
                                     in0=Z1[:, bcol:bcol + TF],
                                     scalar=-2.0, in1=Z1[:, a:a + TF],
                                     op0=OP.mult, op1=OP.add)
                        hh *= 2

                def h128_1():
                    bf1_done = p.c("vv")
                    for ci in range(NCI):
                        b = 2 + ci % 2
                        p.op("tensor", "matmul",
                             wait=(("vv", bf1_done), (bankv[b][0], bankv[b][1]),),
                             inc=(("tm", 1),),
                             out=PB[b][:], lhsT=sb_h[:],
                             rhs=Z1[:, ci * TF:(ci + 1) * TF],
                             start=True, stop=True)
                        zh = z1b[:, ci * 1024: ci * 1024 + 512]
                        zl = z1b[:, ci * 1024 + 512: ci * 1024 + 1024]
                        p.op("vector", "tensor_scalar", wait=(("tm", p.c("tm")),),
                             inc=(("vv", 1),),
                             out=zh, in0=PB[b][:], scalar1=S1,
                             scalar2=None, op0=OP.mult)
                        p.op("vector", "scalar_tensor_tensor", inc=(("vv", 1),),
                             out=zl, in0=PB[b][:], scalar=S1, in1=zh,
                             op0=OP.mult, op1=OP.subtract)
                        bankv[b] = ("vv", p.c("vv"))
                return [mk_ln_tr(0), mk_ln_tr(1), mk_ln_tr(2), mk_ln_tr(3),
                        h16, h128_1]

            def f_upgate_h64(tc):
                nonlocal act_free

                def bfly(a, bcol):
                    p.op("vector", "tensor_tensor", inc=(("vv", 1),),
                         out=ACT[:, a * TF:a * TF + TF], in0=ACT[:, a * TF:a * TF + TF],
                         in1=ACT[:, bcol * TF:bcol * TF + TF], op=OP.add)
                    p.op("vector", "scalar_tensor_tensor", inc=(("vv", 1),),
                         out=ACT[:, bcol * TF:bcol * TF + TF],
                         in0=ACT[:, bcol * TF:bcol * TF + TF],
                         scalar=-2.0, in1=ACT[:, a * TF:a * TF + TF],
                         op0=OP.mult, op1=OP.add)

                z1_done = p.c("vv")
                for co in range(NCO_I):
                    first_use = tc == 0 and co == 0
                    bu = 4 if co % 2 == 0 else 2
                    bg = 5 if co % 2 == 0 else 3
                    whi_t, wlo_t = wsp["wup"]
                    siu, wu1, wu2 = w_load_pair(whi_t[co, :, :], wlo_t[co, :, :],
                                                extra_wait=prol_done["wup"]
                                                if first_use else ())
                    whi_t, wlo_t = wsp["wgt"]
                    sig, wg1, wg2 = w_load_pair(whi_t[co, :, :], wlo_t[co, :, :],
                                                extra_wait=prol_done["wgt"]
                                                if first_use else ())
                    fw = (wu1, wu2, ("vv", z1_done), (bankv[bu][0], bankv[bu][1]))
                    mm_u = mm3(PB[bu][:], siu, z1hi, z1lo, NCI, fw)
                    fw = (wg1, wg2, (bankv[bg][0], bankv[bg][1]))
                    if co == 0:
                        fw = fw + ((act_free[0], act_free[1]),)
                    mm_g = mm3(PB[bg][:], sig, z1hi, z1lo, NCI, fw)
                    w_free[siu] = ("tm", mm_u)
                    w_free[sig] = ("tm", mm_g)
                    si = stg_i[0] % NSTG
                    stg_i[0] += 1
                    p.op("vector", "tensor_scalar",
                         wait=(("tm", mm_u),) + tuple(stg_free[si]),
                         inc=(("vv", 1),),
                         out=stg_slots[si], in0=PB[bu][:],
                         scalar1=sb_wsu[:, co:co + 1], scalar2=None, op0=OP.mult)
                    bankv[bu] = ("vv", p.c("vv"))
                    p.op("scalar", "activation",
                         wait=(("tm", mm_g),) + (((act_free[0], act_free[1]),)
                                                 if co == 0 else ()),
                         inc=(("sc", 1),),
                         out=ACT[:, co * TF:(co + 1) * TF], in_=PB[bg][:],
                         func=AF.Silu, scale=sb_wsg[:, co:co + 1])
                    sg_done = p.c("sc")
                    bankv[bg] = ("sc", sg_done)
                    p.op("vector", "tensor_tensor", wait=(("sc", sg_done),),
                         inc=(("vv", 1),),
                         out=ACT[:, co * TF:(co + 1) * TF], in0=stg_slots[si],
                         in1=ACT[:, co * TF:(co + 1) * TF], op=OP.mult)
                    stg_free[si] = (("vv", p.c("vv")),)
                    for s in range(5):
                        blk_sz = 2 << s
                        if (co + 1) % blk_sz == 0:
                            base = co + 1 - blk_sz
                            for j in range(1 << s):
                                bfly(base + j, base + j + (1 << s))
                return bfly

            def f_h128_2(tc, bfly):
                qvv = [0] * 4

                def h128(co):
                    b = 2 + co % 2
                    p.op("tensor", "matmul",
                         wait=(("vv", p.c("vv")), (bankv[b][0], bankv[b][1]),),
                         inc=(("tm", 1),),
                         out=PB[b][:], lhsT=sb_h[:],
                         rhs=ACT[:, co * TF:(co + 1) * TF],
                         start=True, stop=True)
                    ahi = actb[:, co * 1024: co * 1024 + 512]
                    alo = actb[:, co * 1024 + 512: co * 1024 + 1024]
                    p.op("vector", "tensor_scalar", wait=(("tm", p.c("tm")),),
                         inc=(("vv", 1),),
                         out=ahi, in0=PB[b][:], scalar1=S2, scalar2=None,
                         op0=OP.mult)
                    p.op("vector", "scalar_tensor_tensor", inc=(("vv", 1),),
                         out=alo, in0=PB[b][:], scalar=S2, in1=ahi,
                         op0=OP.mult, op1=OP.subtract)
                    bankv[b] = ("vv", p.c("vv"))

                for j in range(32):
                    bfly(j, j + 32)
                    h128(j)
                    if j % 16 == 15:
                        qvv[j // 16] = p.c("vv")
                for j in range(32):
                    h128(j + 32)
                    if j % 16 == 15:
                        qvv[2 + j // 16] = p.c("vv")
                return qvv

            def f_down(tc, qvv, pieces):
                t0 = tc * TF
                whi_t, wlo_t = wsp["wdn"]
                for oc in range(NCI):
                    b = 4 + oc % 2
                    T0 = p.c("tm")
                    qinfo = []
                    for q in range(4):
                        ew = (("tm", T0 + q - 1),) if q >= 2 else ()
                        if tc == 0 and oc == 0 and q < 2:
                            ew = ew + prol_done["wdn"]
                        sq, w1, w2 = w_load_pair(
                            whi_t[oc, :, q * 2048:(q + 1) * 2048],
                            wlo_t[oc, :, q * 2048:(q + 1) * 2048], extra_wait=ew)
                        qinfo.append((sq, w1, w2))
                    first = True
                    for q in range(4):
                        sq, w1, w2 = qinfo[q]
                        for wsel, xoff in ((w_hi[sq], 0), (w_hi[sq], 512),
                                           (w_lo[sq], 0)):
                            for c16 in range(16):
                                ci = q * 16 + c16
                                w = ()
                                if c16 == 0 and xoff == 0:
                                    w = (w1, w2, ("vv", qvv[q]))
                                    if first:
                                        w = w + ((bankv[b][0], bankv[b][1]),)
                                last = (q == 3 and wsel is w_lo[sq]
                                        and c16 == 15)
                                p.op("tensor", "matmul", wait=w,
                                     inc=((("tm", 1),)
                                          if (wsel is w_lo[sq] and c16 == 15)
                                          else ()),
                                     out=PB[b][:],
                                     lhsT=wsel[:, c16 * 128: c16 * 128 + 128],
                                     rhs=actb[:, ci * 1024 + xoff:
                                              ci * 1024 + xoff + 512],
                                     start=first, stop=last)
                                first = False
                    mm_d = p.c("tm")
                    for sq, _, _ in qinfo:
                        w_free[sq] = ("tm", mm_d)
                    stage_out(b, 512, ft_d[oc * 128:(oc + 1) * 128, t0:t0 + TF],
                              mm_d, scale_col=sb_wdp[:, oc:oc + 1])
                    if pieces and oc % 2 == 1 and oc // 2 < len(pieces):
                        pieces[oc // 2]()

            pieces = f_front_pieces(0)
            for piece in pieces:
                piece()
            for tc in range(NCF):
                bfly_fn = f_upgate_h64(tc)
                z1_free = ("tm", p.c("tm"))
                qvv = f_h128_2(tc, bfly_fn)
                nxt = f_front_pieces(tc + 1) if tc + 1 < NCF else []
                f_down(tc, qvv, nxt)
                act_free = ("tm", p.c("tm"))
            p.barrier()

        if "B" in phases:
            KB = [A_ACT[:, 0:4096], A_ACT[:, 4096:8192]]
            VTB = A_ACT[:, 8192:12288]
            VB = A_ACT[:, 16384:20480]
            SCRP = [A_ACT[:, 12288:16384], A_ACT[:, 20480:24576]]
            PT = A_ACT[:, 24576:28672]
            QH = [A_ACT[:, 28672:29696], A_ACT[:, 29696:30720]]
            GM8 = [A_ACT[:, 30720:31232], A_ACT[:, 31232:31744]]
            GT8 = A_ACT[:, 31744:32256]
            BIA8 = A_ACT[:, 32256:32768]
            OS = [A_S[:, 0:128], A_S[:, 128:256]]
            KGH = A_S[:, 256:768].bitcast(BF16)
            KGL = A_S[:, 768:1280].bitcast(BF16)
            T88 = A_S[:, 1280:1344]

            p.op("vector", "tensor_copy", wait=(("vv", p.c("vv")),),
                 inc=(("vv", 1),), out=KGH, in_=sb_kg[:])
            p.op("vector", "tensor_tensor", inc=(("vv", 1),), small=True,
                 out=KGL, in0=sb_kg[:], in1=KGH, op=OP.subtract)
            kg_split = p.c("vv")

            kb_free = [("tm", 0), ("tm", 0)]
            qh_free = [("tm", 0), ("tm", 0)]
            gm_free = [("vv", 0), ("vv", 0)]
            os_free = [(), ()]
            vtb_free = ("tm", 0)
            vb_free = ("tm", 0)
            scr_free = [("tm", 0), ("tm", 0)]
            pt_free = ("tm", 0)
            SBK = (2, 3, 4)
            TBK = (0, 1)
            NG = NQT // 8

            def emit_gate_group(h, g):
                gp = g % 2
                tq1 = p.dma("sync", QH[gp][:, 0:512].bitcast(BF16),
                            qhi_d[h * 128:(h + 1) * 128,
                                  g * 1024:(g + 1) * 1024],
                            wait=((qh_free[gp][0], qh_free[gp][1]),))
                tq2 = p.dma("sync", QH[gp][:, 512:1024].bitcast(BF16),
                            qlo_d[h * 128:(h + 1) * 128,
                                  g * 1024:(g + 1) * 1024])
                tgm = p.dma("sync", GM8[gp],
                            gmask_d[:, (h * 0 + g * 8) * 64:(g * 8 + 8) * 64],
                            wait=((gm_free[gp][0], gm_free[gp][1]),))
                qhi8 = QH[gp][:, 0:512].bitcast(BF16)
                qlo8 = QH[gp][:, 512:1024].bitcast(BF16)
                first = True
                for j in range(8):
                    for lh, rh in ((qhi8, KGH), (qhi8, KGL), (qlo8, KGH)):
                        w = ()
                        if first:
                            w = (tq1, tq2, ("vv", kg_split),
                                 (bankv[6][0], bankv[6][1]))
                            first = False
                        p.op("tensor", "matmul", wait=w,
                             inc=((("tm", 1),) if (j == 7 and lh is qlo8)
                                  else ()),
                             out=GATE_PS[:, j * 64:(j + 1) * 64],
                             lhsT=lh[:, j * 128:(j + 1) * 128],
                             rhs=rh[:, h * 64:(h + 1) * 64],
                             start=(lh is qhi8 and rh is KGH),
                             stop=(lh is qlo8))
                gmm = p.c("tm")
                p.op("vector", "scalar_tensor_tensor",
                     wait=(("tm", gmm), tgm), inc=(("vv", 1),),
                     out=GT8[:, :], in0=GATE_PS[:, 0:512], scalar=1.0,
                     in1=GM8[gp], op0=OP.mult, op1=OP.add)
                bankv[6] = ("vv", p.c("vv"))
                for j in range(8):
                    p.op("vector", "max", inc=(("vv", 1),), small=(j == 7),
                         out=T88[:, j * 8: j * 8 + 8],
                         in_=GT8[:, j * 64:(j + 1) * 64])
                for j in range(8):
                    p.op("vector", "tensor_scalar", inc=(("vv", 1),),
                         small=(j == 7),
                         out=GM8[gp][:, j * 64:(j + 1) * 64],
                         in0=GT8[:, j * 64:(j + 1) * 64],
                         scalar1=T88[:, j * 8 + 7: j * 8 + 8],
                         scalar2=NEG, op0=OP.is_lt, op1=OP.mult)
                return p.c("vv"), (tq1, tq2), gp

            def emit_tail_tr(h, qt, p_ready):
                nonlocal pt_free
                njt = qt + 1
                par = qt % 2
                srcs = [SCRP[par][:, k * 128:(k + 1) * 128] for k in range(njt)]
                gdsts = [PT[:, g2 * 512: g2 * 512 + min(4, njt - g2 * 4) * 128]
                         for g2 in range((njt + 3) // 4)]
                pt_done = transpose_tiles(srcs, None, 0, TBK, group_dests=gdsts,
                                          extra_wait=(("tm", pt_free[1]),),
                                          ready=p_ready)
                return pt_done

            def emit_tail_pv(h, qt, pt_done):
                nonlocal pt_free
                njt = qt + 1
                par = qt % 2
                bo = 5 if qt % 2 == 0 else 6
                out_ap = PB[5][:, 0:128] if bo == 5 else GATE_PS[:, 256:384]
                for j in range(njt):
                    w = ()
                    if j == 0:
                        w = (("vv", pt_done), ("vv", v_done[0]),
                             (bankv[bo][0], bankv[bo][1]))
                    p.op("tensor", "matmul", wait=w,
                         inc=((("tm", 1),) if j == njt - 1 else ()),
                         out=out_ap, lhsT=VB[:, j * 128:(j + 1) * 128],
                         rhs=PT[:, j * 128:(j + 1) * 128],
                         start=(j == 0), stop=(j == njt - 1))
                pv_mm = p.c("tm")
                pt_free = ("tm", pv_mm)
                scr_free[par] = ("tm", pv_mm)
                oslot = qt % 2
                os_hi = OS[oslot][:, 0:64].bitcast(BF16)
                os_lo = OS[oslot][:, 64:128].bitcast(BF16)
                p.op("vector", "tensor_copy",
                     wait=(("tm", pv_mm),) + tuple(os_free[oslot]),
                     inc=(("vv", 1),),
                     out=os_hi, in_=out_ap)
                p.op("vector", "tensor_tensor", inc=(("vv", 1),),
                     out=os_lo, in0=out_ap, in1=os_hi, op=OP.subtract)
                bankv[bo] = ("vv", p.c("vv"))
                t_hi = p.dma("gpsimd",
                             ot_hi_d[h * 128:(h + 1) * 128,
                                     qt * 128:(qt + 1) * 128],
                             os_hi, wait=(("vv", p.c("vv")),))
                t_lo = p.dma("gpsimd",
                             ot_lo_d[h * 128:(h + 1) * 128,
                                     qt * 128:(qt + 1) * 128],
                             os_lo)
                os_free[oslot] = (t_hi, t_lo)

            v_done = [0]
            for h in range(NH):
                hs = h % 2
                khi = KB[hs][:, 0:2048].bitcast(BF16)
                klo = KB[hs][:, 2048:4096].bitcast(BF16)
                tk1 = p.dma("sync", khi, khi_d[h * 128:(h + 1) * 128, :],
                            wait=((kb_free[hs][0], kb_free[hs][1]),))
                tk2 = p.dma("sync", klo, klo_d[h * 128:(h + 1) * 128, :])
                wv_ld = p.dma("sync", VTB, vt_d[h * 128:(h + 1) * 128, :],
                              wait=((vtb_free[0], vtb_free[1]),))
                srcs = [VTB[:, k * 128:(k + 1) * 128] for k in range(32)]
                gdsts = [VB[:, g2 * 512:(g2 + 1) * 512] for g2 in range(8)]
                v_done[0] = transpose_tiles(srcs, None, 0, TBK, group_dests=gdsts,
                                            extra_wait=(wv_ld,
                                                        (vb_free[0], vb_free[1]),))
                vtb_free = ("tm", p.c("tm"))
                bias_done, _, _ = emit_gate_group(h, 0)
                bias_vv = {0: bias_done}
                prev1 = None
                prev2 = None
                for qt in range(NQT):
                    g = qt // 8
                    par = qt % 2
                    kvlen = (qt + 1) * 128
                    nkc = (kvlen + 511) // 512
                    gp = g % 2
                    qhi8 = QH[gp][:, 0:512].bitcast(BF16)
                    qlo8 = QH[gp][:, 512:1024].bitcast(BF16)
                    jq = qt % 8
                    tail2 = None
                    if prev2 is not None:
                        tail2 = (prev2[0], emit_tail_tr(h, prev2[0], prev2[1]))
                        prev2 = None
                    if jq == 4 and g + 1 < NG:
                        bias_vv[g + 1], _, _ = emit_gate_group(h, g + 1)
                    qk_mms = []
                    qk_handles = []
                    for kc in range(nkc):
                        kcols = min(512, kvlen - kc * 512)
                        b = SBK[kc % 3]
                        first = True
                        for lh, rh in ((qhi8, khi), (qhi8, klo), (qlo8, khi)):
                            w = ()
                            if first:
                                w = (tk1, tk2, (bankv[b][0], bankv[b][1]))
                            hnd = p.op("tensor", "matmul", wait=w,
                                 inc=((("tm", 1),) if lh is qlo8 else ()),
                                 out=PB[b][:, 0:kcols],
                                 lhsT=lh[:, jq * 128:(jq + 1) * 128],
                                 rhs=rh[:, kc * 512: kc * 512 + kcols],
                                 start=first, stop=(lh is qlo8))
                            if first:
                                qk_handles.append(hnd)
                                first = False
                        qk_mms.append((b, p.c("tm")))
                    if jq == 7:
                        qh_free[gp] = ("tm", p.c("tm"))
                    for kc in range(nkc):
                        kcols = min(512, kvlen - kc * 512)
                        b, s_mm = qk_mms[kc]
                        nb = kcols // 64
                        p.op("vector", "scalar_tensor_tensor",
                             wait=(("tm", s_mm), ("vv", bias_vv[g]),)
                             + ((("tm", scr_free[par][1]),) if kc == 0 else ()),
                             inc=(("vv", 1),),
                             out=SCRP[par][:, kc * 512: kc * 512 + kcols].rearrange(
                                 "p (a b) -> p a b", b=64),
                             in0=PB[b][:, 0:kcols].rearrange("p (a b) -> p a b", b=64),
                             scalar=SM_SCALE,
                             in1=GM8[g % 2][:, jq * 64 + kc * 8:
                                            jq * 64 + kc * 8 + nb]
                             .unsqueeze(2).broadcast_to([128, nb, 64]),
                             op0=OP.mult, op1=OP.add)
                        bankv[b] = ("vv", p.c("vv"))
                        if kc + 3 < nkc:
                            p.add_wait(qk_handles[kc + 3], "vv", p.c("vv"))
                    if jq == 7:
                        gm_free[g % 2] = ("vv", p.c("vv"))
                    p.op("vector", "tensor_tensor", inc=(("vv", 1),),
                         out=SCRP[par][:, kvlen - 128: kvlen],
                         in0=SCRP[par][:, kvlen - 128: kvlen], in1=sb_ca[:],
                         op=OP.add)
                    if tail2 is not None:
                        emit_tail_pv(h, tail2[0], tail2[1])
                    p.op("vector", "reduce_max", inc=(("vv", 1),), small=True,
                         out=stb(C_NMAX, ofs=par), in_=SCRP[par][:, 0:kvlen],
                         axis=AX.X, negate=True)
                    sm_stats = p.c("vv")
                    p.op("scalar", "activation", wait=(("vv", sm_stats),),
                         inc=(("sc", 1),), small=True,
                         out=SCRP[par][:, 0:kvlen], in_=SCRP[par][:, 0:kvlen],
                         func=AF.Exp, bias=stb(C_NMAX, ofs=par), scale=1.0,
                         accum_out=stb(C_SSUM, ofs=par))
                    exp_done = p.c("sc")
                    if prev1 is not None:
                        q1, e1 = prev1
                        par1 = q1 % 2
                        p.op("vector", "reciprocal", wait=(e1,),
                             inc=(("vv", 1),), small=True,
                             out=stb(C_RS, ofs=par1), in_=stb(C_SSUM, ofs=par1))
                        p.op("scalar", "activation", wait=(("vv", p.c("vv")),),
                             inc=(("sc", 1),),
                             out=SCRP[par1][:, 0:(q1 + 1) * 128],
                             in_=SCRP[par1][:, 0:(q1 + 1) * 128],
                             func=AF.Copy, scale=stb(C_RS, ofs=par1))
                        prev2 = (q1, ("sc", p.c("sc")))
                    prev1 = (qt, ("sc", exp_done))
                q1, e1 = prev1
                par1 = q1 % 2
                p.op("vector", "reciprocal", wait=(e1,), inc=(("vv", 1),),
                     small=True, out=stb(C_RS, ofs=par1), in_=stb(C_SSUM, ofs=par1))
                p.op("scalar", "activation", wait=(("vv", p.c("vv")),),
                     inc=(("sc", 1),),
                     out=SCRP[par1][:, 0:(q1 + 1) * 128],
                     in_=SCRP[par1][:, 0:(q1 + 1) * 128],
                     func=AF.Copy, scale=stb(C_RS, ofs=par1))
                emit_tail_pv(h, prev2[0], emit_tail_tr(h, prev2[0], prev2[1]))
                sc_last = ("sc", p.c("sc"))
                emit_tail_pv(h, q1, emit_tail_tr(h, q1, sc_last))
                kb_free[hs] = ("tm", p.c("tm"))
                vb_free = ("tm", p.c("tm"))
            p.barrier()

        if "C" in phases:
            OTB_hi = A_ACT[:, 0:8192].bitcast(BF16)
            OTB_lo = A_ACT[:, 8192:16384].bitcast(BF16)
            FTB = A_ACT[:, 16384:32768]
            OUTB = A_Z
            outb_frees = ()
            for tc in range(NCA):
                t0 = tc * TA
                otfb_free = ("tm", p.c("tm"))
                otft_half = []
                for half in range(2):
                    hc = half * 512
                    toks = []
                    for ci in range(NCI):
                        toks.append(p.dma("sync",
                              OTB_hi[:, ci * TA + hc: ci * TA + hc + 512],
                              ot_hi_d[ci * 128:(ci + 1) * 128,
                                      t0 + hc: t0 + hc + 512],
                              wait=(otfb_free,)))
                        toks.append(p.dma("sync",
                              OTB_lo[:, ci * TA + hc: ci * TA + hc + 512],
                              ot_lo_d[ci * 128:(ci + 1) * 128,
                                      t0 + hc: t0 + hc + 512],
                              wait=(otfb_free,)))
                        toks.append(p.dma("sync",
                              FTB[:, ci * TA + hc: ci * TA + hc + 512],
                              ft_d[ci * 128:(ci + 1) * 128,
                                   t0 + hc: t0 + hc + 512],
                              wait=(otfb_free,)))
                    seen = {}
                    for s_, v_ in toks:
                        seen[s_] = max(v_, seen.get(s_, 0))
                    otft_half.append(tuple(seen.items()))
                for th in range(2):
                    out_toks = []
                    for co in range(NCI):
                        whi_t, wlo_t = wsp["wo"]
                        si, wo1, wo2 = w_load_pair(whi_t[co, :, :], wlo_t[co, :, :],
                                                   extra_wait=prol_done["wo"]
                                                   if tc == 0 and th == 0 and co == 0
                                                   else ())
                        b = 2 + co % 2
                        fw = (wo1, wo2) + otft_half[th] + (
                             (bankv[b][0], bankv[b][1]),)
                        mm_o = mm3(PB[b][:], si,
                                   lambda ci, th=th: OTB_hi[:, ci * TA + th * 512:
                                                            ci * TA + th * 512 + 512],
                                   lambda ci, th=th: OTB_lo[:, ci * TA + th * 512:
                                                            ci * TA + th * 512 + 512],
                                   NCI, fw)
                        w_free[si] = ("tm", mm_o)
                        si2 = stg_i[0] % NSTG
                        stg_i[0] += 1
                        p.op("vector", "tensor_tensor",
                             wait=(("tm", mm_o),) + tuple(stg_free[si2]),
                             inc=(("vv", 1),),
                             out=stg_slots[si2], in0=PB[b][:],
                             in1=FTB[:, co * TA + th * 512: co * TA + th * 512 + 512],
                             op=OP.add)
                        st_done = p.c("vv")
                        bankv[b] = ("vv", st_done)
                        tb = 4 + co % 2
                        for j in range(4):
                            w = ()
                            if j == 0:
                                w = (("vv", st_done), (bankv[tb][0], bankv[tb][1]))
                                if co == 0:
                                    w = w + outb_frees
                            p.op("tensor", "transpose", wait=w,
                                 inc=((("tm", 1),) if j == 3 else ()),
                                 out=PB[tb][:, j * 128:(j + 1) * 128],
                                 in_=stg_slots[si2][:, j * 128:(j + 1) * 128],
                                 identity=sb_id[:])
                        tr_done = p.c("tm")
                        stg_free[si2] = (("tm", tr_done),)
                        p.op("vector", "tensor_copy",
                             wait=(("tm", tr_done),), inc=(("vv", 1),),
                             out=bass.AP(A_Z, co * 128,
                                         [[8192, 128], [2048, 4], [1, 128]]),
                             in_=PB[tb][:, 0:512])
                        bankv[tb] = ("vv", p.c("vv"))
                    outb_ready = p.c("vv")
                    for s4 in range(4):
                        s = th * 4 + s4
                        sl = s % 2
                        xa = xa_slots[sl]
                        wx = p.dma("sync", xa, x_d[t0 + s * 128: t0 + s * 128 + 128, :],
                              wait=((xa_free[sl][0], xa_free[sl][1]),))
                        p.op("vector", "tensor_tensor",
                             wait=(wx, ("vv", outb_ready),), inc=(("vv", 1),),
                             out=OUTB[:, s4 * 2048:(s4 + 1) * 2048],
                             in0=OUTB[:, s4 * 2048:(s4 + 1) * 2048], in1=xa[:],
                             op=OP.add)
                        xa_free[sl] = ("vv", p.c("vv"))
                        out_toks.append(
                            p.dma("gpsimd", out_d[t0 + s * 128: t0 + s * 128 + 128, :],
                                  OUTB[:, s4 * 2048:(s4 + 1) * 2048],
                                  wait=(("vv", p.c("vv")),)))
                    seen = {}
                    for s_, v_ in out_toks:
                        seen[s_] = max(v_, seen.get(s_, 0))
                    outb_frees = tuple(seen.items())

        final = tuple((s, p.c(s)) for s in p.POOL["gpsimd"] + p.POOL["sync"])
        p.op("gpsimd", "engine_nop", wait=final)
        p.op("sync", "nop", wait=final)

        @block.sync
        def _(sync):
            p.replay("sync", sync, sems)

        @block.gpsimd
        def _(gpsimd):
            p.replay("gpsimd", gpsimd, sems)

        @block.vector
        def _(vector):
            p.replay("vector", vector, sems)

        @block.scalar
        def _(scalar):
            p.replay("scalar", scalar, sems)

        @block.tensor
        def _(tensor):
            p.replay("tensor", tensor, sems)

    return nc


def _host_inputs(xb, wq, wk, wv, wo, su_ug, su_dp, wscale_ug, wscale_dp,
                 wup, wgt, wdn):
    def blk_in_out(w):
        i_, o_ = w.shape
        nci, nco = i_ // 128, o_ // 128
        return np.ascontiguousarray(
            w.reshape(nci, 128, nco, 128).transpose(2, 1, 0, 3).reshape(nco, 128, i_))

    def blk_out_in(w):
        return blk_in_out(np.ascontiguousarray(w.T))

    q_global = np.arange(S).reshape(NQT, 128)
    qb = q_global // BLK
    n_idx = np.arange(NBLK)
    gm = np.zeros((NQT, 128, NBLK), np.float32)
    gm[qb[:, :, None] == n_idx[None, None, :]] = BIG
    gm[qb[:, :, None] < n_idx[None, None, :]] = -BIG
    gmask = np.ascontiguousarray(gm.transpose(1, 0, 2).reshape(128, NQT * NBLK))

    i = np.arange(128)
    causal = np.where(i[:, None] >= i[None, :], 0.0, NEG).astype(np.float32)

    hm = np.ones((1, 1), np.float32)
    while hm.shape[0] < 128:
        hm = np.block([[hm, hm], [hm, -hm]])

    def fm(v):
        return np.ascontiguousarray(np.asarray(v, np.float32).reshape(-1, 128).T)

    return {
        "x": np.ascontiguousarray(xb),
        "wq": blk_in_out(wq), "wk": blk_in_out(wk), "wv": blk_in_out(wv),
        "wo": blk_in_out(wo),
        "wup": blk_out_in(wup), "wgt": blk_out_in(wgt), "wdn": blk_out_in(wdn),
        "su_ug": fm(su_ug), "su_dp": fm(su_dp),
        "wsu": fm(wscale_ug[:IN]), "wsg": fm(wscale_ug[IN:]),
        "wdp": fm(wscale_dp),
        "h128": np.ascontiguousarray(hm.astype(np.float32)),
        "ident": np.eye(128, dtype=np.float32),
        "causal": causal,
        "gmask": gmask,
    }


def kernel(L0_attn_hidden_states, L0_attn_Wq, L0_attn_Wk, L0_attn_Wv, L0_attn_Wo,
           L0_ffn_SU_ug, L0_ffn_SU_dp, L0_ffn_Wscale_ug, L0_ffn_Wscale_dp,
           L0_ffn_W_up, L0_ffn_W_gate, L0_ffn_W_down):
    x = np.asarray(L0_attn_hidden_states, np.float32)
    args = [np.asarray(a, np.float32) for a in
            (L0_attn_Wq, L0_attn_Wk, L0_attn_Wv, L0_attn_Wo, L0_ffn_SU_ug,
             L0_ffn_SU_dp, L0_ffn_Wscale_ug, L0_ffn_Wscale_dp, L0_ffn_W_up,
             L0_ffn_W_gate, L0_ffn_W_down)]
    nc = build_nc(dbg=False)
    in_maps = [_host_inputs(x[b], *args) for b in range(8)]
    res = run_bass_kernel_spmd(nc, in_maps, core_ids=list(range(8)))
    return np.stack([res.results[b]["out"] for b in range(8)], axis=0)
